# version 15
# baseline (speedup 1.0000x reference)
import contextlib

import numpy as np
import concourse.bass as bass
import concourse.mybir as mybir
from concourse.bass_utils import run_bass_kernel_spmd
from concourse.alu_op_type import AluOpType as ALU

F32 = mybir.dt.float32
BF16 = mybir.dt.bfloat16
I32 = mybir.dt.int32
AF = mybir.ActivationFunctionType
AX = mybir.AxisListType

D = 2048
TOWN = 2048
LSUB = 256
EPS = 1e-5

ENGINES = ("tensor", "vector", "scalar", "gpsimd", "sync")


class SemT:
    def __init__(self, handle):
        self.h = handle
        self.n = 0

    def bump(self, k=1):
        self.n += k
        return self.n


class MK:

    def __init__(self, dump_scr=False):
        self.dump_scr = dump_scr
        self.nc = bass.Bass("TRN2", target_bir_lowering=False)
        nc = self.nc
        ap = lambda n, s, d=F32: nc.dram_tensor(n, s, d,
                                                kind="ExternalInput").ap()
        self.xT_own = ap("xT_own", [D, TOWN])
        self.xT_pref = ap("xT_pref", [D, TOWN])
        self.wq_p = ap("wq_p", [16, 128, 16, 128])
        self.wk_p = ap("wk_p", [16, 128, 16, 128])
        self.wv_m = ap("wv_m", [2, 16, 128, 1024])
        self.wvg_m = ap("wvg_m", [16, 128, 128])
        self.wo_p = ap("wo_p", [16, 16, 128, 128])
        self.wg_p = ap("wg_p", [44, 128, 16, 128], I32)
        self.wu_p = ap("wu_p", [44, 128, 16, 128], I32)
        self.wd_p = ap("wd_p", [4, 44, 128, 4, 128], I32)
        self.sg_r = ap("sg_r", [128, 44])
        self.su_r = ap("su_r", [128, 44])
        self.bo_r = ap("bo_r", [128, 16])
        self.sdb = ap("sdb", [128, 2048])
        self.cos_q = ap("cos_q", [128, 2048])
        self.sin_q = ap("sin_q", [128, 2048])
        self.cos_k = ap("cos_k", [128, 4096])
        self.sin_k = ap("sin_k", [128, 4096])
        self.r2t_f = ap("r2t_f", [128, 128])
        self.masks_f = ap("masks_f", [128, 4 * 512])
        self.pm = ap("pm", [128, 1])
        scr_kind = "ExternalOutput" if dump_scr else "Internal"
        scr = lambda n, s, d: nc.dram_tensor(n, s, d, kind=scr_kind).ap()
        self.xn_scr = scr("xn_scr", [16, 128, 2048], BF16)
        self.ao_scr = scr("ao_scr", [16, 128, 2048], BF16)
        self.k_scr = scr("k_scr", [16, 128, 4096], BF16)
        self.v_scr = scr("v_scr", [32, 128, 2080], BF16)
        self.wg_c = scr("wg_c", [44, 128, 2048], BF16)
        self.wu_c = scr("wu_c", [44, 128, 2048], BF16)
        self.wd_c = scr("wd_c", [4, 44, 128, 512], BF16)
        self.out_fm = nc.dram_tensor("out_fm", [16, 128, 2048], F32,
                                     kind="ExternalOutput").ap()
        if dump_scr:
            self.dbg_negc = nc.dram_tensor("dbg_negc", [4, 128], F32,
                                           kind="ExternalOutput").ap()
            self.dbg_kq = nc.dram_tensor("dbg_kq", [4, 8], F32,
                                         kind="ExternalOutput").ap()
            self.dbg_rec = nc.dram_tensor("dbg_rec", [4, 512], F32,
                                          kind="ExternalOutput").ap()
            self.dbg_sums = nc.dram_tensor("dbg_sums", [4, 64], F32,
                                           kind="ExternalOutput").ap()
            self.dbg_qg = nc.dram_tensor("dbg_qg", [2, 128, 8192], BF16,
                                         kind="ExternalOutput").ap()
            self.dbg_gu = nc.dram_tensor("dbg_gu", [128, 45056], BF16,
                                         kind="ExternalOutput").ap()
            self.dbg_dn = nc.dram_tensor("dbg_dn", [4, 128, 1024], F32,
                                         kind="ExternalOutput").ap()
            self.dbg_wd = nc.dram_tensor("dbg_wd", [4, 44, 128, 512], BF16,
                                         kind="ExternalOutput").ap()

        self.q = {e: [] for e in ENGINES}
        self.last_wait = {}
        self.t_xbuf_free = 0
        self.t_bsq_free = 0
        self.t_xnp_free = 0
        self.t_qg_free = 0
        self.t_kst_free = 0
        self.t_k2_free = 0
        self.t_ropepr = 0
        self.t_cs_free = [0, 0]
        self.t_aost_free = 0
        self.t_ao_written = 0
        self.t_xnscr_all = 0
        self.t_super_done = 0
        self.k_written = [0] * 32
        self.v_written = [0] * 32

    def em(self, eng, fn):
        self.q[eng].append(fn)

    def wt(self, eng, sem, tgt):
        if tgt <= 0:
            return
        key = (eng, id(sem))
        if self.last_wait.get(key, -1) >= tgt:
            return
        self.last_wait[key] = tgt
        self.q[eng].append(lambda E, s=sem.h, t=tgt: E.wait_ge(s, t))

    def flush(self):
        for e in ENGINES:
            fns = self.q[e]
            if not fns:
                continue
            self.q[e] = []

            def section(E, fns=fns):
                for f in fns:
                    f(E)

            getattr(self.block, e)(section)

    def dma(self, qeng, sem, dst, src, waits=()):
        if getattr(self, "lockstep", False):
            self._lock(qeng)
        for (s, t) in waits:
            self.wt(qeng, s, t)
        tgt = sem.bump(16)
        self.em(qeng, lambda Q, d=dst, s=src, h=sem.h:
                Q.dma_start(d, s).then_inc(h, 16))
        return tgt

    def pe_fence(self, elem_ap):
        t = elem_ap.tensor if hasattr(elem_ap, "tensor") else None
        return self.op("tensor", self.cpe, lambda t2: t2.matmul(
            elem_ap, self.zb[:], self.zrow[:], start=False, stop=True))

    def sprinkle(self, k=1):
        if self.cache_jobs:
            self.emit_cache_jobs(k)

    def wd_load_flat(self, dma):
        j = self._wd_next
        if j >= 176:
            return
        self._wd_next += 1
        qtr, h = j // 44, j % 44
        bpar = j % 4
        t_wd = dma("sync", self.qsp, self.wdb[bpar][:], self.wd_c[qtr, h],
                   waits=((self.cpe, self.t_wdb_free[bpar]),
                          (self.qact, self.t_cache_done)))
        self.t_wd_ready[bpar] = t_wd

    def act_fence(self, src_ap):
        return self.op("scalar", self.cact, lambda sc: sc.activation(
            self.fence_sb[0:1, 2:4], src_ap, AF.Copy, bias=0.0, scale=1.0))

    def dve_fence(self, src_ap=None):
        if src_ap is None:
            src_ap = self.fence_sb[0:1, 1:2]
        return self.op("vector", self.cdve, lambda v: v.tensor_copy(
            self.fence_sb[0:1, 0:1], src_ap))

    def _lock_force(self, eng):
        for sem2 in (self.cpe, self.cdve, self.cact, self.qsp):
            self.wt(eng, sem2, sem2.n)

    def _lock(self, eng):
        if self.lock_engs and eng not in self.lock_engs:
            return
        for sem2 in (self.cpe, self.cdve, self.cact, self.cpool,
                     self.qsp, self.qact, self.qpool):
            self.wt(eng, sem2, sem2.n)

    def op(self, eng, sem, fn, waits=(), inc=True):
        if getattr(self, "lockstep", False):
            self._lock(eng)
        for (s, t) in waits:
            self.wt(eng, s, t)
        if inc:
            tgt = sem.bump()
            self.em(eng, lambda E, f=fn, h=sem.h: f(E).then_inc(h, 1))
            return tgt
        self.em(eng, fn)
        return sem.n

    def pe_bank(self, key):
        s, t = self.bank_free[key]
        self.wt("tensor", s, t)

    def free_bank(self, key, sem, tgt):
        self.bank_free[key] = (sem, tgt)

    def psl(self, key):
        i, hlf = key
        return self.pq[i][:, hlf * 512:(hlf + 1) * 512]

    def build(self):
        nc = self.nc
        st = contextlib.ExitStack()
        E = st.enter_context
        with st:
            self.masks = E(nc.sbuf_tensor("masks", [128, 4 * 512], BF16))
            self.r2t = E(nc.sbuf_tensor("r2t", [128, 128], BF16))
            self.sgs = E(nc.sbuf_tensor("sgs", [128, 44], F32))
            self.sus = E(nc.sbuf_tensor("sus", [128, 44], F32))
            self.bos = E(nc.sbuf_tensor("bos", [128, 16], F32))
            self.sdbb = E(nc.sbuf_tensor("sdbb", [128, 2048], BF16))
            self.pmc = E(nc.sbuf_tensor("pmc", [128, 1], F32))
            self.ones_bf = E(nc.sbuf_tensor("ones_bf", [128, 1], BF16))
            self.onesrow = E(nc.sbuf_tensor("onesrow", [1, 128], F32))
            self.zc = E(nc.sbuf_tensor("zc", [128, 1], F32))
            self.fence_sb = E(nc.sbuf_tensor("fence_sb", [1, 8], F32))
            self.dqb = [E(nc.sbuf_tensor("dqb0", [128, 2048], BF16))] * 2
            self.c128b = E(nc.sbuf_tensor("c128b", [128, 2048], BF16))
            self.zb = E(nc.sbuf_tensor("zb", [1, 1], BF16))
            self.zrow = E(nc.sbuf_tensor("zrow", [1, 256], BF16))
            self.epsc = E(nc.sbuf_tensor("epsc", [1, 2], F32))
            self.stats = E(nc.sbuf_tensor("stats", [1, 1536], F32))
            self.kq = E(nc.sbuf_tensor("kq", [1, 8], F32))
            self.negc = E(nc.sbuf_tensor("negc", [128, 1], F32))
            self.rstdb = E(nc.sbuf_tensor("rstdb", [128, 512], F32))
            self.pq = [E(nc.psum_tensor(f"pq{i}", [128, 1024], F32))
                       for i in range(4)]
            self.qsp = SemT(E(nc.semaphore("qsp")))
            self.qact = SemT(E(nc.semaphore("qact")))
            self.qpool = SemT(E(nc.semaphore("qpool")))
            self.cpe = SemT(E(nc.semaphore("cpe")))
            self.cdve = SemT(E(nc.semaphore("cdve")))
            self.cact = SemT(E(nc.semaphore("cact")))
            self.cpool = SemT(E(nc.semaphore("cpool")))
            self.block = E(nc.Block())
            self.bank_free = {(i, h): (self.cdve, 0)
                              for i in range(4) for h in range(2)}

            self.setup()
            self.build_ffn_cache()
            stA = contextlib.ExitStack()
            EA = stA.enter_context
            with stA:
                self.xbuf = EA(nc.sbuf_tensor("xbuf", [128, 16 * LSUB], F32))
                self.recrow = EA(nc.sbuf_tensor("recrow", [1, 512], F32))
                self.recb = EA(nc.sbuf_tensor("recb", [128, 512], F32))
                self.q0b = EA(nc.sbuf_tensor("q0b", [128, 512], BF16))
                self.t1b = EA(nc.sbuf_tensor("t1b", [128, 512], F32))
                self.t2b = EA(nc.sbuf_tensor("t2b", [128, 512], F32))
                self.kst = EA(nc.sbuf_tensor("kst", [128, 512], BF16))
                self.k2b = self.q0b
                self.abuf = [EA(nc.sbuf_tensor(f"abuf{i}", [128, 1024], BF16))
                             for i in range(2)]
                self.csb = [EA(nc.sbuf_tensor(f"csb{i}", [128, 1024], F32))
                            for i in range(2)]


                self.bsq = EA(nc.sbuf_tensor("bsq", [128, 16 * 256], BF16))
                self.xnp = EA(nc.sbuf_tensor("xnp", [128, 16 * 1024], BF16))
                self.qg = [EA(nc.sbuf_tensor(f"qg{i}", [128, 16 * 512], BF16))
                           for i in range(2)]
                self.kbuf = [EA(nc.sbuf_tensor(f"kbuf{i}", [128, 4096], BF16))
                             for i in range(2)]
                self.vbuf = [EA(nc.sbuf_tensor(f"vbuf{i}", [128, 32 * 130],
                                               BF16)) for i in range(2)]
                self.vstage = [EA(nc.sbuf_tensor("vstage0", [128, 1040],
                                                 BF16))] * 2
                self.t_vst_free = [0, 0]
                self.aost = EA(nc.sbuf_tensor("aost", [64, 16 * 512], BF16))
                self.wkq = [EA(nc.sbuf_tensor(f"wkq{i}", [128, 2048], BF16))
                            for i in range(2)]
                self.t_wkq_free = [0, 0]
                self.wvb = [EA(nc.sbuf_tensor(f"wvb{i}", [128, 1024], BF16))
                            for i in range(2)]
                self.t_wvb_free = [0, 0]
                self.t_kbuf_free = [0, 0]
                self.t_vbuf_free = [0, 0]
                stP = contextlib.ExitStack()
                EP = stP.enter_context
                with stP:
                    self.wvgb = EP(nc.sbuf_tensor("wvgb", [128, 2048], BF16))
                    self.wk15 = EP(nc.sbuf_tensor("wk15", [128, 2048], BF16))
                    self.t_wvg = self.dma(
                        "gpsimd", self.qpool,
                        self.wvgb[:].rearrange("p (c j) -> p c j", c=16),
                        self.wvg_m.rearrange("c p j -> p c j"))
                    self.t_wk15 = self.dma(
                        "gpsimd", self.qpool,
                        self.wk15[:].rearrange("p (c j) -> p c j", c=16),
                        self.wk_p[15])
                    for g in range(4):
                        with nc.named_scope(f"prefix{g}"):
                            self.prefix_group(g)
                            self.emit_cache_jobs(12)
                            self.flush()
                self.abuf += [EA(nc.sbuf_tensor(f"abuf{i}", [128, 1024],
                                                BF16)) for i in (2, 3)]
                for pair in range(2):
                    with nc.named_scope(f"pair{pair}"):
                        self.own_pair(pair)
                        self.flush()
                with nc.named_scope("cachetail"):
                    self.emit_cache_jobs(10000)
                    self.flush()
            stB = contextlib.ExitStack()
            EB = stB.enter_context
            with stB:
                self.xnao = EB(nc.sbuf_tensor("xnao", [128, 16 * 1024], BF16))
                self.gub = EB(nc.sbuf_tensor("gub", [128, 44 * 1024], BF16))
                self.wgu = [EB(nc.sbuf_tensor(f"wgu{i}", [128, 2048], BF16))
                            for i in range(3)]
                self.t_wgu_free = [0, 0, 0]
                self.wdb = [EB(nc.sbuf_tensor(f"wdb{i}", [128, 512], BF16))
                            for i in range(4)]
                self.t_wdb_free = [0] * 4
                self.wob = [EB(nc.sbuf_tensor(f"wob{i}", [128, 2048], BF16))
                            for i in range(4)]
                self.t_wob_free = [0] * 4
                self.silu_sb = EB(nc.sbuf_tensor("silu_sb", [128, 1024], F32))
                if self.dump_scr:
                    self.dbg_dn_sb = EB(nc.sbuf_tensor("dbg_dn_sb",
                                                       [128, 1024], F32))
                self.up_sb = EB(nc.sbuf_tensor("up_sb", [128, 1024], F32))
                self.outsb = [EB(nc.sbuf_tensor(f"outsb{i}", [128, 1024], F32))
                              for i in range(2)]
                self.xjit = [EB(nc.sbuf_tensor(f"xjit{i}", [128, 1024], F32))
                             for i in range(2)]
                self.t_outsb_free = [0, 0]
                self.t_xjit_free = [0, 0]
                import os
                self.lockmode = os.environ.get("MK_LOCKMODE", "full")
                self.lock_engs = tuple(
                    x for x in os.environ.get("MK_LOCK_ENGS", "").split(",")
                    if x)
                self.lock_sites = tuple(
                    x for x in os.environ.get("MK_LOCK_SITES", "").split(",")
                    if x)
                for sup in range(2):
                    with nc.named_scope(f"ffn{sup}"):
                        self.ffn_super(sup)
                        self.flush()
                self.lockstep = False
            self.finalize()
            self.flush()
        return nc

    def setup(self):
        self.dma("gpsimd", self.qpool, self.masks[:], self.masks_f)
        self.dma("gpsimd", self.qpool, self.r2t[:], self.r2t_f)
        self.t_sdbb = self.dma("gpsimd", self.qpool, self.sdbb[:], self.sdb)
        self.dma("sync", self.qsp, self.sgs[:], self.sg_r)
        self.dma("sync", self.qsp, self.sus[:], self.su_r)
        self.dma("sync", self.qsp, self.bos[:], self.bo_r)
        self.t_consts = self.dma("sync", self.qsp, self.pmc[:], self.pm)
        self.op("vector", self.cdve, lambda v: v.memset(self.ones_bf[:], 1.0))
        self.op("vector", self.cdve, lambda v: v.memset(self.zc[:], 0.0))
        self.op("vector", self.cdve, lambda v: v.memset(self.epsc[0:1, 0:1], EPS))
        self.op("vector", self.cdve, lambda v: v.memset(self.epsc[0:1, 1:2], 1e-18))
        self.op("vector", self.cdve, lambda v: v.memset(self.onesrow[:], 1.0))
        self.op("vector", self.cdve, lambda v: v.memset(self.zb[:], 0.0))
        self.op("vector", self.cdve, lambda v: v.memset(self.zrow[:], 0.0))
        self.t_c128 = self.op("gpsimd", self.cpool,
                              lambda p: p.memset(self.c128b[:], 128.0))
        self.op("vector", self.cdve, lambda v: v.memset(self.fence_sb[:], 0.0))
        self.t_setup_dve = self.op("vector", self.cdve,
                                   lambda v: v.memset(self.kq[:], 0.0))
        self.flush()

    def build_ffn_cache(self):
        self.t_dq_free = [0, 0]
        self.dq_i = 0
        self.cache_jobs = []
        for h in range(44):
            self.cache_jobs.append(("g", h))
            self.cache_jobs.append(("u", h))
        for qtr in range(4):
            for h in range(44):
                self.cache_jobs.append(("d", (qtr, h)))
        self.t_cache_done = 0

    def emit_cache_jobs(self, k):
        dma, op = self.dma, self.op
        for _ in range(min(k, len(self.cache_jobs))):
            kind, arg = self.cache_jobs.pop(0)
            par = self.dq_i % 2
            self.dq_i += 1
            if kind in ("g", "u"):
                h = arg
                src = (self.wg_p if kind == "g" else self.wu_p)[h]
                dst = (self.wg_c if kind == "g" else self.wu_c)[h]
                cols = 2048
                sd = None
            else:
                qtr, h = arg
                src = self.wd_p[qtr, h]
                dst = self.wd_c[qtr, h]
                cols = 512
                sd = slice(qtr * 512, qtr * 512 + 512)
            t_in = dma("gpsimd", self.qpool, self.dqb[par][:, 0:cols], src,
                       waits=((self.qact, self.t_dq_free[par]),))
            t_d = op("gpsimd", self.cpool, lambda v, par=par, cols=cols:
                     v.tensor_tensor(self.dqb[par][:, 0:cols],
                                     self.dqb[par][:, 0:cols],
                                     self.c128b[:, 0:cols], ALU.subtract),
                     waits=((self.qpool, t_in), (self.cpool, self.t_c128)))
            if sd is not None:
                t_d = op("gpsimd", self.cpool, lambda v, par=par, ss=sd,
                         cols=cols: v.tensor_tensor(
                             self.dqb[par][:, 0:cols],
                             self.dqb[par][:, 0:cols],
                             self.sdbb[:, ss], ALU.mult),
                         waits=((self.qpool, self.t_sdbb),))
            t_o = dma("scalar", self.qact, dst, self.dqb[par][:, 0:cols],
                      waits=((self.cpool, t_d),))
            self.t_dq_free = [t_o, t_o]
        self.t_cache_done = self.qact.n

    def ln_sub(self, xT, lt_base, sub, dst_off):
        op, wt, dma = self.op, self.wt, self.dma
        c0 = lt_base * 128 + sub * LSUB
        xr = xT.rearrange("(c p) t -> p c t", p=128)
        tx = dma("sync", self.qsp, self.xbuf[:].rearrange(
            "p (c t) -> p c t", c=16), xr[:, :, c0:c0 + LSUB],
            waits=((self.cdve, self.t_xbuf_free),))
        txb = dma("gpsimd", self.qpool, self.bsq[:, 0:16 * LSUB].rearrange(
            "p (c t) -> p c t", c=16), xr[:, :, c0:c0 + LSUB],
            waits=((self.cpe, self.t_bsq_free),))
        self.pe_bank((3, 0))
        wt("tensor", self.qpool, txb)
        wt("tensor", self.cdve, self.t_setup_dve)
        t_sum = 0
        for dc in range(16):
            t_sum = op("tensor", self.cpe, lambda t, dc=dc: t.matmul(
                self.pq[3][0:1, 0:LSUB], self.ones_bf[:],
                self.bsq[:, dc * LSUB:(dc + 1) * LSUB],
                start=(dc == 0), stop=(dc == 15)), inc=(dc == 15))
        t_sq = op("vector", self.cdve, lambda v: v.tensor_tensor(
            self.bsq[:, 0:16 * LSUB], self.bsq[:, 0:16 * LSUB],
            self.bsq[:, 0:16 * LSUB], ALU.mult), waits=((self.cpe, t_sum),))
        self.pe_bank((3, 1))
        wt("tensor", self.cdve, t_sq)
        t_ssq = 0
        for dc in range(16):
            t_ssq = op("tensor", self.cpe, lambda t, dc=dc: t.matmul(
                self.pq[3][0:1, 512:512 + LSUB], self.ones_bf[:],
                self.bsq[:, dc * LSUB:(dc + 1) * LSUB],
                start=(dc == 0), stop=(dc == 15)), inc=(dc == 15))
        self.t_bsq_free = t_ssq
        st = self.stats
        MEAN, VAR, M2, RSTD, MRS = (0, 256, 512, 768, 1024)
        op("vector", self.cdve, lambda v: v.tensor_scalar(
            st[0:1, MEAN:MEAN + LSUB], self.pq[3][0:1, 0:LSUB], 1.0 / D, None,
            ALU.mult), waits=((self.cpe, t_sum),))
        t_rd1 = op("vector", self.cdve, lambda v: v.tensor_scalar(
            st[0:1, VAR:VAR + LSUB], self.pq[3][0:1, 512:512 + LSUB],
            1.0 / D, None, ALU.mult), waits=((self.cpe, t_ssq),))
        self.free_bank((3, 0), self.cdve, t_rd1)
        self.free_bank((3, 1), self.cdve, t_rd1)
        op("vector", self.cdve, lambda v: v.tensor_tensor(
            st[0:1, M2:M2 + LSUB], st[0:1, MEAN:MEAN + LSUB],
            st[0:1, MEAN:MEAN + LSUB], ALU.mult))
        t_var = op("vector", self.cdve, lambda v: v.tensor_tensor(
            st[0:1, VAR:VAR + LSUB], st[0:1, VAR:VAR + LSUB],
            st[0:1, M2:M2 + LSUB], ALU.subtract))
        op("scalar", self.cact, lambda s: s.activation(
            st[0:1, RSTD:RSTD + LSUB], st[0:1, VAR:VAR + LSUB], AF.Ln,
            bias=self.epsc[0:1, 0:1], scale=1.0),
            waits=((self.cdve, t_var),))
        t_rstd = op("scalar", self.cact, lambda s: s.activation(
            st[0:1, RSTD:RSTD + LSUB], st[0:1, RSTD:RSTD + LSUB], AF.Exp,
            bias=self.zc[0:1, 0:1], scale=-0.5))
        t_mrs = op("vector", self.cdve, lambda v: v.tensor_tensor(
            st[0:1, MRS:MRS + LSUB], st[0:1, MEAN:MEAN + LSUB],
            st[0:1, RSTD:RSTD + LSUB], ALU.mult),
            waits=((self.cact, t_rstd),))
        self.pe_bank((2, 0))
        self.pe_bank((2, 1))
        wt("tensor", self.cdve, t_mrs)
        op("tensor", self.cpe, lambda t: t.matmul(
            self.pq[2][:, 0:LSUB], self.onesrow[:],
            st[0:1, RSTD:RSTD + LSUB], start=True, stop=True), inc=False)
        t_bc = op("tensor", self.cpe, lambda t: t.matmul(
            self.pq[2][:, 512:512 + LSUB], self.onesrow[:],
            st[0:1, MRS:MRS + LSUB], start=True, stop=True))
        op("vector", self.cdve, lambda v: v.tensor_copy(
            self.rstdb[:, 0:LSUB], self.pq[2][:, 0:LSUB]),
            waits=((self.cpe, t_bc),))
        t_ev = op("vector", self.cdve, lambda v: v.tensor_copy(
            self.rstdb[:, 256:256 + LSUB], self.pq[2][:, 512:512 + LSUB]))
        self.free_bank((2, 0), self.cdve, t_ev)
        self.free_bank((2, 1), self.cdve, t_ev)
        xap = self.xbuf[:].rearrange("p (c t) -> p c t", c=16)
        rrep = bass.AP(self.rstdb, 0, [[512, 128], [0, 16], [1, LSUB]])
        mrep = bass.AP(self.rstdb, 256, [[512, 128], [0, 16], [1, LSUB]])
        wt("vector", self.qsp, tx)
        op("vector", self.cdve, lambda v: v.tensor_tensor(
            xap, xap, rrep, ALU.mult))
        dap = self.xnp[:].rearrange("p (c t) -> p c t", c=16)[
            :, :, dst_off + sub * LSUB: dst_off + sub * LSUB + LSUB]
        wt("vector", self.cpe, self.t_xnp_free)
        t_xn = op("vector", self.cdve, lambda v: v.tensor_tensor(
            dap, xap, mrep, ALU.subtract))
        self.t_xbuf_free = t_xn
        return t_xn

    def rope(self, pkey, dst_ap, cs_i, cs_tgt, t_proj, dst_waits=()):
        op, wt = self.op, self.wt
        ps = self.psl(pkey)
        t_q0 = op("vector", self.cdve, lambda v: v.tensor_copy(
            self.q0b[:], ps),
            waits=((self.cpe, t_proj), (self.cpe, self.t_ropepr)))
        self.pe_bank((2, 0))
        wt("tensor", self.cdve, t_q0)
        t_pr = op("tensor", self.cpe, lambda t: t.matmul(
            self.pq[2][:, 0:512], self.r2t[:], self.q0b[:],
            start=True, stop=True))
        self.t_ropepr = t_pr
        cs = self.csb[cs_i]
        wt("vector", self.qsp, cs_tgt)
        op("vector", self.cdve, lambda v: v.tensor_tensor(
            self.t1b[:], self.q0b[:], cs[:, 0:512], ALU.mult))
        t_t2 = op("vector", self.cdve, lambda v: v.tensor_tensor(
            self.t2b[:], self.pq[2][:, 0:512], cs[:, 512:1024], ALU.mult),
            waits=((self.cpe, t_pr),))
        self.free_bank((2, 0), self.cdve, t_t2)
        for (s, t) in dst_waits:
            wt("vector", s, t)
        t_out = op("vector", self.cdve, lambda v: v.tensor_tensor(
            dst_ap, self.t1b[:], self.t2b[:], ALU.add))
        return t_out

    def knorm(self):
        op, wt = self.op, self.wt
        t_k2 = op("vector", self.cdve, lambda v: v.tensor_tensor(
            self.k2b[:], self.kst[:], self.kst[:], ALU.mult),
            waits=((self.cpe, self.t_k2_free),))
        self.pe_bank((3, 0))
        wt("tensor", self.cdve, t_k2)
        t_mm = op("tensor", self.cpe, lambda t: t.matmul(
            self.pq[3][0:1, 0:512], self.ones_bf[:], self.k2b[:],
            start=True, stop=True))
        self.t_k2_free = t_mm
        t_red = op("vector", self.cdve, lambda v: v.reduce_max(
            self.kq[0:1, 2:3], self.pq[3][0:1, 0:512], axis=AX.X),
            waits=((self.cpe, t_mm),))
        self.wt("vector", self.cdve, t_red)
        self.free_bank((3, 0), self.cdve, t_red)
        op("vector", self.cdve, lambda v: v.tensor_tensor(
            self.kq[0:1, 0:1], self.kq[0:1, 0:1], self.kq[0:1, 2:3], ALU.max))

    def prefix_group(self, g):
        dma, op, wt = self.dma, self.op, self.wt
        for sub in range(2):
            t_xn = self.ln_sub(self.xT_pref, 4 * g, sub, 0)
        dma("sync", self.qsp, self.csb[0][:, 0:512],
            self.cos_k[:, 512 * g:512 * g + 512],
            waits=((self.cdve, self.t_cs_free[0]),))
        t_cs = dma("sync", self.qsp, self.csb[0][:, 512:1024],
                   self.sin_k[:, 512 * g:512 * g + 512])
        pcs = [15] if g < 3 else list(range(16))
        for i, pc in enumerate(pcs):
            par = i % 2
            if pc == 15:
                wb, t_w = self.wk15, self.t_wk15
            else:
                wb = self.wkq[par]
                t_w = dma("gpsimd", self.qpool, wb[:].rearrange("p (c j) -> p c j", c=16),
                          self.wk_p[pc],
                          waits=((self.cpe, self.t_wkq_free[par]),))
            key = (par, 0)
            self.pe_bank(key)
            wt("tensor", self.qpool, t_w)
            wt("tensor", self.cdve, t_xn)
            t_mm = 0
            for dc in range(16):
                t_mm = op("tensor", self.cpe, lambda t, dc=dc, wb=wb, key=key:
                          t.matmul(self.psl(key),
                                   wb[:, dc * 128:dc * 128 + 128],
                                   self.xnp[:, dc * 1024:dc * 1024 + 512],
                                   start=(dc == 0), stop=(dc == 15)),
                          inc=(dc == 15))
            if pc != 15:
                self.t_wkq_free[par] = t_mm
            t_rope = self.rope(key, self.kst[:], 0, t_cs, t_mm,
                               dst_waits=((self.qact, self.t_kst_free),))
            self.free_bank(key, self.cdve, t_rope)
            t_st = dma("scalar", self.qact,
                       self.k_scr[pc][:, 512 * g:512 * g + 512], self.kst[:],
                       waits=((self.cdve, t_rope),))
            self.t_kst_free = t_st
            for lt in range(4 * g, 4 * g + 4):
                self.k_written[lt] = max(self.k_written[lt], t_st)
            self.knorm()
        self.t_cs_free[0] = self.cdve.n
        if g < 3:
            for tt in range(4):
                key = (tt % 2, 1)
                self.pe_bank(key)
                wt("tensor", self.qpool, self.t_wvg)
                t_mm = 0
                for dc in range(16):
                    t_mm = op("tensor", self.cpe,
                              lambda t, dc=dc, tt=tt, key=key: t.matmul(
                                  self.psl(key)[:, 0:128],
                                  self.xnp[:, dc * 1024 + tt * 128:
                                           dc * 1024 + tt * 128 + 128],
                                  self.wvgb[:, dc * 128:dc * 128 + 128],
                                  start=(dc == 0), stop=(dc == 15)),
                              inc=(dc == 15))
                self.v_evac_global(4 * g + tt, key, t_mm)
        else:
            self.v_own_quad(list(range(12, 16)), 0, prefix=True)
        self.t_xnp_free = self.cpe.n

    def v_evac_global(self, lt, key, t_mm):
        dma, op, wt = self.dma, self.op, self.wt
        par = lt % 2
        vs = self.vstage[par]
        wt("vector", self.qact, self.t_vst_free[par])
        src = self.psl(key)[:, 0:128].rearrange("p (h d) -> p h d", d=64)
        vr = vs[:, 0:1040].rearrange("p (h d) -> p h d", d=65)
        t_ev = op("vector", self.cdve, lambda v: v.tensor_scalar(
            vr[:, 14:16, 0:64], src, self.pmc[:, 0:1], None, ALU.mult),
            waits=((self.cpe, t_mm), (self.qsp, self.t_consts)))
        self.free_bank(key, self.cdve, t_ev)
        pmrep = bass.AP(self.pmc, 0, [[1, 128], [0, 2], [0, 1]])
        t_ev = op("vector", self.cdve, lambda v: v.tensor_scalar(
            vr[:, 14:16, 64:65], pmrep, 1.0, None, ALU.mult))
        t_st = dma("scalar", self.qact, self.v_scr[lt][:, 1950:2080],
                   vs[:, 910:1040], waits=((self.cdve, t_ev),))
        self.t_vst_free = [t_st, t_st]
        self.v_written[lt] = max(self.v_written[lt], t_st)

    def v_own_quad(self, lts, xoff, prefix=False):
        dma, op, wt = self.dma, self.op, self.wt
        for half in range(2):
            t_mm = 0
            for dc in range(16):
                par = dc % 2
                t_w = dma("gpsimd", self.qpool, self.wvb[par][:],
                          self.wv_m[half, dc],
                          waits=((self.cpe, self.t_wvb_free[par]),))
                wt("tensor", self.qpool, t_w)
                for i in range(4):
                    if dc == 0:
                        self.pe_bank((i, 0))
                        self.pe_bank((i, 1))
                    for s2 in range(2):
                        t_mm = op("tensor", self.cpe,
                                  lambda t, dc=dc, i=i, s2=s2, par=par:
                                  t.matmul(
                                      self.pq[i][:, s2 * 512:s2 * 512 + 512],
                                      self.xnp[:, dc * 1024 + xoff + i * 128:
                                               dc * 1024 + xoff + i * 128 + 128],
                                      self.wvb[par][:, s2 * 512:s2 * 512 + 512],
                                      start=(dc == 0), stop=(dc == 15)),
                                  inc=(i == 3 and s2 == 1))
                self.t_wvb_free[par] = t_mm
            for i, lt in enumerate(lts):
                par = i % 2
                vs = self.vstage[par]
                wt("vector", self.qact, self.t_vst_free[par])
                src = self.pq[i][:, 0:1024].rearrange("p (h d) -> p h d", d=64)
                vr = vs[:, 0:1040].rearrange("p (h d) -> p h d", d=65)
                if prefix:
                    t_ev = op("vector", self.cdve, lambda v, s=src, vr=vr:
                              v.tensor_scalar(vr[:, :, 0:64], s,
                                              self.pmc[:, 0:1], None, ALU.mult),
                              waits=((self.cpe, t_mm),
                                     (self.qsp, self.t_consts)))
                    pmrep = bass.AP(self.pmc, 0, [[1, 128], [0, 16], [0, 1]])
                    t_ev = op("vector", self.cdve, lambda v, vr=vr:
                              v.tensor_scalar(vr[:, :, 64:65], pmrep, 1.0,
                                              None, ALU.mult))
                else:
                    t_ev = op("vector", self.cdve, lambda v, s=src, vr=vr:
                              v.tensor_copy(vr[:, :, 0:64], s),
                              waits=((self.cpe, t_mm),))
                    t_ev = op("vector", self.cdve, lambda v, vr=vr:
                              v.memset(vr[:, :, 64:65], 1.0))
                t_st = dma("scalar", self.qact,
                           self.v_scr[lt][:, half * 1040:half * 1040 + 1040],
                           vs[:], waits=((self.cdve, t_ev),))
                self.t_vst_free = [t_st, t_st]
                self.v_written[lt] = max(self.v_written[lt], t_st)
                self.free_bank((i, 0), self.cdve, t_ev)
                self.free_bank((i, 1), self.cdve, t_ev)

    def own_pair(self, pair):
        dma, op, wt = self.dma, self.op, self.wt
        g0, g1 = 2 * pair, 2 * pair + 1
        xsr = self.xn_scr.rearrange("c p t -> p c t")

        def ln_group(gi, g):
            t_xn = 0
            for sub in range(2):
                t_xn = self.ln_sub(self.xT_own, 4 * g, sub, gi * 512)
            src = self.xnp[:].rearrange("p (c t) -> p c t", c=16)[
                :, :, gi * 512:gi * 512 + 512]
            self.t_xnscr_all = dma(
                "scalar", self.qact, xsr[:, :, g * 512:g * 512 + 512], src,
                waits=((self.cdve, t_xn),))
            return t_xn

        def proj(which, sp, g, t_xn):
            t_cs0 = dma("sync", self.qsp, self.csb[sp][:, 0:512],
                        (self.cos_q if which == "q" else self.cos_k)[
                            :, (2048 if which == "k" else 0) + g * 512:
                            (2048 if which == "k" else 0) + g * 512 + 512],
                        waits=((self.cdve, self.t_cs_free[sp]),))
            t_cs = dma("sync", self.qsp, self.csb[sp][:, 512:1024],
                       (self.sin_q if which == "q" else self.sin_k)[
                           :, (2048 if which == "k" else 0) + g * 512:
                           (2048 if which == "k" else 0) + g * 512 + 512])
            wsrc = self.wq_p if which == "q" else self.wk_p
            for pc in range(16):
                par = pc % 2
                key = (pc % 2, 0)
                t_w = dma("gpsimd", self.qpool,
                          self.wkq[par][:].rearrange("p (c j) -> p c j", c=16),
                          wsrc[pc], waits=((self.cpe, self.t_wkq_free[par]),))
                self.pe_bank(key)
                wt("tensor", self.qpool, t_w)
                wt("tensor", self.cdve, t_xn)
                t_mm = 0
                for dc in range(16):
                    t_mm = op("tensor", self.cpe,
                              lambda t, dc=dc, sp=sp, par=par, key=key:
                              t.matmul(
                                  self.psl(key),
                                  self.wkq[par][:, dc * 128:dc * 128 + 128],
                                  self.xnp[:, dc * 1024 + sp * 512:
                                           dc * 1024 + sp * 512 + 512],
                                  start=(dc == 0), stop=(dc == 15)),
                              inc=False)
                t_mm = self.pe_fence(self.psl(key)[0:1, 0:256])
                self.t_wkq_free[par] = t_mm
                if which == "q":
                    t_rope = self.rope(
                        key, self.qg[sp][:, pc * 512:pc * 512 + 512], sp,
                        t_cs, t_mm, dst_waits=((self.cpe, self.t_qg_free),))
                    self.free_bank(key, self.cdve, t_rope)
                else:
                    t_rope = self.rope(key, self.kst[:], sp, t_cs, t_mm,
                                       dst_waits=((self.qact,
                                                   self.t_kst_free),))
                    self.free_bank(key, self.cdve, t_rope)
                    col = 2048 + g * 512
                    t_st = dma("scalar", self.qact,
                               self.k_scr[pc][:, col:col + 512], self.kst[:],
                               waits=((self.cdve, t_rope),))
                    self.t_kst_free = t_st
                    for lt in range(16 + 4 * g, 16 + 4 * g + 4):
                        self.k_written[lt] = max(self.k_written[lt], t_st)
                    self.knorm()
                self.sprinkle()
            self.t_cs_free[sp] = self.cdve.n

        t_xn0 = ln_group(0, g0)
        proj("q", 0, g0, t_xn0)
        proj("k", 0, g0, t_xn0)
        t_xn1 = ln_group(1, g1)
        self.v_own_quad([16 + 8 * pair + i for i in range(4)], 0,
                        prefix=False)
        self.attention(g0, 0)
        proj("q", 1, g1, t_xn1)
        proj("k", 1, g1, t_xn1)
        self.v_own_quad([20 + 8 * pair + i for i in range(4)], 512,
                        prefix=False)
        self.t_xnp_free = self.cpe.n
        self.attention(g1, 1)
        self.t_qg_free = self.cpe.n

    def attention(self, g, gi):
        dma, op, wt = self.dma, self.op, self.wt
        op("vector", self.cdve, lambda v: v.memset(self.kq[0:1, 1:2], 0.0))
        for hf in range(2):
            t_q2 = op("vector", self.cdve, lambda v, hf=hf: v.tensor_tensor(
                self.bsq[:], self.qg[gi][:, hf * 4096:hf * 4096 + 4096],
                self.qg[gi][:, hf * 4096:hf * 4096 + 4096], ALU.mult),
                waits=((self.cpe, self.t_bsq_free),))
            for pc in range(8):
                self.pe_bank((3, 1))
                wt("tensor", self.cdve, t_q2)
                t_mm = op("tensor", self.cpe, lambda t, pc=pc: t.matmul(
                    self.pq[3][0:1, 512:1024], self.ones_bf[:],
                    self.bsq[:, pc * 512:pc * 512 + 512],
                    start=True, stop=True))
                t_red = op("vector", self.cdve, lambda v: v.reduce_max(
                    self.kq[0:1, 2:3], self.pq[3][0:1, 512:1024], axis=AX.X),
                    waits=((self.cpe, t_mm),))
                self.wt("vector", self.cdve, t_red)
                self.free_bank((3, 1), self.cdve, t_red)
                op("vector", self.cdve, lambda v: v.tensor_tensor(
                    self.kq[0:1, 1:2], self.kq[0:1, 1:2], self.kq[0:1, 2:3],
                    ALU.max))
            self.t_bsq_free = self.cpe.n
        t_prod = op("vector", self.cdve, lambda v: v.tensor_tensor(
            self.kq[0:1, 3:4], self.kq[0:1, 0:1], self.kq[0:1, 1:2],
            ALU.mult))
        op("scalar", self.cact, lambda s: s.activation(
            self.kq[0:1, 3:4], self.kq[0:1, 3:4], AF.Ln,
            bias=self.epsc[0:1, 1:2], scale=1.0),
            waits=((self.cdve, t_prod),))
        t_c = op("scalar", self.cact, lambda s: s.activation(
            self.kq[0:1, 3:4], self.kq[0:1, 3:4], AF.Exp,
            bias=self.zc[0:1, 0:1], scale=0.5))
        t_negc0 = op("vector", self.cdve, lambda v: v.tensor_scalar(
            self.kq[0:1, 4:5], self.kq[0:1, 3:4], -1.0, None, ALU.mult),
            waits=((self.cact, t_c),))
        self.pe_bank((3, 1))
        wt("tensor", self.cdve, t_negc0)
        op("tensor", self.cpe, lambda t: t.matmul(
            self.pq[3][:, 512:513], self.onesrow[:], self.kq[0:1, 4:5],
            start=True, stop=True), inc=False)
        t_bc = self.pe_fence(self.pq[3][0:1, 512:768])
        t_negc = op("vector", self.cdve, lambda v: v.tensor_copy(
            self.negc[:], self.pq[3][:, 512:513]), waits=((self.cpe, t_bc),))
        self.free_bank((3, 1), self.cdve, t_negc)
        if self.dump_scr:
            self.dma("scalar", self.qact,
                     self.dbg_negc[g:g + 1].rearrange("a b -> b a"),
                     self.negc[:], waits=((self.cdve, t_negc),))
            self.dma("scalar", self.qact, self.dbg_kq[g:g + 1], self.kq[:],
                     waits=((self.cdve, t_negc),))
        def osl(s):
            return (self.pq[2][:, 0:512], self.pq[2][:, 512:1024],
                    self.pq[3][:, 0:512])[s]
        OKEY = {0: (2, 0), 1: (2, 1), 2: (3, 0)}
        pending = []

        def emit_chain(job):
            c0, e0, slot, t_avh, hh, g0_, last = job
            if self.dump_scr:
                h_ = 2 * c0 + e0
                op("vector", self.cdve, lambda v, slot=slot, h_=h_:
                   v.tensor_copy(self.stats[0:1, 1280 + 2 * h_:1280 + 2 * h_ + 2],
                                 osl(slot)[64:65, 0:2]),
                   waits=((self.cpe, t_avh),))
            t_rec = op("vector", self.cdve, lambda v, slot=slot:
                       v.reciprocal(self.recrow[:], osl(slot)[64:65, :]),
                       waits=((self.cpe, t_avh),))
            self.pe_bank((3, 1))
            wt("tensor", self.cdve, t_rec)
            op("tensor", self.cpe, lambda t: t.matmul(
                self.pq[3][:, 512:1024], self.onesrow[:], self.recrow[:],
                start=True, stop=True), inc=False)
            t_bcr = self.pe_fence(self.pq[3][0:1, 512:768])
            t_recb = op("vector", self.cdve, lambda v: v.tensor_copy(
                self.recb[:], self.pq[3][:, 512:1024]),
                waits=((self.cpe, t_bcr),))
            self.free_bank((3, 1), self.cdve, t_recb)
            wt("vector", self.qact, self.t_aost_free)
            t_stage = op("vector", self.cdve, lambda v, slot=slot, hh=hh:
                         v.tensor_tensor(
                             self.aost[0:64, hh * 512:hh * 512 + 512],
                             osl(slot)[0:64, :], self.recb[0:64, 0:512],
                             ALU.mult))
            self.free_bank(OKEY[slot], self.cdve, t_stage)
            if hh == 15:
                h0 = 2 * c0 + e0 - 15
                dstap = self.ao_scr.rearrange("c (e p) t -> p c e t", e=2)[
                    :, h0 // 2:h0 // 2 + 8, :, g0_ * 512:g0_ * 512 + 512]
                srcap = self.aost[:].rearrange("p (c e t) -> p c e t",
                                               c=8, e=2)
                t_fl = dma("scalar", self.qact, dstap, srcap,
                           waits=((self.cdve, t_stage),))
                self.t_aost_free = t_fl
                self.t_ao_written = t_fl

        t_exp = {}
        for c in range(16):
            if c < 15:
                lo, hi = 12 + 8 * (g // 2), 20 + 4 * g
            else:
                lo, hi = 0, 20 + 4 * g
            par = c % 2
            t_k = dma("sync", self.qsp, self.kbuf[par][:, 0:(hi - lo) * 128],
                      self.k_scr[c][:, lo * 128:hi * 128],
                      waits=((self.qact, max(self.k_written[lo:hi])),
                             (self.cpe, self.t_kbuf_free[par])))
            t_v = dma("sync", self.qsp, self.vbuf[par][:, 0:(hi - lo) * 130],
                      self.v_scr.rearrange("t p c -> p t c")[
                          :, lo:hi, c * 130:c * 130 + 130],
                      waits=((self.qact, max(self.v_written[lo:hi])),
                             (self.cpe, self.t_vbuf_free[par])))
            tiles = list(range(lo, hi))
            nb = (len(tiles) + 1) // 2
            slot = {0: (2 * c) % 3, 1: (2 * c + 1) % 3}
            t_av = {0: 0, 1: 0}
            first_av = {0: True, 1: True}
            t_avdone = {0: 0, 1: 0}
            for bi in range(nb):
                bt = tiles[2 * bi: 2 * bi + 2]
                for e in range(2):
                    if bi >= 1:
                        self.wt("tensor", self.cact, t_exp.get((e, bi - 1), 0))
                    else:
                        self.pe_bank((e, 0))
                        self.pe_bank((e, 1))
                    wt("tensor", self.qsp, t_k)
                    for k, lt in enumerate(bt):
                        op("tensor", self.cpe, lambda t, k=k, lt=lt, par=par,
                           e=e, gi=gi, c=c, lo=lo: t.matmul(
                               self.pq[e][0:128, k * 512:k * 512 + 512],
                               self.kbuf[par][64 * e:64 * e + 64,
                                              (lt - lo) * 128:
                                              (lt - lo) * 128 + 128],
                               self.qg[gi][64 * e:64 * e + 64,
                                           c * 512:c * 512 + 512],
                               start=True, stop=True), inc=False)
                    t_s = self.pe_fence(self.pq[e][0:1, 0:256])
                    t_exp[(e, bi)] = op(
                        "scalar", self.cact, lambda s, e=e, n=len(bt), bi=bi:
                        s.activation(
                            self.abuf[2 * e + bi % 2][:, 0:n * 512],
                            self.pq[e][:, 0:n * 512], AF.Exp,
                            bias=self.negc[:, 0:1], scale=1.0),
                        waits=((self.cpe, t_s), (self.cdve, t_negc)))
                if c == 0 or (bi == 0 and pending):
                    pass
                if bi == 0 and pending:
                    for job in pending:
                        emit_chain(job)
                    pending = []
                for e in range(2):
                    ab = self.abuf[2 * e + bi % 2]
                    t_msk = 0
                    for k, lt in enumerate(bt):
                        mi = lt - (16 + 4 * g)
                        if mi >= 0:
                            t_msk = op("vector", self.cdve,
                                       lambda v, k=k, ab=ab, mi=mi:
                                       v.tensor_tensor(
                                           ab[:, k * 512:k * 512 + 512],
                                           ab[:, k * 512:k * 512 + 512],
                                           self.masks[:,
                                                      mi * 512:mi * 512 + 512],
                                           ALU.mult),
                                       waits=((self.cact, t_exp[(e, bi)]),
                                              (self.qpool, self.t_sdbb)))
                    wt("tensor", self.cact, t_exp[(e, bi)])
                    if t_msk:
                        wt("tensor", self.cdve, t_msk)
                    wt("tensor", self.qsp, t_v)
                    if first_av[e]:
                        self.pe_bank(OKEY[slot[e]])
                    for k, lt in enumerate(bt):
                        op("tensor", self.cpe, lambda t, k=k, lt=lt, par=par,
                           e=e, ab=ab, first=first_av[e],
                           last=(lt == tiles[-1]), lo=lo, sl=slot[e]:
                           t.matmul(
                               osl(sl)[0:65, :],
                               self.vbuf[par][:, (lt - lo) * 130 + 65 * e:
                                              (lt - lo) * 130 + 65 * e + 65],
                               ab[:, k * 512:k * 512 + 512],
                               start=first, stop=last), inc=False)
                        first_av[e] = False
                    if bt[-1] == tiles[-1]:
                        t_avdone[e] = self.pe_fence(osl(slot[e])[0:1, 0:256])
            for e in range(2):
                self.free_bank((e, 0), self.cact, t_exp[(e, nb - 1)])
                self.free_bank((e, 1), self.cact, t_exp[(e, nb - 1)])
            import os as _os
            for e in range(2):
                pending.append((c, e, slot[e], t_avdone[e], (2 * c + e) % 16,
                                g, c == 15))
            if _os.environ.get("MK_NODEFER"):
                for job in pending:
                    emit_chain(job)
                pending = []
            self.t_kbuf_free[par] = self.cpe.n
            self.t_vbuf_free[par] = self.cpe.n
            self.sprinkle(2)
        for job in pending:
            emit_chain(job)
        pending = []
        if self.dump_scr:
            dma("scalar", self.qact, self.dbg_sums[g:g + 1],
                self.stats[0:1, 1280:1344], waits=((self.cdve, self.cdve.n),))

    def ffn_super(self, sup):
        dma, op, wt = self.dma, self.op, self.wt
        cols = slice(1024 * sup, 1024 * sup + 1024)
        self.lockstep = (self.lockmode == "full")
        t_prev = self.t_super_done
        if sup == 0:
            for eng in ("sync", "gpsimd"):
                for sem in (self.cpe, self.cdve, self.cact, self.cpool):
                    self.wt(eng, sem, sem.n)
                self.wt(eng, self.qsp, self.qsp.n)
                self.wt(eng, self.qact, self.qact.n)
                self.wt(eng, self.qpool, self.qpool.n)
        xsr = self.xn_scr.rearrange("c p t -> p c t")
        t_xn = dma("sync", self.qsp, self.xnao[:].rearrange(
            "p (c t) -> p c t", c=16), xsr[:, :, cols],
            waits=((self.cpe, t_prev), (self.qact, self.t_xnscr_all)))
        t_mm = 0
        for h in range(44):
            par = h % 2
            rg, ru = (2 * h) % 3, (2 * h + 1) % 3
            t_wg = dma("sync", self.qsp, self.wgu[rg][:], self.wg_c[h],
                       waits=((self.cpe, self.t_wgu_free[rg]),
                              (self.qact, self.t_cache_done)))
            t_wu = dma("sync", self.qsp, self.wgu[ru][:], self.wu_c[h],
                       waits=((self.cpe, self.t_wgu_free[ru]),))
            t_gmm = t_umm = 0
            for which, wbuf, t_w, pi in (("g", self.wgu[rg], t_wg, par),
                                         ("u", self.wgu[ru], t_wu,
                                          2 + par)):
                self.pe_bank((pi, 0))
                self.pe_bank((pi, 1))
                wt("tensor", self.qsp, t_w)
                wt("tensor", self.qsp, t_xn)
                for dc in range(16):
                    for s2 in range(2):
                        t_mm = op("tensor", self.cpe,
                                  lambda t, dc=dc, s2=s2, wbuf=wbuf, pi=pi:
                                  t.matmul(
                                      self.pq[pi][:, s2 * 512:s2 * 512 + 512],
                                      wbuf[:, dc * 128:dc * 128 + 128],
                                      self.xnao[:, dc * 1024 + s2 * 512:
                                                dc * 1024 + s2 * 512 + 512],
                                      start=(dc == 0), stop=(dc == 15)),
                                  inc=(dc == 15 and s2 == 1))
                if which == "g":
                    self.t_wgu_free[rg] = t_mm
                    t_gmm = t_mm
                else:
                    self.t_wgu_free[ru] = t_mm
                    t_umm = t_mm
            t_si = op("scalar", self.cact, lambda s, par=par, h=h:
                      s.activation(self.silu_sb[:], self.pq[par][:, 0:1024],
                                   AF.Silu, bias=self.zc[:, 0:1],
                                   scale=self.sgs[:, h:h + 1]),
                      waits=((self.cpe, t_gmm), (self.qsp, self.t_consts),
                             (self.cdve, getattr(self, "t_gu_prev", 0)),))
            t_up = op("scalar", self.cact, lambda s, par=par, h=h:
                      s.activation(self.up_sb[:], self.pq[2 + par][:, 0:1024],
                                   AF.Copy, bias=0.0,
                                   scale=self.sus[:, h:h + 1]),
                      waits=((self.cpe, t_umm),))
            t_up = self.act_fence(self.up_sb[0:1, 1022:1024])
            t_gu = op("vector", self.cdve, lambda v, h=h: v.tensor_tensor(
                self.gub[:, h * 1024:h * 1024 + 1024], self.silu_sb[:],
                self.up_sb[:], ALU.mult),
                waits=((self.cact, t_up), (self.cpe, t_prev)))
            self.t_gu_prev = t_gu
            self.free_bank((par, 0), self.cact, t_si)
            self.free_bank((par, 1), self.cact, t_si)
            self.free_bank((2 + par, 0), self.cact, t_up)
            self.free_bank((2 + par, 1), self.cact, t_up)
        t_gu_all = self.cdve.n
        if self.lockmode in ("tail", "full"):
            self.lockstep = True
        if self.dump_scr and sup == 0:
            self.dma("scalar", self.qact, self.dbg_gu, self.gub[:],
                     waits=((self.cdve, t_gu_all),))
        t_ao = dma("sync", self.qsp, self.xnao[:].rearrange(
            "p (c t) -> p c t", c=16),
            self.ao_scr.rearrange("c p t -> p c t")[:, :, cols],
            waits=((self.cpe, t_mm), (self.qact, self.t_ao_written)))
        for qtr in range(4):
            self._lock_force("tensor")
            self._lock_force("vector")
            for dcq in range(4):
                self.pe_bank((dcq, 0))
                self.pe_bank((dcq, 1))
            DP = 4
            if qtr == 0:
                self.t_wd_ready = [0] * 4
                self._wd_next = 0
                for _ in range(DP):
                    self.wd_load_flat(dma)
            for h in range(44):
                bpar = h % 4
                wt("tensor", self.qsp, self.t_wd_ready[bpar])
                wt("tensor", self.cdve, t_gu_all)
                t_dmm = 0
                for dcq in range(4):
                    for s2 in range(2):
                        t_dmm = op("tensor", self.cpe, lambda t, dcq=dcq,
                                   s2=s2, bpar=bpar, h=h: t.matmul(
                                       self.pq[dcq][:, s2 * 512:s2 * 512 + 512],
                                       self.wdb[bpar][:,
                                                      dcq * 128:dcq * 128 + 128],
                                       self.gub[:, h * 1024 + s2 * 512:
                                                h * 1024 + s2 * 512 + 512],
                                       start=(h == 0), stop=False),
                                   inc=(dcq == 3 and s2 == 1))
                self.t_wdb_free[bpar] = t_dmm
                self.wd_load_flat(dma)
            t_wo_r = [0] * 4
            for dcq in range(4):
                t_wo_r[dcq] = dma(
                    "gpsimd", self.qpool,
                    self.wob[dcq][:].rearrange("p (c j) -> p c j", c=16),
                    self.wo_p[4 * qtr + dcq].rearrange("c p j -> p c j"),
                    waits=((self.cpe, self.t_wob_free[dcq]),))
            for dcq in range(4):
                dc = 4 * qtr + dcq
                par = dcq % 2
                t_wo = t_wo_r[dcq]
                wt("tensor", self.qpool, t_wo)
                wt("tensor", self.qsp, t_ao)
                t_mm2 = 0
                for pc in range(16):
                    for s2 in range(2):
                        t_mm2 = op("tensor", self.cpe,
                                   lambda t, pc=pc, s2=s2, par=par, dcq=dcq:
                                   t.matmul(
                                       self.pq[dcq][:,
                                                    s2 * 512:s2 * 512 + 512],
                                       self.wob[dcq][:,
                                                     pc * 128:pc * 128 + 128],
                                       self.xnao[:, pc * 1024 + s2 * 512:
                                                 pc * 1024 + s2 * 512 + 512],
                                       start=False, stop=(pc == 15)),
                                   inc=(pc == 15 and s2 == 1))
                self.t_wob_free[dcq] = t_mm2
                t_mm2 = self.pe_fence(self.pq[dcq][0:1, 0:256])
                t_xj = dma("sync", self.qsp, self.xjit[par][:],
                           self.xT_own.rearrange("(c p) t -> p c t", p=128)[
                               :, dc, cols],
                           waits=((self.cdve, self.t_xjit_free[par]),))
                if "evac" in getattr(self, "lock_sites", ()):
                    self._lock_force("vector")
                t_ev = op("vector", self.cdve, lambda v, dcq=dcq, dc=dc,
                          par=par: v.scalar_tensor_tensor(
                              self.outsb[par][:], self.pq[dcq][:, 0:1024],
                              self.bos[0:128, dc:dc + 1], self.xjit[par][:],
                              ALU.add, ALU.add),
                          waits=((self.cpe, t_mm2), (self.qsp, t_xj),
                                 (self.qsp, self.t_consts),
                                 (self.qact, self.t_outsb_free[par])))
                self.t_xjit_free[par] = t_ev
                t_evf = self.dve_fence(self.outsb[par][0:1, 0:1])
                self.free_bank((dcq, 0), self.cdve, t_evf)
                self.free_bank((dcq, 1), self.cdve, t_evf)
                t_st = dma("scalar", self.qact, self.out_fm[dc][:, cols],
                           self.outsb[par][:], waits=((self.cdve, t_ev),))
                self.t_outsb_free[par] = t_st
        self.t_super_done = self.cpe.n

    def finalize(self):
        self.wt("sync", self.qact, self.qact.n)
        self.wt("sync", self.qsp, self.qsp.n)
        self.wt("sync", self.qpool, self.qpool.n)



def _prep_shared(Wq, Wk, Wv, Wo, bo, wg, sg, wu, su, wd, sd):
    f32 = np.float32
    pc_layout = lambda w: np.ascontiguousarray(
        w.reshape(16, 128, 16, 128).transpose(0, 3, 2, 1).astype(f32))
    d = {}
    d["wq_p"] = pc_layout(Wq)
    d["wk_p"] = pc_layout(Wk)
    wvT = np.ascontiguousarray(Wv.T)
    d["wv_m"] = np.ascontiguousarray(
        wvT.reshape(16, 128, 2, 1024).transpose(2, 0, 1, 3).astype(f32))
    d["wvg_m"] = np.ascontiguousarray(
        wvT[:, 1920:2048].reshape(16, 128, 128).astype(f32))
    d["wo_p"] = np.ascontiguousarray(
        Wo.reshape(16, 128, 16, 128).transpose(0, 2, 3, 1).astype(f32))
    gl = lambda w: np.ascontiguousarray(
        w.reshape(44, 128, 16, 128).transpose(0, 3, 2, 1).astype(np.int32))
    d["wg_p"] = gl(wg)
    d["wu_p"] = gl(wu)
    wdT = np.ascontiguousarray(wd.T)
    d["wd_p"] = np.ascontiguousarray(
        wdT.reshape(44, 128, 4, 4, 128).transpose(2, 0, 1, 3, 4).astype(
            np.int32))
    d["sg_r"] = np.ascontiguousarray(sg.reshape(44, 128).T.astype(f32))
    d["su_r"] = np.ascontiguousarray(su.reshape(44, 128).T.astype(f32))
    d["bo_r"] = np.ascontiguousarray(bo.reshape(16, 128).T.astype(f32))
    d["sdb"] = np.ascontiguousarray(
        np.broadcast_to(sd[None, :].astype(f32), (128, 2048)))
    m = np.zeros((128, 4, 512), dtype=f32)
    ii = np.arange(128)[:, None]
    jj = np.arange(512)[None, :]
    for dd in range(4):
        m[:, dd, :] = (jj >= ii + dd * 128).astype(f32)
    d["masks_f"] = np.ascontiguousarray(m.reshape(128, 2048))
    r2 = np.zeros((128, 128), dtype=f32)
    for i2 in range(128):
        base = (i2 // 64) * 64
        k = i2 - base
        if k < 32:
            r2[base + k + 32, i2] = -1.0
        else:
            r2[base + k - 32, i2] = 1.0
    d["r2t_f"] = r2
    inv = 1.0 / (10000.0 ** (np.arange(0, 64, 2, dtype=np.float64) / 64.0))
    rowf = inv[np.arange(128) % 32]
    return d, rowf


def _rope_tables(rowf, half):
    f32 = np.float32
    pos_own = np.arange(2048, dtype=np.float64) + 2048.0 * half
    pos_pref = np.arange(2048, dtype=np.float64)
    ang_own = rowf[:, None] * pos_own[None, :]
    ang_pref = rowf[:, None] * pos_pref[None, :]
    cos_q = (np.cos(ang_own) * 0.125).astype(f32)
    sin_q = (np.sin(ang_own) * 0.125).astype(f32)
    cos_k = np.concatenate([np.cos(ang_pref), np.cos(ang_own)],
                           axis=1).astype(f32)
    sin_k = np.concatenate([np.sin(ang_pref), np.sin(ang_own)],
                           axis=1).astype(f32)
    return cos_q, sin_q, cos_k, sin_k


_PROG_CACHE = {}


def _get_prog():
    if "nc" not in _PROG_CACHE:
        _PROG_CACHE["nc"] = MK().build()
    return _PROG_CACHE["nc"]


def kernel(L0_attn_hidden_states, L0_attn_Wq, L0_attn_Wk, L0_attn_Wv,
           L0_attn_Wo, L0_attn_bo, L0_ffn_wq_gate, L0_ffn_s_gate,
           L0_ffn_wq_up, L0_ffn_s_up, L0_ffn_wq_down, L0_ffn_s_down):
    x = np.asarray(L0_attn_hidden_states, dtype=np.float32)
    shared, rowf = _prep_shared(
        np.asarray(L0_attn_Wq, np.float32), np.asarray(L0_attn_Wk, np.float32),
        np.asarray(L0_attn_Wv, np.float32), np.asarray(L0_attn_Wo, np.float32),
        np.asarray(L0_attn_bo, np.float32),
        np.asarray(L0_ffn_wq_gate, np.int32),
        np.asarray(L0_ffn_s_gate, np.float32),
        np.asarray(L0_ffn_wq_up, np.int32),
        np.asarray(L0_ffn_s_up, np.float32),
        np.asarray(L0_ffn_wq_down, np.int32),
        np.asarray(L0_ffn_s_down, np.float32))
    in_maps = []
    for core in range(8):
        b2, half = core // 2, core % 2
        xT = np.ascontiguousarray(x[b2].T)
        cos_q, sin_q, cos_k, sin_k = _rope_tables(rowf, half)
        m = dict(shared)
        m["xT_own"] = np.ascontiguousarray(
            xT[:, 2048 * half:2048 * half + 2048])
        m["xT_pref"] = np.ascontiguousarray(xT[:, 0:2048])
        m["cos_q"], m["sin_q"] = cos_q, sin_q
        m["cos_k"], m["sin_k"] = cos_k, sin_k
        m["pm"] = np.full((128, 1), 1.0 if half else 0.0, dtype=np.float32)
        in_maps.append(m)
    nc = _get_prog()
    res = run_bass_kernel_spmd(nc, in_maps, core_ids=list(range(8)))
    out = np.empty((4, 4096, 2048), dtype=np.float32)
    for core in range(8):
        b2, half = core // 2, core % 2
        fm = res.results[core]["out_fm"].reshape(2048, 2048)
        out[b2, 2048 * half:2048 * half + 2048, :] = fm.T
    return out


# version 16
# speedup vs baseline: 1.0585x; 1.0585x over previous
import contextlib

import numpy as np
import concourse.bass as bass
import concourse.mybir as mybir
from concourse.bass_utils import run_bass_kernel_spmd
from concourse.alu_op_type import AluOpType as ALU

F32 = mybir.dt.float32
BF16 = mybir.dt.bfloat16
I32 = mybir.dt.int32
AF = mybir.ActivationFunctionType
AX = mybir.AxisListType

D = 2048
TOWN = 2048
LSUB = 256
EPS = 1e-5

ENGINES = ("tensor", "vector", "scalar", "gpsimd", "sync")


class SemT:
    def __init__(self, handle):
        self.h = handle
        self.n = 0

    def bump(self, k=1):
        self.n += k
        return self.n


class MK:

    def __init__(self, dump_scr=False):
        self.dump_scr = dump_scr
        self.nc = bass.Bass("TRN2", target_bir_lowering=False)
        nc = self.nc
        ap = lambda n, s, d=F32: nc.dram_tensor(n, s, d,
                                                kind="ExternalInput").ap()
        self.xT_own = ap("xT_own", [D, TOWN])
        self.xT_pref = ap("xT_pref", [D, TOWN])
        self.wq_p = ap("wq_p", [16, 128, 16, 128])
        self.wk_p = ap("wk_p", [16, 128, 16, 128])
        self.wv_m = ap("wv_m", [2, 16, 128, 1024])
        self.wvg_m = ap("wvg_m", [16, 128, 128])
        self.wo_p = ap("wo_p", [16, 16, 128, 128])
        self.wg_p = ap("wg_p", [44, 128, 16, 128], I32)
        self.wu_p = ap("wu_p", [44, 128, 16, 128], I32)
        self.wd_p = ap("wd_p", [4, 44, 128, 4, 128], I32)
        self.sg_r = ap("sg_r", [128, 44])
        self.su_r = ap("su_r", [128, 44])
        self.bo_r = ap("bo_r", [128, 16])
        self.sdb = ap("sdb", [128, 2048])
        self.cos_q = ap("cos_q", [128, 2048])
        self.sin_q = ap("sin_q", [128, 2048])
        self.cos_k = ap("cos_k", [128, 4096])
        self.sin_k = ap("sin_k", [128, 4096])
        self.r2t_f = ap("r2t_f", [128, 128])
        self.masks_f = ap("masks_f", [128, 4 * 512])
        self.pm = ap("pm", [128, 1])
        scr_kind = "ExternalOutput" if dump_scr else "Internal"
        scr = lambda n, s, d: nc.dram_tensor(n, s, d, kind=scr_kind).ap()
        self.xn_scr = scr("xn_scr", [16, 128, 2048], BF16)
        self.ao_scr = scr("ao_scr", [16, 128, 2048], BF16)
        self.k_scr = scr("k_scr", [16, 128, 4096], BF16)
        self.v_scr = scr("v_scr", [32, 128, 2080], BF16)
        self.wg_c = scr("wg_c", [44, 128, 2048], BF16)
        self.wu_c = scr("wu_c", [44, 128, 2048], BF16)
        self.wd_c = scr("wd_c", [4, 44, 128, 512], BF16)
        self.out_fm = nc.dram_tensor("out_fm", [16, 128, 2048], F32,
                                     kind="ExternalOutput").ap()
        if dump_scr:
            self.dbg_negc = nc.dram_tensor("dbg_negc", [4, 128], F32,
                                           kind="ExternalOutput").ap()
            self.dbg_kq = nc.dram_tensor("dbg_kq", [4, 8], F32,
                                         kind="ExternalOutput").ap()
            self.dbg_rec = nc.dram_tensor("dbg_rec", [4, 512], F32,
                                          kind="ExternalOutput").ap()
            self.dbg_sums = nc.dram_tensor("dbg_sums", [4, 64], F32,
                                           kind="ExternalOutput").ap()
            self.dbg_qg = nc.dram_tensor("dbg_qg", [2, 128, 8192], BF16,
                                         kind="ExternalOutput").ap()
            self.dbg_gu = nc.dram_tensor("dbg_gu", [128, 45056], BF16,
                                         kind="ExternalOutput").ap()
            self.dbg_dn = nc.dram_tensor("dbg_dn", [4, 128, 1024], F32,
                                         kind="ExternalOutput").ap()
            self.dbg_wd = nc.dram_tensor("dbg_wd", [4, 44, 128, 512], BF16,
                                         kind="ExternalOutput").ap()

        self.q = {e: [] for e in ENGINES}
        self.last_wait = {}
        self.t_xbuf_free = 0
        self.t_bsq_free = 0
        self.t_xnp_free = 0
        self.t_qg_free = 0
        self.t_kst_free = 0
        self.t_k2_free = 0
        self.t_ropepr = 0
        self.t_cs_free = [0, 0]
        self.t_aost_free = 0
        self.t_ao_written = 0
        self.t_xnscr_all = 0
        self.t_super_done = 0
        self.k_written = [0] * 32
        self.v_written = [0] * 32

    def em(self, eng, fn):
        self.q[eng].append(fn)

    def wt(self, eng, sem, tgt):
        if tgt <= 0:
            return
        key = (eng, id(sem))
        if self.last_wait.get(key, -1) >= tgt:
            return
        self.last_wait[key] = tgt
        self.q[eng].append(lambda E, s=sem.h, t=tgt: E.wait_ge(s, t))

    def flush(self):
        for e in ENGINES:
            fns = self.q[e]
            if not fns:
                continue
            self.q[e] = []

            def section(E, fns=fns):
                for f in fns:
                    f(E)

            getattr(self.block, e)(section)

    def dma(self, qeng, sem, dst, src, waits=()):
        if getattr(self, "lockstep", False):
            self._lock(qeng)
        for (s, t) in waits:
            self.wt(qeng, s, t)
        tgt = sem.bump(16)
        self.em(qeng, lambda Q, d=dst, s=src, h=sem.h:
                Q.dma_start(d, s).then_inc(h, 16))
        return tgt

    def pe_fence(self, elem_ap):
        t = elem_ap.tensor if hasattr(elem_ap, "tensor") else None
        return self.op("tensor", self.cpe, lambda t2: t2.matmul(
            elem_ap, self.zb[:], self.zrow[:], start=False, stop=True))

    def sprinkle(self, k=1):
        if self.cache_jobs:
            self.emit_cache_jobs(k)

    def wd_load_flat(self, dma):
        j = self._wd_next
        if j >= 176:
            return
        self._wd_next += 1
        qtr, h = j // 44, j % 44
        bpar = j % 4
        t_wd = dma("scalar", self.qact, self.wdb[bpar][:],
                   self.wd_c[qtr, h],
                   waits=((self.cpe, self.t_wdb_free[bpar]),
                          (self.qact, self.t_cache_done)))
        self.t_wd_ready[bpar] = t_wd

    def act_fence(self, src_ap):
        return self.op("scalar", self.cact, lambda sc: sc.activation(
            self.fence_sb[0:1, 2:4], src_ap, AF.Copy, bias=0.0, scale=1.0))

    def dve_fence(self, src_ap=None):
        if src_ap is None:
            src_ap = self.fence_sb[0:1, 1:2]
        return self.op("vector", self.cdve, lambda v: v.tensor_copy(
            self.fence_sb[0:1, 0:1], src_ap))

    def _lock_force(self, eng):
        for sem2 in (self.cpe, self.cdve, self.cact, self.qsp):
            self.wt(eng, sem2, sem2.n)

    def _lock(self, eng):
        if self.lock_engs and eng not in self.lock_engs:
            return
        for sem2 in (self.cpe, self.cdve, self.cact, self.cpool,
                     self.qsp, self.qact, self.qpool):
            self.wt(eng, sem2, sem2.n)

    def op(self, eng, sem, fn, waits=(), inc=True):
        if getattr(self, "lockstep", False):
            self._lock(eng)
        for (s, t) in waits:
            self.wt(eng, s, t)
        if inc:
            tgt = sem.bump()
            self.em(eng, lambda E, f=fn, h=sem.h: f(E).then_inc(h, 1))
            return tgt
        self.em(eng, fn)
        return sem.n

    def pe_bank(self, key):
        s, t = self.bank_free[key]
        self.wt("tensor", s, t)

    def free_bank(self, key, sem, tgt):
        self.bank_free[key] = (sem, tgt)

    def psl(self, key):
        i, hlf = key
        return self.pq[i][:, hlf * 512:(hlf + 1) * 512]

    def build(self):
        nc = self.nc
        st = contextlib.ExitStack()
        E = st.enter_context
        with st:
            self.masks = E(nc.sbuf_tensor("masks", [128, 4 * 512], BF16))
            self.r2t = E(nc.sbuf_tensor("r2t", [128, 128], BF16))
            self.sgs = E(nc.sbuf_tensor("sgs", [128, 44], F32))
            self.sus = E(nc.sbuf_tensor("sus", [128, 44], F32))
            self.bos = E(nc.sbuf_tensor("bos", [128, 16], F32))
            self.sdbb = E(nc.sbuf_tensor("sdbb", [128, 2048], BF16))
            self.pmc = E(nc.sbuf_tensor("pmc", [128, 1], F32))
            self.ones_bf = E(nc.sbuf_tensor("ones_bf", [128, 1], BF16))
            self.onesrow = E(nc.sbuf_tensor("onesrow", [1, 128], F32))
            self.zc = E(nc.sbuf_tensor("zc", [128, 1], F32))
            self.fence_sb = E(nc.sbuf_tensor("fence_sb", [1, 8], F32))
            self.dqb = [E(nc.sbuf_tensor("dqb0", [128, 2048], BF16))] * 2
            self.c128b = E(nc.sbuf_tensor("c128b", [128, 2048], BF16))
            self.zb = E(nc.sbuf_tensor("zb", [1, 1], BF16))
            self.zrow = E(nc.sbuf_tensor("zrow", [1, 256], BF16))
            self.epsc = E(nc.sbuf_tensor("epsc", [1, 2], F32))
            self.stats = E(nc.sbuf_tensor("stats", [1, 1536], F32))
            self.kq = E(nc.sbuf_tensor("kq", [1, 8], F32))
            self.negc = E(nc.sbuf_tensor("negc", [128, 1], F32))
            self.rstdb = E(nc.sbuf_tensor("rstdb", [128, 512], F32))
            self.pq = [E(nc.psum_tensor(f"pq{i}", [128, 1024], F32))
                       for i in range(4)]
            self.qsp = SemT(E(nc.semaphore("qsp")))
            self.qact = SemT(E(nc.semaphore("qact")))
            self.qpool = SemT(E(nc.semaphore("qpool")))
            self.cpe = SemT(E(nc.semaphore("cpe")))
            self.cdve = SemT(E(nc.semaphore("cdve")))
            self.cact = SemT(E(nc.semaphore("cact")))
            self.cpool = SemT(E(nc.semaphore("cpool")))
            self.block = E(nc.Block())
            self.bank_free = {(i, h): (self.cdve, 0)
                              for i in range(4) for h in range(2)}

            self.setup()
            self.build_ffn_cache()
            stA = contextlib.ExitStack()
            EA = stA.enter_context
            with stA:
                self.xbuf = EA(nc.sbuf_tensor("xbuf", [128, 16 * LSUB], F32))
                self.recrow = EA(nc.sbuf_tensor("recrow", [1, 512], F32))
                self.recb = EA(nc.sbuf_tensor("recb", [128, 512], F32))
                self.q0b = EA(nc.sbuf_tensor("q0b", [128, 512], BF16))
                self.t1b = EA(nc.sbuf_tensor("t1b", [128, 512], F32))
                self.t2b = EA(nc.sbuf_tensor("t2b", [128, 512], F32))
                self.kst = EA(nc.sbuf_tensor("kst", [128, 512], BF16))
                self.k2b = self.q0b
                self.abuf = [EA(nc.sbuf_tensor(f"abuf{i}", [128, 1024], BF16))
                             for i in range(2)]
                self.csb = [EA(nc.sbuf_tensor(f"csb{i}", [128, 1024], F32))
                            for i in range(2)]


                self.bsq = EA(nc.sbuf_tensor("bsq", [128, 16 * 256], BF16))
                self.xnp = EA(nc.sbuf_tensor("xnp", [128, 16 * 1024], BF16))
                self.qg = [EA(nc.sbuf_tensor(f"qg{i}", [128, 16 * 512], BF16))
                           for i in range(2)]
                self.kbuf = [EA(nc.sbuf_tensor(f"kbuf{i}", [128, 4096], BF16))
                             for i in range(2)]
                self.vbuf = [EA(nc.sbuf_tensor(f"vbuf{i}", [128, 32 * 130],
                                               BF16)) for i in range(2)]
                self.vstage = [EA(nc.sbuf_tensor("vstage0", [128, 1040],
                                                 BF16))] * 2
                self.t_vst_free = [0, 0]
                self.aost = EA(nc.sbuf_tensor("aost", [64, 16 * 512], BF16))
                self.wkq = [EA(nc.sbuf_tensor(f"wkq{i}", [128, 2048], BF16))
                            for i in range(2)]
                self.t_wkq_free = [0, 0]
                self.wvb = [EA(nc.sbuf_tensor(f"wvb{i}", [128, 1024], BF16))
                            for i in range(2)]
                self.t_wvb_free = [0, 0]
                self.t_kbuf_free = [0, 0]
                self.t_vbuf_free = [0, 0]
                stP = contextlib.ExitStack()
                EP = stP.enter_context
                with stP:
                    self.wvgb = EP(nc.sbuf_tensor("wvgb", [128, 2048], BF16))
                    self.wk15 = EP(nc.sbuf_tensor("wk15", [128, 2048], BF16))
                    self.t_wvg = self.dma(
                        "gpsimd", self.qpool,
                        self.wvgb[:].rearrange("p (c j) -> p c j", c=16),
                        self.wvg_m.rearrange("c p j -> p c j"))
                    self.t_wk15 = self.dma(
                        "gpsimd", self.qpool,
                        self.wk15[:].rearrange("p (c j) -> p c j", c=16),
                        self.wk_p[15])
                    for g in range(4):
                        with nc.named_scope(f"prefix{g}"):
                            self.prefix_group(g)
                            self.emit_cache_jobs(12)
                            self.flush()
                self.abuf += [EA(nc.sbuf_tensor(f"abuf{i}", [128, 1024],
                                                BF16)) for i in (2, 3)]
                for pair in range(2):
                    with nc.named_scope(f"pair{pair}"):
                        self.own_pair(pair)
                        self.flush()
                with nc.named_scope("cachetail"):
                    self.emit_cache_jobs(10000)
                    self.flush()
            stB = contextlib.ExitStack()
            EB = stB.enter_context
            with stB:
                self.xnao = EB(nc.sbuf_tensor("xnao", [128, 16 * 1024], BF16))
                self.gub = EB(nc.sbuf_tensor("gub", [128, 44 * 1024], BF16))
                self.wgu = [EB(nc.sbuf_tensor(f"wgu{i}", [128, 2048], BF16))
                            for i in range(3)]
                self.t_wgu_free = [0, 0, 0]
                self.wdb = [EB(nc.sbuf_tensor(f"wdb{i}", [128, 512], BF16))
                            for i in range(4)]
                self.t_wdb_free = [0] * 4
                self.wob = [EB(nc.sbuf_tensor(f"wob{i}", [128, 2048], BF16))
                            for i in range(4)]
                self.t_wob_free = [0] * 4
                self.silu_sb = EB(nc.sbuf_tensor("silu_sb", [128, 1024], F32))
                if self.dump_scr:
                    self.dbg_dn_sb = EB(nc.sbuf_tensor("dbg_dn_sb",
                                                       [128, 1024], F32))
                self.up_sb = EB(nc.sbuf_tensor("up_sb", [128, 1024], F32))
                self.outsb = [EB(nc.sbuf_tensor(f"outsb{i}", [128, 1024], F32))
                              for i in range(2)]
                self.xjit = [EB(nc.sbuf_tensor(f"xjit{i}", [128, 1024], F32))
                             for i in range(2)]
                self.t_outsb_free = [0, 0]
                self.t_xjit_free = [0, 0]
                import os
                self.lockmode = os.environ.get("MK_LOCKMODE", "tail")
                self.lock_engs = tuple(
                    x for x in os.environ.get("MK_LOCK_ENGS", "").split(",")
                    if x)
                self.lock_sites = tuple(
                    x for x in os.environ.get("MK_LOCK_SITES", "").split(",")
                    if x)
                for sup in range(2):
                    with nc.named_scope(f"ffn{sup}"):
                        self.ffn_super(sup)
                        self.flush()
                self.lockstep = False
            self.finalize()
            self.flush()
        return nc

    def setup(self):
        self.dma("gpsimd", self.qpool, self.masks[:], self.masks_f)
        self.dma("gpsimd", self.qpool, self.r2t[:], self.r2t_f)
        self.t_sdbb = self.dma("gpsimd", self.qpool, self.sdbb[:], self.sdb)
        self.dma("sync", self.qsp, self.sgs[:], self.sg_r)
        self.dma("sync", self.qsp, self.sus[:], self.su_r)
        self.dma("sync", self.qsp, self.bos[:], self.bo_r)
        self.t_consts = self.dma("sync", self.qsp, self.pmc[:], self.pm)
        self.op("vector", self.cdve, lambda v: v.memset(self.ones_bf[:], 1.0))
        self.op("vector", self.cdve, lambda v: v.memset(self.zc[:], 0.0))
        self.op("vector", self.cdve, lambda v: v.memset(self.epsc[0:1, 0:1], EPS))
        self.op("vector", self.cdve, lambda v: v.memset(self.epsc[0:1, 1:2], 1e-18))
        self.op("vector", self.cdve, lambda v: v.memset(self.onesrow[:], 1.0))
        self.op("vector", self.cdve, lambda v: v.memset(self.zb[:], 0.0))
        self.op("vector", self.cdve, lambda v: v.memset(self.zrow[:], 0.0))
        self.t_c128 = self.op("gpsimd", self.cpool,
                              lambda p: p.memset(self.c128b[:], 128.0))
        self.op("vector", self.cdve, lambda v: v.memset(self.fence_sb[:], 0.0))
        self.t_setup_dve = self.op("vector", self.cdve,
                                   lambda v: v.memset(self.kq[:], 0.0))
        self.flush()

    def build_ffn_cache(self):
        self.t_dq_free = [0, 0]
        self.dq_i = 0
        self.cache_jobs = []
        for h in range(44):
            self.cache_jobs.append(("g", h))
            self.cache_jobs.append(("u", h))
        for qtr in range(4):
            for h in range(44):
                self.cache_jobs.append(("d", (qtr, h)))
        self.t_cache_done = 0

    def emit_cache_jobs(self, k):
        dma, op = self.dma, self.op
        for _ in range(min(k, len(self.cache_jobs))):
            kind, arg = self.cache_jobs.pop(0)
            par = self.dq_i % 2
            self.dq_i += 1
            if kind in ("g", "u"):
                h = arg
                src = (self.wg_p if kind == "g" else self.wu_p)[h]
                dst = (self.wg_c if kind == "g" else self.wu_c)[h]
                cols = 2048
                sd = None
            else:
                qtr, h = arg
                src = self.wd_p[qtr, h]
                dst = self.wd_c[qtr, h]
                cols = 512
                sd = slice(qtr * 512, qtr * 512 + 512)
            t_in = dma("gpsimd", self.qpool, self.dqb[par][:, 0:cols], src,
                       waits=((self.qact, self.t_dq_free[par]),))
            t_d = op("gpsimd", self.cpool, lambda v, par=par, cols=cols:
                     v.tensor_tensor(self.dqb[par][:, 0:cols],
                                     self.dqb[par][:, 0:cols],
                                     self.c128b[:, 0:cols], ALU.subtract),
                     waits=((self.qpool, t_in), (self.cpool, self.t_c128)))
            if sd is not None:
                t_d = op("gpsimd", self.cpool, lambda v, par=par, ss=sd,
                         cols=cols: v.tensor_tensor(
                             self.dqb[par][:, 0:cols],
                             self.dqb[par][:, 0:cols],
                             self.sdbb[:, ss], ALU.mult),
                         waits=((self.qpool, self.t_sdbb),))
            t_o = dma("scalar", self.qact, dst, self.dqb[par][:, 0:cols],
                      waits=((self.cpool, t_d),))
            self.t_dq_free = [t_o, t_o]
        self.t_cache_done = self.qact.n

    def ln_sub(self, xT, lt_base, sub, dst_off):
        op, wt, dma = self.op, self.wt, self.dma
        c0 = lt_base * 128 + sub * LSUB
        xr = xT.rearrange("(c p) t -> p c t", p=128)
        tx = dma("sync", self.qsp, self.xbuf[:].rearrange(
            "p (c t) -> p c t", c=16), xr[:, :, c0:c0 + LSUB],
            waits=((self.cdve, self.t_xbuf_free),))
        txb = dma("gpsimd", self.qpool, self.bsq[:, 0:16 * LSUB].rearrange(
            "p (c t) -> p c t", c=16), xr[:, :, c0:c0 + LSUB],
            waits=((self.cpe, self.t_bsq_free),))
        self.pe_bank((3, 0))
        wt("tensor", self.qpool, txb)
        wt("tensor", self.cdve, self.t_setup_dve)
        t_sum = 0
        for dc in range(16):
            t_sum = op("tensor", self.cpe, lambda t, dc=dc: t.matmul(
                self.pq[3][0:1, 0:LSUB], self.ones_bf[:],
                self.bsq[:, dc * LSUB:(dc + 1) * LSUB],
                start=(dc == 0), stop=(dc == 15)), inc=(dc == 15))
        t_sq = op("vector", self.cdve, lambda v: v.tensor_tensor(
            self.bsq[:, 0:16 * LSUB], self.bsq[:, 0:16 * LSUB],
            self.bsq[:, 0:16 * LSUB], ALU.mult), waits=((self.cpe, t_sum),))
        self.pe_bank((3, 1))
        wt("tensor", self.cdve, t_sq)
        t_ssq = 0
        for dc in range(16):
            t_ssq = op("tensor", self.cpe, lambda t, dc=dc: t.matmul(
                self.pq[3][0:1, 512:512 + LSUB], self.ones_bf[:],
                self.bsq[:, dc * LSUB:(dc + 1) * LSUB],
                start=(dc == 0), stop=(dc == 15)), inc=(dc == 15))
        self.t_bsq_free = t_ssq
        st = self.stats
        MEAN, VAR, M2, RSTD, MRS = (0, 256, 512, 768, 1024)
        op("vector", self.cdve, lambda v: v.tensor_scalar(
            st[0:1, MEAN:MEAN + LSUB], self.pq[3][0:1, 0:LSUB], 1.0 / D, None,
            ALU.mult), waits=((self.cpe, t_sum),))
        t_rd1 = op("vector", self.cdve, lambda v: v.tensor_scalar(
            st[0:1, VAR:VAR + LSUB], self.pq[3][0:1, 512:512 + LSUB],
            1.0 / D, None, ALU.mult), waits=((self.cpe, t_ssq),))
        self.free_bank((3, 0), self.cdve, t_rd1)
        self.free_bank((3, 1), self.cdve, t_rd1)
        op("vector", self.cdve, lambda v: v.tensor_tensor(
            st[0:1, M2:M2 + LSUB], st[0:1, MEAN:MEAN + LSUB],
            st[0:1, MEAN:MEAN + LSUB], ALU.mult))
        t_var = op("vector", self.cdve, lambda v: v.tensor_tensor(
            st[0:1, VAR:VAR + LSUB], st[0:1, VAR:VAR + LSUB],
            st[0:1, M2:M2 + LSUB], ALU.subtract))
        op("scalar", self.cact, lambda s: s.activation(
            st[0:1, RSTD:RSTD + LSUB], st[0:1, VAR:VAR + LSUB], AF.Ln,
            bias=self.epsc[0:1, 0:1], scale=1.0),
            waits=((self.cdve, t_var),))
        t_rstd = op("scalar", self.cact, lambda s: s.activation(
            st[0:1, RSTD:RSTD + LSUB], st[0:1, RSTD:RSTD + LSUB], AF.Exp,
            bias=self.zc[0:1, 0:1], scale=-0.5))
        t_mrs = op("vector", self.cdve, lambda v: v.tensor_tensor(
            st[0:1, MRS:MRS + LSUB], st[0:1, MEAN:MEAN + LSUB],
            st[0:1, RSTD:RSTD + LSUB], ALU.mult),
            waits=((self.cact, t_rstd),))
        self.pe_bank((2, 0))
        self.pe_bank((2, 1))
        wt("tensor", self.cdve, t_mrs)
        op("tensor", self.cpe, lambda t: t.matmul(
            self.pq[2][:, 0:LSUB], self.onesrow[:],
            st[0:1, RSTD:RSTD + LSUB], start=True, stop=True), inc=False)
        t_bc = op("tensor", self.cpe, lambda t: t.matmul(
            self.pq[2][:, 512:512 + LSUB], self.onesrow[:],
            st[0:1, MRS:MRS + LSUB], start=True, stop=True))
        op("vector", self.cdve, lambda v: v.tensor_copy(
            self.rstdb[:, 0:LSUB], self.pq[2][:, 0:LSUB]),
            waits=((self.cpe, t_bc),))
        t_ev = op("vector", self.cdve, lambda v: v.tensor_copy(
            self.rstdb[:, 256:256 + LSUB], self.pq[2][:, 512:512 + LSUB]))
        self.free_bank((2, 0), self.cdve, t_ev)
        self.free_bank((2, 1), self.cdve, t_ev)
        xap = self.xbuf[:].rearrange("p (c t) -> p c t", c=16)
        rrep = bass.AP(self.rstdb, 0, [[512, 128], [0, 16], [1, LSUB]])
        mrep = bass.AP(self.rstdb, 256, [[512, 128], [0, 16], [1, LSUB]])
        wt("vector", self.qsp, tx)
        op("vector", self.cdve, lambda v: v.tensor_tensor(
            xap, xap, rrep, ALU.mult))
        dap = self.xnp[:].rearrange("p (c t) -> p c t", c=16)[
            :, :, dst_off + sub * LSUB: dst_off + sub * LSUB + LSUB]
        wt("vector", self.cpe, self.t_xnp_free)
        t_xn = op("vector", self.cdve, lambda v: v.tensor_tensor(
            dap, xap, mrep, ALU.subtract))
        self.t_xbuf_free = t_xn
        return t_xn

    def rope(self, pkey, dst_ap, cs_i, cs_tgt, t_proj, dst_waits=()):
        op, wt = self.op, self.wt
        ps = self.psl(pkey)
        t_q0 = op("vector", self.cdve, lambda v: v.tensor_copy(
            self.q0b[:], ps),
            waits=((self.cpe, t_proj), (self.cpe, self.t_ropepr)))
        self.pe_bank((2, 0))
        wt("tensor", self.cdve, t_q0)
        t_pr = op("tensor", self.cpe, lambda t: t.matmul(
            self.pq[2][:, 0:512], self.r2t[:], self.q0b[:],
            start=True, stop=True))
        self.t_ropepr = t_pr
        cs = self.csb[cs_i]
        wt("vector", self.qsp, cs_tgt)
        op("vector", self.cdve, lambda v: v.tensor_tensor(
            self.t1b[:], self.q0b[:], cs[:, 0:512], ALU.mult))
        t_t2 = op("vector", self.cdve, lambda v: v.tensor_tensor(
            self.t2b[:], self.pq[2][:, 0:512], cs[:, 512:1024], ALU.mult),
            waits=((self.cpe, t_pr),))
        self.free_bank((2, 0), self.cdve, t_t2)
        for (s, t) in dst_waits:
            wt("vector", s, t)
        t_out = op("vector", self.cdve, lambda v: v.tensor_tensor(
            dst_ap, self.t1b[:], self.t2b[:], ALU.add))
        return t_out

    def knorm(self):
        op, wt = self.op, self.wt
        t_k2 = op("vector", self.cdve, lambda v: v.tensor_tensor(
            self.k2b[:], self.kst[:], self.kst[:], ALU.mult),
            waits=((self.cpe, self.t_k2_free),))
        self.pe_bank((3, 0))
        wt("tensor", self.cdve, t_k2)
        t_mm = op("tensor", self.cpe, lambda t: t.matmul(
            self.pq[3][0:1, 0:512], self.ones_bf[:], self.k2b[:],
            start=True, stop=True))
        self.t_k2_free = t_mm
        t_red = op("vector", self.cdve, lambda v: v.reduce_max(
            self.kq[0:1, 2:3], self.pq[3][0:1, 0:512], axis=AX.X),
            waits=((self.cpe, t_mm),))
        self.wt("vector", self.cdve, t_red)
        self.free_bank((3, 0), self.cdve, t_red)
        op("vector", self.cdve, lambda v: v.tensor_tensor(
            self.kq[0:1, 0:1], self.kq[0:1, 0:1], self.kq[0:1, 2:3], ALU.max))

    def prefix_group(self, g):
        dma, op, wt = self.dma, self.op, self.wt
        for sub in range(2):
            t_xn = self.ln_sub(self.xT_pref, 4 * g, sub, 0)
        dma("sync", self.qsp, self.csb[0][:, 0:512],
            self.cos_k[:, 512 * g:512 * g + 512],
            waits=((self.cdve, self.t_cs_free[0]),))
        t_cs = dma("sync", self.qsp, self.csb[0][:, 512:1024],
                   self.sin_k[:, 512 * g:512 * g + 512])
        pcs = [15] if g < 3 else list(range(16))
        for i, pc in enumerate(pcs):
            par = i % 2
            if pc == 15:
                wb, t_w = self.wk15, self.t_wk15
            else:
                wb = self.wkq[par]
                t_w = dma("gpsimd", self.qpool, wb[:].rearrange("p (c j) -> p c j", c=16),
                          self.wk_p[pc],
                          waits=((self.cpe, self.t_wkq_free[par]),))
            key = (par, 0)
            self.pe_bank(key)
            wt("tensor", self.qpool, t_w)
            wt("tensor", self.cdve, t_xn)
            t_mm = 0
            for dc in range(16):
                t_mm = op("tensor", self.cpe, lambda t, dc=dc, wb=wb, key=key:
                          t.matmul(self.psl(key),
                                   wb[:, dc * 128:dc * 128 + 128],
                                   self.xnp[:, dc * 1024:dc * 1024 + 512],
                                   start=(dc == 0), stop=(dc == 15)),
                          inc=(dc == 15))
            if pc != 15:
                self.t_wkq_free[par] = t_mm
            t_rope = self.rope(key, self.kst[:], 0, t_cs, t_mm,
                               dst_waits=((self.qact, self.t_kst_free),))
            self.free_bank(key, self.cdve, t_rope)
            t_st = dma("scalar", self.qact,
                       self.k_scr[pc][:, 512 * g:512 * g + 512], self.kst[:],
                       waits=((self.cdve, t_rope),))
            self.t_kst_free = t_st
            for lt in range(4 * g, 4 * g + 4):
                self.k_written[lt] = max(self.k_written[lt], t_st)
            self.knorm()
        self.t_cs_free[0] = self.cdve.n
        if g < 3:
            for tt in range(4):
                key = (tt % 2, 1)
                self.pe_bank(key)
                wt("tensor", self.qpool, self.t_wvg)
                t_mm = 0
                for dc in range(16):
                    t_mm = op("tensor", self.cpe,
                              lambda t, dc=dc, tt=tt, key=key: t.matmul(
                                  self.psl(key)[:, 0:128],
                                  self.xnp[:, dc * 1024 + tt * 128:
                                           dc * 1024 + tt * 128 + 128],
                                  self.wvgb[:, dc * 128:dc * 128 + 128],
                                  start=(dc == 0), stop=(dc == 15)),
                              inc=(dc == 15))
                self.v_evac_global(4 * g + tt, key, t_mm)
        else:
            self.v_own_quad(list(range(12, 16)), 0, prefix=True)
        self.t_xnp_free = self.cpe.n

    def v_evac_global(self, lt, key, t_mm):
        dma, op, wt = self.dma, self.op, self.wt
        par = lt % 2
        vs = self.vstage[par]
        wt("vector", self.qact, self.t_vst_free[par])
        src = self.psl(key)[:, 0:128].rearrange("p (h d) -> p h d", d=64)
        vr = vs[:, 0:1040].rearrange("p (h d) -> p h d", d=65)
        t_ev = op("vector", self.cdve, lambda v: v.tensor_scalar(
            vr[:, 14:16, 0:64], src, self.pmc[:, 0:1], None, ALU.mult),
            waits=((self.cpe, t_mm), (self.qsp, self.t_consts)))
        self.free_bank(key, self.cdve, t_ev)
        pmrep = bass.AP(self.pmc, 0, [[1, 128], [0, 2], [0, 1]])
        t_ev = op("vector", self.cdve, lambda v: v.tensor_scalar(
            vr[:, 14:16, 64:65], pmrep, 1.0, None, ALU.mult))
        t_st = dma("scalar", self.qact, self.v_scr[lt][:, 1950:2080],
                   vs[:, 910:1040], waits=((self.cdve, t_ev),))
        self.t_vst_free = [t_st, t_st]
        self.v_written[lt] = max(self.v_written[lt], t_st)

    def v_own_quad(self, lts, xoff, prefix=False):
        dma, op, wt = self.dma, self.op, self.wt
        for half in range(2):
            t_mm = 0
            for dc in range(16):
                par = dc % 2
                t_w = dma("gpsimd", self.qpool, self.wvb[par][:],
                          self.wv_m[half, dc],
                          waits=((self.cpe, self.t_wvb_free[par]),))
                wt("tensor", self.qpool, t_w)
                for i in range(4):
                    if dc == 0:
                        self.pe_bank((i, 0))
                        self.pe_bank((i, 1))
                    for s2 in range(2):
                        t_mm = op("tensor", self.cpe,
                                  lambda t, dc=dc, i=i, s2=s2, par=par:
                                  t.matmul(
                                      self.pq[i][:, s2 * 512:s2 * 512 + 512],
                                      self.xnp[:, dc * 1024 + xoff + i * 128:
                                               dc * 1024 + xoff + i * 128 + 128],
                                      self.wvb[par][:, s2 * 512:s2 * 512 + 512],
                                      start=(dc == 0), stop=(dc == 15)),
                                  inc=(i == 3 and s2 == 1))
                self.t_wvb_free[par] = t_mm
            for i, lt in enumerate(lts):
                par = i % 2
                vs = self.vstage[par]
                wt("vector", self.qact, self.t_vst_free[par])
                src = self.pq[i][:, 0:1024].rearrange("p (h d) -> p h d", d=64)
                vr = vs[:, 0:1040].rearrange("p (h d) -> p h d", d=65)
                if prefix:
                    t_ev = op("vector", self.cdve, lambda v, s=src, vr=vr:
                              v.tensor_scalar(vr[:, :, 0:64], s,
                                              self.pmc[:, 0:1], None, ALU.mult),
                              waits=((self.cpe, t_mm),
                                     (self.qsp, self.t_consts)))
                    pmrep = bass.AP(self.pmc, 0, [[1, 128], [0, 16], [0, 1]])
                    t_ev = op("vector", self.cdve, lambda v, vr=vr:
                              v.tensor_scalar(vr[:, :, 64:65], pmrep, 1.0,
                                              None, ALU.mult))
                else:
                    t_ev = op("vector", self.cdve, lambda v, s=src, vr=vr:
                              v.tensor_copy(vr[:, :, 0:64], s),
                              waits=((self.cpe, t_mm),))
                    t_ev = op("vector", self.cdve, lambda v, vr=vr:
                              v.memset(vr[:, :, 64:65], 1.0))
                t_st = dma("scalar", self.qact,
                           self.v_scr[lt][:, half * 1040:half * 1040 + 1040],
                           vs[:], waits=((self.cdve, t_ev),))
                self.t_vst_free = [t_st, t_st]
                self.v_written[lt] = max(self.v_written[lt], t_st)
                self.free_bank((i, 0), self.cdve, t_ev)
                self.free_bank((i, 1), self.cdve, t_ev)

    def own_pair(self, pair):
        dma, op, wt = self.dma, self.op, self.wt
        g0, g1 = 2 * pair, 2 * pair + 1
        xsr = self.xn_scr.rearrange("c p t -> p c t")

        def ln_group(gi, g):
            t_xn = 0
            for sub in range(2):
                t_xn = self.ln_sub(self.xT_own, 4 * g, sub, gi * 512)
            src = self.xnp[:].rearrange("p (c t) -> p c t", c=16)[
                :, :, gi * 512:gi * 512 + 512]
            self.t_xnscr_all = dma(
                "scalar", self.qact, xsr[:, :, g * 512:g * 512 + 512], src,
                waits=((self.cdve, t_xn),))
            return t_xn

        def proj(which, sp, g, t_xn):
            t_cs0 = dma("sync", self.qsp, self.csb[sp][:, 0:512],
                        (self.cos_q if which == "q" else self.cos_k)[
                            :, (2048 if which == "k" else 0) + g * 512:
                            (2048 if which == "k" else 0) + g * 512 + 512],
                        waits=((self.cdve, self.t_cs_free[sp]),))
            t_cs = dma("sync", self.qsp, self.csb[sp][:, 512:1024],
                       (self.sin_q if which == "q" else self.sin_k)[
                           :, (2048 if which == "k" else 0) + g * 512:
                           (2048 if which == "k" else 0) + g * 512 + 512])
            wsrc = self.wq_p if which == "q" else self.wk_p
            for pc in range(16):
                par = pc % 2
                key = (pc % 2, 0)
                t_w = dma("gpsimd", self.qpool,
                          self.wkq[par][:].rearrange("p (c j) -> p c j", c=16),
                          wsrc[pc], waits=((self.cpe, self.t_wkq_free[par]),))
                self.pe_bank(key)
                wt("tensor", self.qpool, t_w)
                wt("tensor", self.cdve, t_xn)
                t_mm = 0
                for dc in range(16):
                    t_mm = op("tensor", self.cpe,
                              lambda t, dc=dc, sp=sp, par=par, key=key:
                              t.matmul(
                                  self.psl(key),
                                  self.wkq[par][:, dc * 128:dc * 128 + 128],
                                  self.xnp[:, dc * 1024 + sp * 512:
                                           dc * 1024 + sp * 512 + 512],
                                  start=(dc == 0), stop=(dc == 15)),
                              inc=False)
                t_mm = self.pe_fence(self.psl(key)[0:1, 0:256])
                self.t_wkq_free[par] = t_mm
                if which == "q":
                    t_rope = self.rope(
                        key, self.qg[sp][:, pc * 512:pc * 512 + 512], sp,
                        t_cs, t_mm, dst_waits=((self.cpe, self.t_qg_free),))
                    self.free_bank(key, self.cdve, t_rope)
                else:
                    t_rope = self.rope(key, self.kst[:], sp, t_cs, t_mm,
                                       dst_waits=((self.qact,
                                                   self.t_kst_free),))
                    self.free_bank(key, self.cdve, t_rope)
                    col = 2048 + g * 512
                    t_st = dma("scalar", self.qact,
                               self.k_scr[pc][:, col:col + 512], self.kst[:],
                               waits=((self.cdve, t_rope),))
                    self.t_kst_free = t_st
                    for lt in range(16 + 4 * g, 16 + 4 * g + 4):
                        self.k_written[lt] = max(self.k_written[lt], t_st)
                    self.knorm()
                self.sprinkle()
            self.t_cs_free[sp] = self.cdve.n

        t_xn0 = ln_group(0, g0)
        proj("q", 0, g0, t_xn0)
        proj("k", 0, g0, t_xn0)
        t_xn1 = ln_group(1, g1)
        self.v_own_quad([16 + 8 * pair + i for i in range(4)], 0,
                        prefix=False)
        self.attention(g0, 0)
        proj("q", 1, g1, t_xn1)
        proj("k", 1, g1, t_xn1)
        self.v_own_quad([20 + 8 * pair + i for i in range(4)], 512,
                        prefix=False)
        self.t_xnp_free = self.cpe.n
        self.attention(g1, 1)
        self.t_qg_free = self.cpe.n

    def attention(self, g, gi):
        dma, op, wt = self.dma, self.op, self.wt
        op("vector", self.cdve, lambda v: v.memset(self.kq[0:1, 1:2], 0.0))
        for hf in range(2):
            t_q2 = op("vector", self.cdve, lambda v, hf=hf: v.tensor_tensor(
                self.bsq[:], self.qg[gi][:, hf * 4096:hf * 4096 + 4096],
                self.qg[gi][:, hf * 4096:hf * 4096 + 4096], ALU.mult),
                waits=((self.cpe, self.t_bsq_free),))
            for pc in range(8):
                self.pe_bank((3, 1))
                wt("tensor", self.cdve, t_q2)
                t_mm = op("tensor", self.cpe, lambda t, pc=pc: t.matmul(
                    self.pq[3][0:1, 512:1024], self.ones_bf[:],
                    self.bsq[:, pc * 512:pc * 512 + 512],
                    start=True, stop=True))
                t_red = op("vector", self.cdve, lambda v: v.reduce_max(
                    self.kq[0:1, 2:3], self.pq[3][0:1, 512:1024], axis=AX.X),
                    waits=((self.cpe, t_mm),))
                self.wt("vector", self.cdve, t_red)
                self.free_bank((3, 1), self.cdve, t_red)
                op("vector", self.cdve, lambda v: v.tensor_tensor(
                    self.kq[0:1, 1:2], self.kq[0:1, 1:2], self.kq[0:1, 2:3],
                    ALU.max))
            self.t_bsq_free = self.cpe.n
        t_prod = op("vector", self.cdve, lambda v: v.tensor_tensor(
            self.kq[0:1, 3:4], self.kq[0:1, 0:1], self.kq[0:1, 1:2],
            ALU.mult))
        op("scalar", self.cact, lambda s: s.activation(
            self.kq[0:1, 3:4], self.kq[0:1, 3:4], AF.Ln,
            bias=self.epsc[0:1, 1:2], scale=1.0),
            waits=((self.cdve, t_prod),))
        t_c = op("scalar", self.cact, lambda s: s.activation(
            self.kq[0:1, 3:4], self.kq[0:1, 3:4], AF.Exp,
            bias=self.zc[0:1, 0:1], scale=0.5))
        t_negc0 = op("vector", self.cdve, lambda v: v.tensor_scalar(
            self.kq[0:1, 4:5], self.kq[0:1, 3:4], -1.0, None, ALU.mult),
            waits=((self.cact, t_c),))
        self.pe_bank((3, 1))
        wt("tensor", self.cdve, t_negc0)
        op("tensor", self.cpe, lambda t: t.matmul(
            self.pq[3][:, 512:513], self.onesrow[:], self.kq[0:1, 4:5],
            start=True, stop=True), inc=False)
        t_bc = self.pe_fence(self.pq[3][0:1, 512:768])
        t_negc = op("vector", self.cdve, lambda v: v.tensor_copy(
            self.negc[:], self.pq[3][:, 512:513]), waits=((self.cpe, t_bc),))
        self.free_bank((3, 1), self.cdve, t_negc)
        if self.dump_scr:
            self.dma("scalar", self.qact,
                     self.dbg_negc[g:g + 1].rearrange("a b -> b a"),
                     self.negc[:], waits=((self.cdve, t_negc),))
            self.dma("scalar", self.qact, self.dbg_kq[g:g + 1], self.kq[:],
                     waits=((self.cdve, t_negc),))
        def osl(s):
            return (self.pq[2][:, 0:512], self.pq[2][:, 512:1024],
                    self.pq[3][:, 0:512])[s]
        OKEY = {0: (2, 0), 1: (2, 1), 2: (3, 0)}
        pending = []

        def emit_chain(job):
            c0, e0, slot, t_avh, hh, g0_, last = job
            if self.dump_scr:
                h_ = 2 * c0 + e0
                op("vector", self.cdve, lambda v, slot=slot, h_=h_:
                   v.tensor_copy(self.stats[0:1, 1280 + 2 * h_:1280 + 2 * h_ + 2],
                                 osl(slot)[64:65, 0:2]),
                   waits=((self.cpe, t_avh),))
            t_rec = op("vector", self.cdve, lambda v, slot=slot:
                       v.reciprocal(self.recrow[:], osl(slot)[64:65, :]),
                       waits=((self.cpe, t_avh),))
            self.pe_bank((3, 1))
            wt("tensor", self.cdve, t_rec)
            op("tensor", self.cpe, lambda t: t.matmul(
                self.pq[3][:, 512:1024], self.onesrow[:], self.recrow[:],
                start=True, stop=True), inc=False)
            t_bcr = self.pe_fence(self.pq[3][0:1, 512:768])
            t_recb = op("vector", self.cdve, lambda v: v.tensor_copy(
                self.recb[:], self.pq[3][:, 512:1024]),
                waits=((self.cpe, t_bcr),))
            self.free_bank((3, 1), self.cdve, t_recb)
            wt("vector", self.qact, self.t_aost_free)
            t_stage = op("vector", self.cdve, lambda v, slot=slot, hh=hh:
                         v.tensor_tensor(
                             self.aost[0:64, hh * 512:hh * 512 + 512],
                             osl(slot)[0:64, :], self.recb[0:64, 0:512],
                             ALU.mult))
            self.free_bank(OKEY[slot], self.cdve, t_stage)
            if hh == 15:
                h0 = 2 * c0 + e0 - 15
                dstap = self.ao_scr.rearrange("c (e p) t -> p c e t", e=2)[
                    :, h0 // 2:h0 // 2 + 8, :, g0_ * 512:g0_ * 512 + 512]
                srcap = self.aost[:].rearrange("p (c e t) -> p c e t",
                                               c=8, e=2)
                t_fl = dma("scalar", self.qact, dstap, srcap,
                           waits=((self.cdve, t_stage),))
                self.t_aost_free = t_fl
                self.t_ao_written = t_fl

        t_exp = {}
        for c in range(16):
            if c < 15:
                lo, hi = 12 + 8 * (g // 2), 20 + 4 * g
            else:
                lo, hi = 0, 20 + 4 * g
            par = c % 2
            t_k = dma("sync", self.qsp, self.kbuf[par][:, 0:(hi - lo) * 128],
                      self.k_scr[c][:, lo * 128:hi * 128],
                      waits=((self.qact, max(self.k_written[lo:hi])),
                             (self.cpe, self.t_kbuf_free[par])))
            t_v = dma("sync", self.qsp, self.vbuf[par][:, 0:(hi - lo) * 130],
                      self.v_scr.rearrange("t p c -> p t c")[
                          :, lo:hi, c * 130:c * 130 + 130],
                      waits=((self.qact, max(self.v_written[lo:hi])),
                             (self.cpe, self.t_vbuf_free[par])))
            tiles = list(range(lo, hi))
            nb = (len(tiles) + 1) // 2
            slot = {0: (2 * c) % 3, 1: (2 * c + 1) % 3}
            t_av = {0: 0, 1: 0}
            first_av = {0: True, 1: True}
            t_avdone = {0: 0, 1: 0}
            for bi in range(nb):
                bt = tiles[2 * bi: 2 * bi + 2]
                for e in range(2):
                    if bi >= 1:
                        self.wt("tensor", self.cact, t_exp.get((e, bi - 1), 0))
                    else:
                        self.pe_bank((e, 0))
                        self.pe_bank((e, 1))
                    wt("tensor", self.qsp, t_k)
                    for k, lt in enumerate(bt):
                        op("tensor", self.cpe, lambda t, k=k, lt=lt, par=par,
                           e=e, gi=gi, c=c, lo=lo: t.matmul(
                               self.pq[e][0:128, k * 512:k * 512 + 512],
                               self.kbuf[par][64 * e:64 * e + 64,
                                              (lt - lo) * 128:
                                              (lt - lo) * 128 + 128],
                               self.qg[gi][64 * e:64 * e + 64,
                                           c * 512:c * 512 + 512],
                               start=True, stop=True), inc=False)
                    t_s = self.pe_fence(self.pq[e][0:1, 0:256])
                    t_exp[(e, bi)] = op(
                        "scalar", self.cact, lambda s, e=e, n=len(bt), bi=bi:
                        s.activation(
                            self.abuf[2 * e + bi % 2][:, 0:n * 512],
                            self.pq[e][:, 0:n * 512], AF.Exp,
                            bias=self.negc[:, 0:1], scale=1.0),
                        waits=((self.cpe, t_s), (self.cdve, t_negc)))
                if c == 0 or (bi == 0 and pending):
                    pass
                if bi == 0 and pending:
                    for job in pending:
                        emit_chain(job)
                    pending = []
                for e in range(2):
                    ab = self.abuf[2 * e + bi % 2]
                    t_msk = 0
                    for k, lt in enumerate(bt):
                        mi = lt - (16 + 4 * g)
                        if mi >= 0:
                            t_msk = op("vector", self.cdve,
                                       lambda v, k=k, ab=ab, mi=mi:
                                       v.tensor_tensor(
                                           ab[:, k * 512:k * 512 + 512],
                                           ab[:, k * 512:k * 512 + 512],
                                           self.masks[:,
                                                      mi * 512:mi * 512 + 512],
                                           ALU.mult),
                                       waits=((self.cact, t_exp[(e, bi)]),
                                              (self.qpool, self.t_sdbb)))
                    wt("tensor", self.cact, t_exp[(e, bi)])
                    if t_msk:
                        wt("tensor", self.cdve, t_msk)
                    wt("tensor", self.qsp, t_v)
                    if first_av[e]:
                        self.pe_bank(OKEY[slot[e]])
                    for k, lt in enumerate(bt):
                        op("tensor", self.cpe, lambda t, k=k, lt=lt, par=par,
                           e=e, ab=ab, first=first_av[e],
                           last=(lt == tiles[-1]), lo=lo, sl=slot[e]:
                           t.matmul(
                               osl(sl)[0:65, :],
                               self.vbuf[par][:, (lt - lo) * 130 + 65 * e:
                                              (lt - lo) * 130 + 65 * e + 65],
                               ab[:, k * 512:k * 512 + 512],
                               start=first, stop=last), inc=False)
                        first_av[e] = False
                    if bt[-1] == tiles[-1]:
                        t_avdone[e] = self.pe_fence(osl(slot[e])[0:1, 0:256])
            for e in range(2):
                self.free_bank((e, 0), self.cact, t_exp[(e, nb - 1)])
                self.free_bank((e, 1), self.cact, t_exp[(e, nb - 1)])
            import os as _os
            for e in range(2):
                pending.append((c, e, slot[e], t_avdone[e], (2 * c + e) % 16,
                                g, c == 15))
            if _os.environ.get("MK_NODEFER"):
                for job in pending:
                    emit_chain(job)
                pending = []
            self.t_kbuf_free[par] = self.cpe.n
            self.t_vbuf_free[par] = self.cpe.n
            self.sprinkle(2)
        for job in pending:
            emit_chain(job)
        pending = []
        if self.dump_scr:
            dma("scalar", self.qact, self.dbg_sums[g:g + 1],
                self.stats[0:1, 1280:1344], waits=((self.cdve, self.cdve.n),))

    def ffn_super(self, sup):
        dma, op, wt = self.dma, self.op, self.wt
        cols = slice(1024 * sup, 1024 * sup + 1024)
        self.lockstep = (self.lockmode == "full")
        t_prev = self.t_super_done
        if sup == 0:
            for eng in ("sync", "gpsimd"):
                for sem in (self.cpe, self.cdve, self.cact, self.cpool):
                    self.wt(eng, sem, sem.n)
                self.wt(eng, self.qsp, self.qsp.n)
                self.wt(eng, self.qact, self.qact.n)
                self.wt(eng, self.qpool, self.qpool.n)
        xsr = self.xn_scr.rearrange("c p t -> p c t")
        t_xn = dma("sync", self.qsp, self.xnao[:].rearrange(
            "p (c t) -> p c t", c=16), xsr[:, :, cols],
            waits=((self.cpe, t_prev), (self.qact, self.t_xnscr_all)))
        t_mm = 0
        for h in range(44):
            par = h % 2
            rg, ru = (2 * h) % 3, (2 * h + 1) % 3
            t_wg = dma("scalar", self.qact, self.wgu[rg][:], self.wg_c[h],
                       waits=((self.cpe, self.t_wgu_free[rg]),
                              (self.qact, self.t_cache_done)))
            t_wu = dma("scalar", self.qact, self.wgu[ru][:], self.wu_c[h],
                       waits=((self.cpe, self.t_wgu_free[ru]),))
            t_gmm = t_umm = 0
            for which, wbuf, t_w, pi in (("g", self.wgu[rg], t_wg, par),
                                         ("u", self.wgu[ru], t_wu,
                                          2 + par)):
                self.pe_bank((pi, 0))
                self.pe_bank((pi, 1))
                wt("tensor", self.qact, t_w)
                wt("tensor", self.qsp, t_xn)
                for dc in range(16):
                    for s2 in range(2):
                        t_mm = op("tensor", self.cpe,
                                  lambda t, dc=dc, s2=s2, wbuf=wbuf, pi=pi:
                                  t.matmul(
                                      self.pq[pi][:, s2 * 512:s2 * 512 + 512],
                                      wbuf[:, dc * 128:dc * 128 + 128],
                                      self.xnao[:, dc * 1024 + s2 * 512:
                                                dc * 1024 + s2 * 512 + 512],
                                      start=(dc == 0), stop=(dc == 15)),
                                  inc=(dc == 15 and s2 == 1))
                if which == "g":
                    self.t_wgu_free[rg] = t_mm
                    t_gmm = t_mm
                else:
                    self.t_wgu_free[ru] = t_mm
                    t_umm = t_mm
            t_si = op("scalar", self.cact, lambda s, par=par, h=h:
                      s.activation(self.silu_sb[:], self.pq[par][:, 0:1024],
                                   AF.Silu, bias=self.zc[:, 0:1],
                                   scale=self.sgs[:, h:h + 1]),
                      waits=((self.cpe, t_gmm), (self.qsp, self.t_consts),
                             (self.cdve, getattr(self, "t_gu_prev", 0)),))
            t_up = op("scalar", self.cact, lambda s, par=par, h=h:
                      s.activation(self.up_sb[:], self.pq[2 + par][:, 0:1024],
                                   AF.Copy, bias=0.0,
                                   scale=self.sus[:, h:h + 1]),
                      waits=((self.cpe, t_umm),))
            t_up = self.act_fence(self.up_sb[0:1, 1022:1024])
            t_gu = op("vector", self.cdve, lambda v, h=h: v.tensor_tensor(
                self.gub[:, h * 1024:h * 1024 + 1024], self.silu_sb[:],
                self.up_sb[:], ALU.mult),
                waits=((self.cact, t_up), (self.cpe, t_prev)))
            self.t_gu_prev = t_gu
            self.free_bank((par, 0), self.cact, t_si)
            self.free_bank((par, 1), self.cact, t_si)
            self.free_bank((2 + par, 0), self.cact, t_up)
            self.free_bank((2 + par, 1), self.cact, t_up)
        t_gu_all = self.cdve.n
        if self.lockmode in ("tail", "full"):
            self.lockstep = True
        if self.dump_scr and sup == 0:
            self.dma("scalar", self.qact, self.dbg_gu, self.gub[:],
                     waits=((self.cdve, t_gu_all),))
        for iq in range(4):
            t_ao = dma("sync", self.qsp, self.xnao[:].rearrange(
                "p (c t) -> p c t", c=16)[:, 4 * iq:4 * iq + 4, :],
                self.ao_scr.rearrange("c p t -> p c t")[
                    :, 4 * iq:4 * iq + 4, cols],
                waits=((self.cpe, t_mm), (self.qact, self.t_ao_written)))
        for qtr in range(4):
            self._lock_force("tensor")
            self._lock_force("vector")
            for dcq in range(4):
                self.pe_bank((dcq, 0))
                self.pe_bank((dcq, 1))
            DP = 4
            if qtr == 0:
                self.t_wd_ready = [0] * 4
                self._wd_next = 0
                for _ in range(DP):
                    self.wd_load_flat(dma)
            for h in range(44):
                bpar = h % 4
                wt("tensor", self.qact, self.t_wd_ready[bpar])
                wt("tensor", self.cdve, t_gu_all)
                t_dmm = 0
                for dcq in range(4):
                    for s2 in range(2):
                        t_dmm = op("tensor", self.cpe, lambda t, dcq=dcq,
                                   s2=s2, bpar=bpar, h=h: t.matmul(
                                       self.pq[dcq][:, s2 * 512:s2 * 512 + 512],
                                       self.wdb[bpar][:,
                                                      dcq * 128:dcq * 128 + 128],
                                       self.gub[:, h * 1024 + s2 * 512:
                                                h * 1024 + s2 * 512 + 512],
                                       start=(h == 0), stop=False),
                                   inc=(dcq == 3 and s2 == 1))
                self.t_wdb_free[bpar] = t_dmm
                self.wd_load_flat(dma)
            t_wo_r = [0] * 4
            for dcq in range(4):
                t_wo_r[dcq] = dma(
                    "gpsimd", self.qpool,
                    self.wob[dcq][:].rearrange("p (c j) -> p c j", c=16),
                    self.wo_p[4 * qtr + dcq].rearrange("c p j -> p c j"),
                    waits=((self.cpe, self.t_wob_free[dcq]),))
            for dcq in range(4):
                dc = 4 * qtr + dcq
                par = dcq % 2
                t_wo = t_wo_r[dcq]
                wt("tensor", self.qpool, t_wo)
                wt("tensor", self.qsp, t_ao)
                t_mm2 = 0
                for pc in range(16):
                    for s2 in range(2):
                        t_mm2 = op("tensor", self.cpe,
                                   lambda t, pc=pc, s2=s2, par=par, dcq=dcq:
                                   t.matmul(
                                       self.pq[dcq][:,
                                                    s2 * 512:s2 * 512 + 512],
                                       self.wob[dcq][:,
                                                     pc * 128:pc * 128 + 128],
                                       self.xnao[:, pc * 1024 + s2 * 512:
                                                 pc * 1024 + s2 * 512 + 512],
                                       start=False, stop=(pc == 15)),
                                   inc=(pc == 15 and s2 == 1))
                self.t_wob_free[dcq] = t_mm2
                t_mm2 = self.pe_fence(self.pq[dcq][0:1, 0:256])
                t_xj = dma("sync", self.qsp, self.xjit[par][:],
                           self.xT_own.rearrange("(c p) t -> p c t", p=128)[
                               :, dc, cols],
                           waits=((self.cdve, self.t_xjit_free[par]),))
                if "evac" in getattr(self, "lock_sites", ()):
                    self._lock_force("vector")
                t_ev = op("vector", self.cdve, lambda v, dcq=dcq, dc=dc,
                          par=par: v.scalar_tensor_tensor(
                              self.outsb[par][:], self.pq[dcq][:, 0:1024],
                              self.bos[0:128, dc:dc + 1], self.xjit[par][:],
                              ALU.add, ALU.add),
                          waits=((self.cpe, t_mm2), (self.qsp, t_xj),
                                 (self.qsp, self.t_consts),
                                 (self.qact, self.t_outsb_free[par])))
                self.t_xjit_free[par] = t_ev
                t_evf = self.dve_fence(self.outsb[par][0:1, 0:1])
                self.free_bank((dcq, 0), self.cdve, t_evf)
                self.free_bank((dcq, 1), self.cdve, t_evf)
                t_st = dma("scalar", self.qact, self.out_fm[dc][:, cols],
                           self.outsb[par][:], waits=((self.cdve, t_ev),))
                self.t_outsb_free[par] = t_st
        self.t_super_done = self.cpe.n

    def finalize(self):
        self.wt("sync", self.qact, self.qact.n)
        self.wt("sync", self.qsp, self.qsp.n)
        self.wt("sync", self.qpool, self.qpool.n)



def _prep_shared(Wq, Wk, Wv, Wo, bo, wg, sg, wu, su, wd, sd):
    f32 = np.float32
    pc_layout = lambda w: np.ascontiguousarray(
        w.reshape(16, 128, 16, 128).transpose(0, 3, 2, 1).astype(f32))
    d = {}
    d["wq_p"] = pc_layout(Wq)
    d["wk_p"] = pc_layout(Wk)
    wvT = np.ascontiguousarray(Wv.T)
    d["wv_m"] = np.ascontiguousarray(
        wvT.reshape(16, 128, 2, 1024).transpose(2, 0, 1, 3).astype(f32))
    d["wvg_m"] = np.ascontiguousarray(
        wvT[:, 1920:2048].reshape(16, 128, 128).astype(f32))
    d["wo_p"] = np.ascontiguousarray(
        Wo.reshape(16, 128, 16, 128).transpose(0, 2, 3, 1).astype(f32))
    gl = lambda w: np.ascontiguousarray(
        w.reshape(44, 128, 16, 128).transpose(0, 3, 2, 1).astype(np.int32))
    d["wg_p"] = gl(wg)
    d["wu_p"] = gl(wu)
    wdT = np.ascontiguousarray(wd.T)
    d["wd_p"] = np.ascontiguousarray(
        wdT.reshape(44, 128, 4, 4, 128).transpose(2, 0, 1, 3, 4).astype(
            np.int32))
    d["sg_r"] = np.ascontiguousarray(sg.reshape(44, 128).T.astype(f32))
    d["su_r"] = np.ascontiguousarray(su.reshape(44, 128).T.astype(f32))
    d["bo_r"] = np.ascontiguousarray(bo.reshape(16, 128).T.astype(f32))
    d["sdb"] = np.ascontiguousarray(
        np.broadcast_to(sd[None, :].astype(f32), (128, 2048)))
    m = np.zeros((128, 4, 512), dtype=f32)
    ii = np.arange(128)[:, None]
    jj = np.arange(512)[None, :]
    for dd in range(4):
        m[:, dd, :] = (jj >= ii + dd * 128).astype(f32)
    d["masks_f"] = np.ascontiguousarray(m.reshape(128, 2048))
    r2 = np.zeros((128, 128), dtype=f32)
    for i2 in range(128):
        base = (i2 // 64) * 64
        k = i2 - base
        if k < 32:
            r2[base + k + 32, i2] = -1.0
        else:
            r2[base + k - 32, i2] = 1.0
    d["r2t_f"] = r2
    inv = 1.0 / (10000.0 ** (np.arange(0, 64, 2, dtype=np.float64) / 64.0))
    rowf = inv[np.arange(128) % 32]
    return d, rowf


def _rope_tables(rowf, half):
    f32 = np.float32
    pos_own = np.arange(2048, dtype=np.float64) + 2048.0 * half
    pos_pref = np.arange(2048, dtype=np.float64)
    ang_own = rowf[:, None] * pos_own[None, :]
    ang_pref = rowf[:, None] * pos_pref[None, :]
    cos_q = (np.cos(ang_own) * 0.125).astype(f32)
    sin_q = (np.sin(ang_own) * 0.125).astype(f32)
    cos_k = np.concatenate([np.cos(ang_pref), np.cos(ang_own)],
                           axis=1).astype(f32)
    sin_k = np.concatenate([np.sin(ang_pref), np.sin(ang_own)],
                           axis=1).astype(f32)
    return cos_q, sin_q, cos_k, sin_k


_PROG_CACHE = {}


def _get_prog():
    if "nc" not in _PROG_CACHE:
        _PROG_CACHE["nc"] = MK().build()
    return _PROG_CACHE["nc"]


def kernel(L0_attn_hidden_states, L0_attn_Wq, L0_attn_Wk, L0_attn_Wv,
           L0_attn_Wo, L0_attn_bo, L0_ffn_wq_gate, L0_ffn_s_gate,
           L0_ffn_wq_up, L0_ffn_s_up, L0_ffn_wq_down, L0_ffn_s_down):
    x = np.asarray(L0_attn_hidden_states, dtype=np.float32)
    shared, rowf = _prep_shared(
        np.asarray(L0_attn_Wq, np.float32), np.asarray(L0_attn_Wk, np.float32),
        np.asarray(L0_attn_Wv, np.float32), np.asarray(L0_attn_Wo, np.float32),
        np.asarray(L0_attn_bo, np.float32),
        np.asarray(L0_ffn_wq_gate, np.int32),
        np.asarray(L0_ffn_s_gate, np.float32),
        np.asarray(L0_ffn_wq_up, np.int32),
        np.asarray(L0_ffn_s_up, np.float32),
        np.asarray(L0_ffn_wq_down, np.int32),
        np.asarray(L0_ffn_s_down, np.float32))
    in_maps = []
    for core in range(8):
        b2, half = core // 2, core % 2
        xT = np.ascontiguousarray(x[b2].T)
        cos_q, sin_q, cos_k, sin_k = _rope_tables(rowf, half)
        m = dict(shared)
        m["xT_own"] = np.ascontiguousarray(
            xT[:, 2048 * half:2048 * half + 2048])
        m["xT_pref"] = np.ascontiguousarray(xT[:, 0:2048])
        m["cos_q"], m["sin_q"] = cos_q, sin_q
        m["cos_k"], m["sin_k"] = cos_k, sin_k
        m["pm"] = np.full((128, 1), 1.0 if half else 0.0, dtype=np.float32)
        in_maps.append(m)
    nc = _get_prog()
    res = run_bass_kernel_spmd(nc, in_maps, core_ids=list(range(8)))
    out = np.empty((4, 4096, 2048), dtype=np.float32)
    for core in range(8):
        b2, half = core // 2, core % 2
        fm = res.results[core]["out_fm"].reshape(2048, 2048)
        out[b2, 2048 * half:2048 * half + 2048, :] = fm.T
    return out


# version 17
# speedup vs baseline: 1.1385x; 1.0756x over previous
import contextlib

import numpy as np
import concourse.bass as bass
import concourse.mybir as mybir
from concourse.bass_utils import run_bass_kernel_spmd
from concourse.alu_op_type import AluOpType as ALU

F32 = mybir.dt.float32
BF16 = mybir.dt.bfloat16
I32 = mybir.dt.int32
AF = mybir.ActivationFunctionType
AX = mybir.AxisListType

D = 2048
TOWN = 2048
LSUB = 256
EPS = 1e-5

ENGINES = ("tensor", "vector", "scalar", "gpsimd", "sync")


class SemT:
    def __init__(self, handle):
        self.h = handle
        self.n = 0

    def bump(self, k=1):
        self.n += k
        return self.n


class MK:

    def __init__(self, dump_scr=False):
        self.dump_scr = dump_scr
        self.nc = bass.Bass("TRN2", target_bir_lowering=False)
        nc = self.nc
        ap = lambda n, s, d=F32: nc.dram_tensor(n, s, d,
                                                kind="ExternalInput").ap()
        self.xT_own = ap("xT_own", [D, TOWN])
        self.xT_pref = ap("xT_pref", [D, TOWN])
        self.wq_p = ap("wq_p", [16, 128, 16, 128])
        self.wk_p = ap("wk_p", [16, 128, 16, 128])
        self.wv_m = ap("wv_m", [2, 16, 128, 1024])
        self.wvg_m = ap("wvg_m", [16, 128, 128])
        self.wo_p = ap("wo_p", [16, 16, 128, 128])
        self.wg_p = ap("wg_p", [44, 128, 16, 128], I32)
        self.wu_p = ap("wu_p", [44, 128, 16, 128], I32)
        self.wd_p = ap("wd_p", [4, 44, 128, 4, 128], I32)
        self.sg_r = ap("sg_r", [128, 44])
        self.su_r = ap("su_r", [128, 44])
        self.bo_r = ap("bo_r", [128, 16])
        self.sdb = ap("sdb", [128, 2048])
        self.cos_q = ap("cos_q", [128, 2048])
        self.sin_q = ap("sin_q", [128, 2048])
        self.cos_k = ap("cos_k", [128, 4096])
        self.sin_k = ap("sin_k", [128, 4096])
        self.r2t_f = ap("r2t_f", [128, 128])
        self.masks_f = ap("masks_f", [128, 4 * 512])
        self.pm = ap("pm", [128, 1])
        scr_kind = "ExternalOutput" if dump_scr else "Internal"
        scr = lambda n, s, d: nc.dram_tensor(n, s, d, kind=scr_kind).ap()
        self.xn_scr = scr("xn_scr", [16, 128, 2048], BF16)
        self.ao_scr = scr("ao_scr", [16, 128, 2048], BF16)
        self.k_scr = scr("k_scr", [16, 128, 4096], BF16)
        self.v_scr = scr("v_scr", [32, 128, 2080], BF16)
        self.wg_c = scr("wg_c", [44, 128, 2048], BF16)
        self.wu_c = scr("wu_c", [44, 128, 2048], BF16)
        self.wd_c = scr("wd_c", [4, 44, 128, 512], BF16)
        self.out_fm = nc.dram_tensor("out_fm", [16, 128, 2048], F32,
                                     kind="ExternalOutput").ap()
        if dump_scr:
            self.dbg_negc = nc.dram_tensor("dbg_negc", [4, 128], F32,
                                           kind="ExternalOutput").ap()
            self.dbg_kq = nc.dram_tensor("dbg_kq", [4, 8], F32,
                                         kind="ExternalOutput").ap()
            self.dbg_rec = nc.dram_tensor("dbg_rec", [4, 512], F32,
                                          kind="ExternalOutput").ap()
            self.dbg_sums = nc.dram_tensor("dbg_sums", [4, 64], F32,
                                           kind="ExternalOutput").ap()
            self.dbg_qg = nc.dram_tensor("dbg_qg", [2, 128, 8192], BF16,
                                         kind="ExternalOutput").ap()
            self.dbg_gu = nc.dram_tensor("dbg_gu", [128, 45056], BF16,
                                         kind="ExternalOutput").ap()
            self.dbg_dn = nc.dram_tensor("dbg_dn", [4, 128, 1024], F32,
                                         kind="ExternalOutput").ap()
            self.dbg_wd = nc.dram_tensor("dbg_wd", [4, 44, 128, 512], BF16,
                                         kind="ExternalOutput").ap()

        self.q = {e: [] for e in ENGINES}
        self.last_wait = {}
        self.t_xbuf_free = 0
        self.t_bsq_free = 0
        self.t_xnp_free = 0
        self.t_qg_free = 0
        self.t_kst_free = 0
        self.t_k2_free = 0
        self.t_ropepr = 0
        self.t_cs_free = [0, 0]
        self.t_aost_free = 0
        self.t_ao_written = 0
        self.t_xnscr_all = 0
        self.t_super_done = 0
        self.k_written = [0] * 32
        self.v_written = [0] * 32

    def em(self, eng, fn):
        self.q[eng].append(fn)

    def wt(self, eng, sem, tgt):
        if tgt <= 0:
            return
        key = (eng, id(sem))
        if self.last_wait.get(key, -1) >= tgt:
            return
        self.last_wait[key] = tgt
        self.q[eng].append(lambda E, s=sem.h, t=tgt: E.wait_ge(s, t))

    def flush(self):
        for e in ENGINES:
            fns = self.q[e]
            if not fns:
                continue
            self.q[e] = []

            def section(E, fns=fns):
                for f in fns:
                    f(E)

            getattr(self.block, e)(section)

    def dma(self, qeng, sem, dst, src, waits=()):
        if getattr(self, "lockstep", False):
            self._lock(qeng)
        for (s, t) in waits:
            self.wt(qeng, s, t)
        tgt = sem.bump(16)
        self.em(qeng, lambda Q, d=dst, s=src, h=sem.h:
                Q.dma_start(d, s).then_inc(h, 16))
        return tgt

    def pe_fence(self, elem_ap):
        t = elem_ap.tensor if hasattr(elem_ap, "tensor") else None
        return self.op("tensor", self.cpe, lambda t2: t2.matmul(
            elem_ap, self.zb[:], self.zrow[:], start=False, stop=True))

    def sprinkle(self, k=1):
        if self.cache_jobs:
            self.emit_cache_jobs(k)

    def wd_load_flat(self, dma):
        j = self._wd_next
        if j >= 176:
            return
        self._wd_next += 1
        qtr, h = j // 44, j % 44
        bpar = j % 4
        t_wd = dma("scalar", self.qact, self.wdb[bpar][:],
                   self.wd_c[qtr, h],
                   waits=((self.cpe, self.t_wdb_free[bpar]),
                          (self.qact, self.t_cache_done)))
        self.t_wd_ready[bpar] = t_wd

    def act_fence(self, src_ap):
        return self.op("scalar", self.cact, lambda sc: sc.activation(
            self.fence_sb[0:1, 2:4], src_ap, AF.Copy, bias=0.0, scale=1.0))

    def dve_fence(self, src_ap=None):
        if src_ap is None:
            src_ap = self.fence_sb[0:1, 1:2]
        return self.op("vector", self.cdve, lambda v: v.tensor_copy(
            self.fence_sb[0:1, 0:1], src_ap))

    def _lock_force(self, eng):
        for sem2 in (self.cpe, self.cdve, self.cact, self.qsp):
            self.wt(eng, sem2, sem2.n)

    def _lock(self, eng):
        if self.lock_engs and eng not in self.lock_engs:
            return
        for sem2 in (self.cpe, self.cdve, self.cact, self.cpool,
                     self.qsp, self.qact, self.qpool):
            self.wt(eng, sem2, sem2.n)

    def op(self, eng, sem, fn, waits=(), inc=True):
        if getattr(self, "lockstep", False):
            self._lock(eng)
        for (s, t) in waits:
            self.wt(eng, s, t)
        if inc:
            tgt = sem.bump()
            self.em(eng, lambda E, f=fn, h=sem.h: f(E).then_inc(h, 1))
            return tgt
        self.em(eng, fn)
        return sem.n

    def pe_bank(self, key):
        s, t = self.bank_free[key]
        self.wt("tensor", s, t)

    def free_bank(self, key, sem, tgt):
        self.bank_free[key] = (sem, tgt)

    def psl(self, key):
        i, hlf = key
        return self.pq[i][:, hlf * 512:(hlf + 1) * 512]

    def build(self):
        nc = self.nc
        st = contextlib.ExitStack()
        E = st.enter_context
        with st:
            self.masks = E(nc.sbuf_tensor("masks", [128, 4 * 512], BF16))
            self.r2t = E(nc.sbuf_tensor("r2t", [128, 128], BF16))
            self.sgs = E(nc.sbuf_tensor("sgs", [128, 44], F32))
            self.sus = E(nc.sbuf_tensor("sus", [128, 44], F32))
            self.bos = E(nc.sbuf_tensor("bos", [128, 16], F32))
            self.sdbb = E(nc.sbuf_tensor("sdbb", [128, 2048], BF16))
            self.pmc = E(nc.sbuf_tensor("pmc", [128, 1], F32))
            self.ones_bf = E(nc.sbuf_tensor("ones_bf", [128, 1], BF16))
            self.onesrow = E(nc.sbuf_tensor("onesrow", [1, 128], F32))
            self.zc = E(nc.sbuf_tensor("zc", [128, 1], F32))
            self.fence_sb = E(nc.sbuf_tensor("fence_sb", [1, 8], F32))
            self.dqb = [E(nc.sbuf_tensor("dqb0", [128, 2048], BF16))]
            self.c128b = E(nc.sbuf_tensor("c128b", [128, 2048], BF16))
            self.zb = E(nc.sbuf_tensor("zb", [1, 1], BF16))
            self.zrow = E(nc.sbuf_tensor("zrow", [1, 256], BF16))
            self.epsc = E(nc.sbuf_tensor("epsc", [1, 2], F32))
            self.stats = E(nc.sbuf_tensor("stats", [1, 1536], F32))
            self.kq = E(nc.sbuf_tensor("kq", [1, 8], F32))
            self.negc = E(nc.sbuf_tensor("negc", [128, 1], F32))
            self.rstdb = E(nc.sbuf_tensor("rstdb", [128, 512], F32))
            self.pq = [E(nc.psum_tensor(f"pq{i}", [128, 1024], F32))
                       for i in range(4)]
            self.qsp = SemT(E(nc.semaphore("qsp")))
            self.qact = SemT(E(nc.semaphore("qact")))
            self.qpool = SemT(E(nc.semaphore("qpool")))
            self.cpe = SemT(E(nc.semaphore("cpe")))
            self.cdve = SemT(E(nc.semaphore("cdve")))
            self.cact = SemT(E(nc.semaphore("cact")))
            self.cpool = SemT(E(nc.semaphore("cpool")))
            self.block = E(nc.Block())
            self.bank_free = {(i, h): (self.cdve, 0)
                              for i in range(4) for h in range(2)}

            self.setup()
            self.build_ffn_cache()
            stA = contextlib.ExitStack()
            EA = stA.enter_context
            with stA:
                self.xbuf = EA(nc.sbuf_tensor("xbuf", [128, 16 * LSUB], F32))
                self.recrow = EA(nc.sbuf_tensor("recrow", [1, 512], F32))
                self.recb = EA(nc.sbuf_tensor("recb", [128, 512], F32))
                self.q0b = EA(nc.sbuf_tensor("q0b", [128, 512], BF16))
                self.t1b = EA(nc.sbuf_tensor("t1b", [128, 512], F32))
                self.t2b = EA(nc.sbuf_tensor("t2b", [128, 512], F32))
                self.kst = EA(nc.sbuf_tensor("kst", [128, 512], BF16))
                self.k2b = self.q0b
                self.abuf = [EA(nc.sbuf_tensor(f"abuf{i}", [128, 1024], BF16))
                             for i in range(2)]
                self.csb = [EA(nc.sbuf_tensor(f"csb{i}", [128, 1024], F32))
                            for i in range(2)]


                self.bsq = EA(nc.sbuf_tensor("bsq", [128, 16 * 256], BF16))
                self.xnp = EA(nc.sbuf_tensor("xnp", [128, 16 * 1024], BF16))
                self.qg = [EA(nc.sbuf_tensor(f"qg{i}", [128, 16 * 512], BF16))
                           for i in range(2)]
                self.kbuf = [EA(nc.sbuf_tensor(f"kbuf{i}", [128, 4096], BF16))
                             for i in range(2)]
                self.vbuf = [EA(nc.sbuf_tensor(f"vbuf{i}", [128, 32 * 130],
                                               BF16)) for i in range(2)]
                self.vstage = [EA(nc.sbuf_tensor("vstage0", [128, 1040],
                                                 BF16))] * 2
                self.t_vst_free = [0, 0]
                self.aost = EA(nc.sbuf_tensor("aost", [64, 16 * 512], BF16))
                self.wkq = [EA(nc.sbuf_tensor(f"wkq{i}", [128, 2048], BF16))
                            for i in range(2)]
                self.t_wkq_free = [0, 0]
                self.wvb = [EA(nc.sbuf_tensor(f"wvb{i}", [128, 1024], BF16))
                            for i in range(2)]
                self.t_wvb_free = [0, 0]
                self.t_kbuf_free = [0, 0]
                self.t_vbuf_free = [0, 0]
                stP = contextlib.ExitStack()
                EP = stP.enter_context
                with stP:
                    self.wvgb = EP(nc.sbuf_tensor("wvgb", [128, 2048], BF16))
                    self.wk15 = EP(nc.sbuf_tensor("wk15", [128, 2048], BF16))
                    self.t_wvg = self.dma(
                        "gpsimd", self.qpool,
                        self.wvgb[:].rearrange("p (c j) -> p c j", c=16),
                        self.wvg_m.rearrange("c p j -> p c j"))
                    self.t_wk15 = self.dma(
                        "gpsimd", self.qpool,
                        self.wk15[:].rearrange("p (c j) -> p c j", c=16),
                        self.wk_p[15])
                    for g in range(4):
                        with nc.named_scope(f"prefix{g}"):
                            self.prefix_group(g)
                            self.emit_cache_jobs(6)
                            self.flush()
                self.abuf += [EA(nc.sbuf_tensor(f"abuf{i}", [128, 1024],
                                                BF16)) for i in (2, 3)]
                self.dqb.append(EA(nc.sbuf_tensor("dqb1", [128, 2048],
                                                  BF16)))
                for pair in range(2):
                    with nc.named_scope(f"pair{pair}"):
                        self.own_pair(pair)
                        self.flush()
                with nc.named_scope("cachetail"):
                    self.emit_cache_jobs(10000)
                    self.flush()
            stB = contextlib.ExitStack()
            EB = stB.enter_context
            with stB:
                self.xnao = EB(nc.sbuf_tensor("xnao", [128, 16 * 1024], BF16))
                self.gub = EB(nc.sbuf_tensor("gub", [128, 44 * 1024], BF16))
                self.wgu = [EB(nc.sbuf_tensor(f"wgu{i}", [128, 2048], BF16))
                            for i in range(3)]
                self.t_wgu_free = [0, 0, 0]
                self.wdb = [EB(nc.sbuf_tensor(f"wdb{i}", [128, 512], BF16))
                            for i in range(4)]
                self.t_wdb_free = [0] * 4
                self.wob = [EB(nc.sbuf_tensor(f"wob{i}", [128, 2048], BF16))
                            for i in range(4)]
                self.t_wob_free = [0] * 4
                self.silu_sb = EB(nc.sbuf_tensor("silu_sb", [128, 1024], F32))
                if self.dump_scr:
                    self.dbg_dn_sb = EB(nc.sbuf_tensor("dbg_dn_sb",
                                                       [128, 1024], F32))
                self.up_sb = EB(nc.sbuf_tensor("up_sb", [128, 1024], F32))
                self.outsb = [EB(nc.sbuf_tensor(f"outsb{i}", [128, 1024], F32))
                              for i in range(2)]
                self.xjit = [EB(nc.sbuf_tensor(f"xjit{i}", [128, 1024], F32))
                             for i in range(2)]
                self.t_outsb_free = [0, 0]
                self.t_xjit_free = [0, 0]
                import os
                self.lockmode = os.environ.get("MK_LOCKMODE", "tail")
                self.lock_engs = tuple(
                    x for x in os.environ.get("MK_LOCK_ENGS", "").split(",")
                    if x)
                self.lock_sites = tuple(
                    x for x in os.environ.get("MK_LOCK_SITES", "").split(",")
                    if x)
                for sup in range(2):
                    with nc.named_scope(f"ffn{sup}"):
                        self.ffn_super(sup)
                        self.flush()
                self.lockstep = False
            self.finalize()
            self.flush()
        return nc

    def setup(self):
        self.dma("gpsimd", self.qpool, self.masks[:], self.masks_f)
        self.dma("gpsimd", self.qpool, self.r2t[:], self.r2t_f)
        self.t_sdbb = self.dma("gpsimd", self.qpool, self.sdbb[:], self.sdb)
        self.dma("sync", self.qsp, self.sgs[:], self.sg_r)
        self.dma("sync", self.qsp, self.sus[:], self.su_r)
        self.dma("sync", self.qsp, self.bos[:], self.bo_r)
        self.t_consts = self.dma("sync", self.qsp, self.pmc[:], self.pm)
        self.op("vector", self.cdve, lambda v: v.memset(self.ones_bf[:], 1.0))
        self.op("vector", self.cdve, lambda v: v.memset(self.zc[:], 0.0))
        self.op("vector", self.cdve, lambda v: v.memset(self.epsc[0:1, 0:1], EPS))
        self.op("vector", self.cdve, lambda v: v.memset(self.epsc[0:1, 1:2], 1e-18))
        self.op("vector", self.cdve, lambda v: v.memset(self.onesrow[:], 1.0))
        self.op("vector", self.cdve, lambda v: v.memset(self.zb[:], 0.0))
        self.op("vector", self.cdve, lambda v: v.memset(self.zrow[:], 0.0))
        self.t_c128 = self.op("gpsimd", self.cpool,
                              lambda p: p.memset(self.c128b[:], 128.0))
        self.op("vector", self.cdve, lambda v: v.memset(self.fence_sb[:], 0.0))
        self.t_setup_dve = self.op("vector", self.cdve,
                                   lambda v: v.memset(self.kq[:], 0.0))
        self.flush()

    def build_ffn_cache(self):
        self.t_dq_free = [0, 0]
        self.dq_i = 0
        self.cache_jobs = []
        for h in range(44):
            self.cache_jobs.append(("g", h))
            self.cache_jobs.append(("u", h))
        for qtr in range(4):
            for h in range(44):
                self.cache_jobs.append(("d", (qtr, h)))
        self.t_cache_done = 0

    def emit_cache_jobs(self, k):
        dma, op = self.dma, self.op
        for _ in range(min(k, len(self.cache_jobs))):
            kind, arg = self.cache_jobs.pop(0)
            par = self.dq_i % len(self.dqb)
            self.dq_i += 1
            if kind in ("g", "u"):
                h = arg
                src = (self.wg_p if kind == "g" else self.wu_p)[h]
                dst = (self.wg_c if kind == "g" else self.wu_c)[h]
                cols = 2048
                sd = None
            else:
                qtr, h = arg
                src = self.wd_p[qtr, h]
                dst = self.wd_c[qtr, h]
                cols = 512
                sd = slice(qtr * 512, qtr * 512 + 512)
            t_in = dma("gpsimd", self.qpool, self.dqb[par][:, 0:cols], src,
                       waits=((self.qact, self.t_dq_free[par]),))
            t_d = op("gpsimd", self.cpool, lambda v, par=par, cols=cols:
                     v.tensor_tensor(self.dqb[par][:, 0:cols],
                                     self.dqb[par][:, 0:cols],
                                     self.c128b[:, 0:cols], ALU.subtract),
                     waits=((self.qpool, t_in), (self.cpool, self.t_c128)))
            if sd is not None:
                t_d = op("gpsimd", self.cpool, lambda v, par=par, ss=sd,
                         cols=cols: v.tensor_tensor(
                             self.dqb[par][:, 0:cols],
                             self.dqb[par][:, 0:cols],
                             self.sdbb[:, ss], ALU.mult),
                         waits=((self.qpool, self.t_sdbb),))
            t_o = dma("scalar", self.qact, dst, self.dqb[par][:, 0:cols],
                      waits=((self.cpool, t_d),))
            self.t_dq_free[par] = t_o
        self.t_cache_done = self.qact.n

    def ln_sub(self, xT, lt_base, sub, dst_off):
        op, wt, dma = self.op, self.wt, self.dma
        c0 = lt_base * 128 + sub * LSUB
        xr = xT.rearrange("(c p) t -> p c t", p=128)
        tx = dma("sync", self.qsp, self.xbuf[:].rearrange(
            "p (c t) -> p c t", c=16), xr[:, :, c0:c0 + LSUB],
            waits=((self.cdve, self.t_xbuf_free),))
        txb = dma("gpsimd", self.qpool, self.bsq[:, 0:16 * LSUB].rearrange(
            "p (c t) -> p c t", c=16), xr[:, :, c0:c0 + LSUB],
            waits=((self.cpe, self.t_bsq_free),))
        self.pe_bank((3, 0))
        wt("tensor", self.qpool, txb)
        wt("tensor", self.cdve, self.t_setup_dve)
        t_sum = 0
        for dc in range(16):
            t_sum = op("tensor", self.cpe, lambda t, dc=dc: t.matmul(
                self.pq[3][0:1, 0:LSUB], self.ones_bf[:],
                self.bsq[:, dc * LSUB:(dc + 1) * LSUB],
                start=(dc == 0), stop=(dc == 15)), inc=(dc == 15))
        t_sq = op("vector", self.cdve, lambda v: v.tensor_tensor(
            self.bsq[:, 0:16 * LSUB], self.bsq[:, 0:16 * LSUB],
            self.bsq[:, 0:16 * LSUB], ALU.mult), waits=((self.cpe, t_sum),))
        self.pe_bank((3, 1))
        wt("tensor", self.cdve, t_sq)
        t_ssq = 0
        for dc in range(16):
            t_ssq = op("tensor", self.cpe, lambda t, dc=dc: t.matmul(
                self.pq[3][0:1, 512:512 + LSUB], self.ones_bf[:],
                self.bsq[:, dc * LSUB:(dc + 1) * LSUB],
                start=(dc == 0), stop=(dc == 15)), inc=(dc == 15))
        self.t_bsq_free = t_ssq
        st = self.stats
        MEAN, VAR, M2, RSTD, MRS = (0, 256, 512, 768, 1024)
        op("vector", self.cdve, lambda v: v.tensor_scalar(
            st[0:1, MEAN:MEAN + LSUB], self.pq[3][0:1, 0:LSUB], 1.0 / D, None,
            ALU.mult), waits=((self.cpe, t_sum),))
        t_rd1 = op("vector", self.cdve, lambda v: v.tensor_scalar(
            st[0:1, VAR:VAR + LSUB], self.pq[3][0:1, 512:512 + LSUB],
            1.0 / D, None, ALU.mult), waits=((self.cpe, t_ssq),))
        self.free_bank((3, 0), self.cdve, t_rd1)
        self.free_bank((3, 1), self.cdve, t_rd1)
        op("vector", self.cdve, lambda v: v.tensor_tensor(
            st[0:1, M2:M2 + LSUB], st[0:1, MEAN:MEAN + LSUB],
            st[0:1, MEAN:MEAN + LSUB], ALU.mult))
        t_var = op("vector", self.cdve, lambda v: v.tensor_tensor(
            st[0:1, VAR:VAR + LSUB], st[0:1, VAR:VAR + LSUB],
            st[0:1, M2:M2 + LSUB], ALU.subtract))
        op("scalar", self.cact, lambda s: s.activation(
            st[0:1, RSTD:RSTD + LSUB], st[0:1, VAR:VAR + LSUB], AF.Ln,
            bias=self.epsc[0:1, 0:1], scale=1.0),
            waits=((self.cdve, t_var),))
        t_rstd = op("scalar", self.cact, lambda s: s.activation(
            st[0:1, RSTD:RSTD + LSUB], st[0:1, RSTD:RSTD + LSUB], AF.Exp,
            bias=self.zc[0:1, 0:1], scale=-0.5))
        t_mrs = op("vector", self.cdve, lambda v: v.tensor_tensor(
            st[0:1, MRS:MRS + LSUB], st[0:1, MEAN:MEAN + LSUB],
            st[0:1, RSTD:RSTD + LSUB], ALU.mult),
            waits=((self.cact, t_rstd),))
        self.pe_bank((2, 0))
        self.pe_bank((2, 1))
        wt("tensor", self.cdve, t_mrs)
        op("tensor", self.cpe, lambda t: t.matmul(
            self.pq[2][:, 0:LSUB], self.onesrow[:],
            st[0:1, RSTD:RSTD + LSUB], start=True, stop=True), inc=False)
        t_bc = op("tensor", self.cpe, lambda t: t.matmul(
            self.pq[2][:, 512:512 + LSUB], self.onesrow[:],
            st[0:1, MRS:MRS + LSUB], start=True, stop=True))
        op("vector", self.cdve, lambda v: v.tensor_copy(
            self.rstdb[:, 0:LSUB], self.pq[2][:, 0:LSUB]),
            waits=((self.cpe, t_bc),))
        t_ev = op("vector", self.cdve, lambda v: v.tensor_copy(
            self.rstdb[:, 256:256 + LSUB], self.pq[2][:, 512:512 + LSUB]))
        self.free_bank((2, 0), self.cdve, t_ev)
        self.free_bank((2, 1), self.cdve, t_ev)
        xap = self.xbuf[:].rearrange("p (c t) -> p c t", c=16)
        rrep = bass.AP(self.rstdb, 0, [[512, 128], [0, 16], [1, LSUB]])
        mrep = bass.AP(self.rstdb, 256, [[512, 128], [0, 16], [1, LSUB]])
        wt("vector", self.qsp, tx)
        op("vector", self.cdve, lambda v: v.tensor_tensor(
            xap, xap, rrep, ALU.mult))
        dap = self.xnp[:].rearrange("p (c t) -> p c t", c=16)[
            :, :, dst_off + sub * LSUB: dst_off + sub * LSUB + LSUB]
        wt("vector", self.cpe, self.t_xnp_free)
        t_xn = op("vector", self.cdve, lambda v: v.tensor_tensor(
            dap, xap, mrep, ALU.subtract))
        self.t_xbuf_free = t_xn
        return t_xn

    def rope(self, pkey, dst_ap, cs_i, cs_tgt, t_proj, dst_waits=()):
        op, wt = self.op, self.wt
        ps = self.psl(pkey)
        t_q0 = op("vector", self.cdve, lambda v: v.tensor_copy(
            self.q0b[:], ps),
            waits=((self.cpe, t_proj), (self.cpe, self.t_ropepr)))
        self.pe_bank((2, 0))
        wt("tensor", self.cdve, t_q0)
        t_pr = op("tensor", self.cpe, lambda t: t.matmul(
            self.pq[2][:, 0:512], self.r2t[:], self.q0b[:],
            start=True, stop=True))
        self.t_ropepr = t_pr
        cs = self.csb[cs_i]
        wt("vector", self.qsp, cs_tgt)
        op("vector", self.cdve, lambda v: v.tensor_tensor(
            self.t1b[:], self.q0b[:], cs[:, 0:512], ALU.mult))
        t_t2 = op("vector", self.cdve, lambda v: v.tensor_tensor(
            self.t2b[:], self.pq[2][:, 0:512], cs[:, 512:1024], ALU.mult),
            waits=((self.cpe, t_pr),))
        self.free_bank((2, 0), self.cdve, t_t2)
        for (s, t) in dst_waits:
            wt("vector", s, t)
        t_out = op("vector", self.cdve, lambda v: v.tensor_tensor(
            dst_ap, self.t1b[:], self.t2b[:], ALU.add))
        return t_out

    def knorm(self):
        op, wt = self.op, self.wt
        t_k2 = op("vector", self.cdve, lambda v: v.tensor_tensor(
            self.k2b[:], self.kst[:], self.kst[:], ALU.mult),
            waits=((self.cpe, self.t_k2_free),))
        self.pe_bank((3, 0))
        wt("tensor", self.cdve, t_k2)
        t_mm = op("tensor", self.cpe, lambda t: t.matmul(
            self.pq[3][0:1, 0:512], self.ones_bf[:], self.k2b[:],
            start=True, stop=True))
        self.t_k2_free = t_mm
        t_red = op("vector", self.cdve, lambda v: v.reduce_max(
            self.kq[0:1, 2:3], self.pq[3][0:1, 0:512], axis=AX.X),
            waits=((self.cpe, t_mm),))
        self.wt("vector", self.cdve, t_red)
        self.free_bank((3, 0), self.cdve, t_red)
        op("vector", self.cdve, lambda v: v.tensor_tensor(
            self.kq[0:1, 0:1], self.kq[0:1, 0:1], self.kq[0:1, 2:3], ALU.max))

    def prefix_group(self, g):
        dma, op, wt = self.dma, self.op, self.wt
        for sub in range(2):
            t_xn = self.ln_sub(self.xT_pref, 4 * g, sub, 0)
        dma("sync", self.qsp, self.csb[0][:, 0:512],
            self.cos_k[:, 512 * g:512 * g + 512],
            waits=((self.cdve, self.t_cs_free[0]),))
        t_cs = dma("sync", self.qsp, self.csb[0][:, 512:1024],
                   self.sin_k[:, 512 * g:512 * g + 512])
        pcs = [15] if g < 3 else list(range(16))
        for i, pc in enumerate(pcs):
            par = i % 2
            if pc == 15:
                wb, t_w = self.wk15, self.t_wk15
            else:
                wb = self.wkq[par]
                t_w = dma("gpsimd", self.qpool, wb[:].rearrange("p (c j) -> p c j", c=16),
                          self.wk_p[pc],
                          waits=((self.cpe, self.t_wkq_free[par]),))
            key = (par, 0)
            self.pe_bank(key)
            wt("tensor", self.qpool, t_w)
            wt("tensor", self.cdve, t_xn)
            t_mm = 0
            for dc in range(16):
                t_mm = op("tensor", self.cpe, lambda t, dc=dc, wb=wb, key=key:
                          t.matmul(self.psl(key),
                                   wb[:, dc * 128:dc * 128 + 128],
                                   self.xnp[:, dc * 1024:dc * 1024 + 512],
                                   start=(dc == 0), stop=(dc == 15)),
                          inc=(dc == 15))
            if pc != 15:
                self.t_wkq_free[par] = t_mm
            t_rope = self.rope(key, self.kst[:], 0, t_cs, t_mm,
                               dst_waits=((self.qact, self.t_kst_free),))
            self.free_bank(key, self.cdve, t_rope)
            t_st = dma("scalar", self.qact,
                       self.k_scr[pc][:, 512 * g:512 * g + 512], self.kst[:],
                       waits=((self.cdve, t_rope),))
            self.t_kst_free = t_st
            for lt in range(4 * g, 4 * g + 4):
                self.k_written[lt] = max(self.k_written[lt], t_st)
            self.knorm()
        self.t_cs_free[0] = self.cdve.n
        if g < 3:
            for tt in range(4):
                key = (tt % 2, 1)
                self.pe_bank(key)
                wt("tensor", self.qpool, self.t_wvg)
                t_mm = 0
                for dc in range(16):
                    t_mm = op("tensor", self.cpe,
                              lambda t, dc=dc, tt=tt, key=key: t.matmul(
                                  self.psl(key)[:, 0:128],
                                  self.xnp[:, dc * 1024 + tt * 128:
                                           dc * 1024 + tt * 128 + 128],
                                  self.wvgb[:, dc * 128:dc * 128 + 128],
                                  start=(dc == 0), stop=(dc == 15)),
                              inc=(dc == 15))
                self.v_evac_global(4 * g + tt, key, t_mm)
        else:
            self.v_own_quad(list(range(12, 16)), 0, prefix=True)
        self.t_xnp_free = self.cpe.n

    def v_evac_global(self, lt, key, t_mm):
        dma, op, wt = self.dma, self.op, self.wt
        par = lt % 2
        vs = self.vstage[par]
        wt("vector", self.qact, self.t_vst_free[par])
        src = self.psl(key)[:, 0:128].rearrange("p (h d) -> p h d", d=64)
        vr = vs[:, 0:1040].rearrange("p (h d) -> p h d", d=65)
        t_ev = op("vector", self.cdve, lambda v: v.tensor_scalar(
            vr[:, 14:16, 0:64], src, self.pmc[:, 0:1], None, ALU.mult),
            waits=((self.cpe, t_mm), (self.qsp, self.t_consts)))
        self.free_bank(key, self.cdve, t_ev)
        pmrep = bass.AP(self.pmc, 0, [[1, 128], [0, 2], [0, 1]])
        t_ev = op("vector", self.cdve, lambda v: v.tensor_scalar(
            vr[:, 14:16, 64:65], pmrep, 1.0, None, ALU.mult))
        t_st = dma("scalar", self.qact, self.v_scr[lt][:, 1950:2080],
                   vs[:, 910:1040], waits=((self.cdve, t_ev),))
        self.t_vst_free = [t_st, t_st]
        self.v_written[lt] = max(self.v_written[lt], t_st)

    def v_own_quad(self, lts, xoff, prefix=False):
        dma, op, wt = self.dma, self.op, self.wt
        for half in range(2):
            t_mm = 0
            for dc in range(16):
                par = dc % 2
                t_w = dma("gpsimd", self.qpool, self.wvb[par][:],
                          self.wv_m[half, dc],
                          waits=((self.cpe, self.t_wvb_free[par]),))
                wt("tensor", self.qpool, t_w)
                for i in range(4):
                    if dc == 0:
                        self.pe_bank((i, 0))
                        self.pe_bank((i, 1))
                    for s2 in range(2):
                        t_mm = op("tensor", self.cpe,
                                  lambda t, dc=dc, i=i, s2=s2, par=par:
                                  t.matmul(
                                      self.pq[i][:, s2 * 512:s2 * 512 + 512],
                                      self.xnp[:, dc * 1024 + xoff + i * 128:
                                               dc * 1024 + xoff + i * 128 + 128],
                                      self.wvb[par][:, s2 * 512:s2 * 512 + 512],
                                      start=(dc == 0), stop=(dc == 15)),
                                  inc=(i == 3 and s2 == 1))
                self.t_wvb_free[par] = t_mm
            for i, lt in enumerate(lts):
                par = i % 2
                vs = self.vstage[par]
                wt("vector", self.qact, self.t_vst_free[par])
                src = self.pq[i][:, 0:1024].rearrange("p (h d) -> p h d", d=64)
                vr = vs[:, 0:1040].rearrange("p (h d) -> p h d", d=65)
                if prefix:
                    t_ev = op("vector", self.cdve, lambda v, s=src, vr=vr:
                              v.tensor_scalar(vr[:, :, 0:64], s,
                                              self.pmc[:, 0:1], None, ALU.mult),
                              waits=((self.cpe, t_mm),
                                     (self.qsp, self.t_consts)))
                    pmrep = bass.AP(self.pmc, 0, [[1, 128], [0, 16], [0, 1]])
                    t_ev = op("vector", self.cdve, lambda v, vr=vr:
                              v.tensor_scalar(vr[:, :, 64:65], pmrep, 1.0,
                                              None, ALU.mult))
                else:
                    t_ev = op("vector", self.cdve, lambda v, s=src, vr=vr:
                              v.tensor_copy(vr[:, :, 0:64], s),
                              waits=((self.cpe, t_mm),))
                    t_ev = op("vector", self.cdve, lambda v, vr=vr:
                              v.memset(vr[:, :, 64:65], 1.0))
                t_st = dma("scalar", self.qact,
                           self.v_scr[lt][:, half * 1040:half * 1040 + 1040],
                           vs[:], waits=((self.cdve, t_ev),))
                self.t_vst_free = [t_st, t_st]
                self.v_written[lt] = max(self.v_written[lt], t_st)
                self.free_bank((i, 0), self.cdve, t_ev)
                self.free_bank((i, 1), self.cdve, t_ev)

    def own_pair(self, pair):
        dma, op, wt = self.dma, self.op, self.wt
        g0, g1 = 2 * pair, 2 * pair + 1
        xsr = self.xn_scr.rearrange("c p t -> p c t")

        def ln_group(gi, g):
            t_xn = 0
            for sub in range(2):
                t_xn = self.ln_sub(self.xT_own, 4 * g, sub, gi * 512)
            src = self.xnp[:].rearrange("p (c t) -> p c t", c=16)[
                :, :, gi * 512:gi * 512 + 512]
            self.t_xnscr_all = dma(
                "scalar", self.qact, xsr[:, :, g * 512:g * 512 + 512], src,
                waits=((self.cdve, t_xn),))
            return t_xn

        def proj(which, sp, g, t_xn):
            t_cs0 = dma("sync", self.qsp, self.csb[sp][:, 0:512],
                        (self.cos_q if which == "q" else self.cos_k)[
                            :, (2048 if which == "k" else 0) + g * 512:
                            (2048 if which == "k" else 0) + g * 512 + 512],
                        waits=((self.cdve, self.t_cs_free[sp]),))
            t_cs = dma("sync", self.qsp, self.csb[sp][:, 512:1024],
                       (self.sin_q if which == "q" else self.sin_k)[
                           :, (2048 if which == "k" else 0) + g * 512:
                           (2048 if which == "k" else 0) + g * 512 + 512])
            wsrc = self.wq_p if which == "q" else self.wk_p
            for pc in range(16):
                par = pc % 2
                key = (pc % 2, 0)
                t_w = dma("gpsimd", self.qpool,
                          self.wkq[par][:].rearrange("p (c j) -> p c j", c=16),
                          wsrc[pc], waits=((self.cpe, self.t_wkq_free[par]),))
                self.pe_bank(key)
                wt("tensor", self.qpool, t_w)
                wt("tensor", self.cdve, t_xn)
                t_mm = 0
                for dc in range(16):
                    t_mm = op("tensor", self.cpe,
                              lambda t, dc=dc, sp=sp, par=par, key=key:
                              t.matmul(
                                  self.psl(key),
                                  self.wkq[par][:, dc * 128:dc * 128 + 128],
                                  self.xnp[:, dc * 1024 + sp * 512:
                                           dc * 1024 + sp * 512 + 512],
                                  start=(dc == 0), stop=(dc == 15)),
                              inc=False)
                t_mm = self.pe_fence(self.psl(key)[0:1, 0:256])
                self.t_wkq_free[par] = t_mm
                if which == "q":
                    t_rope = self.rope(
                        key, self.qg[sp][:, pc * 512:pc * 512 + 512], sp,
                        t_cs, t_mm, dst_waits=((self.cpe, self.t_qg_free),))
                    self.free_bank(key, self.cdve, t_rope)
                else:
                    t_rope = self.rope(key, self.kst[:], sp, t_cs, t_mm,
                                       dst_waits=((self.qact,
                                                   self.t_kst_free),))
                    self.free_bank(key, self.cdve, t_rope)
                    col = 2048 + g * 512
                    t_st = dma("scalar", self.qact,
                               self.k_scr[pc][:, col:col + 512], self.kst[:],
                               waits=((self.cdve, t_rope),))
                    self.t_kst_free = t_st
                    for lt in range(16 + 4 * g, 16 + 4 * g + 4):
                        self.k_written[lt] = max(self.k_written[lt], t_st)
                    self.knorm()
                self.sprinkle()
            self.t_cs_free[sp] = self.cdve.n

        t_xn0 = ln_group(0, g0)
        proj("q", 0, g0, t_xn0)
        proj("k", 0, g0, t_xn0)
        t_xn1 = ln_group(1, g1)
        self.v_own_quad([16 + 8 * pair + i for i in range(4)], 0,
                        prefix=False)
        self.attention(g0, 0)
        proj("q", 1, g1, t_xn1)
        proj("k", 1, g1, t_xn1)
        self.v_own_quad([20 + 8 * pair + i for i in range(4)], 512,
                        prefix=False)
        self.t_xnp_free = self.cpe.n
        self.attention(g1, 1)
        self.t_qg_free = self.cpe.n

    def attention(self, g, gi):
        dma, op, wt = self.dma, self.op, self.wt
        op("vector", self.cdve, lambda v: v.memset(self.kq[0:1, 1:2], 0.0))
        for hf in range(2):
            t_q2 = op("vector", self.cdve, lambda v, hf=hf: v.tensor_tensor(
                self.bsq[:], self.qg[gi][:, hf * 4096:hf * 4096 + 4096],
                self.qg[gi][:, hf * 4096:hf * 4096 + 4096], ALU.mult),
                waits=((self.cpe, self.t_bsq_free),))
            for pc in range(8):
                self.pe_bank((3, 1))
                wt("tensor", self.cdve, t_q2)
                t_mm = op("tensor", self.cpe, lambda t, pc=pc: t.matmul(
                    self.pq[3][0:1, 512:1024], self.ones_bf[:],
                    self.bsq[:, pc * 512:pc * 512 + 512],
                    start=True, stop=True))
                t_red = op("vector", self.cdve, lambda v: v.reduce_max(
                    self.kq[0:1, 2:3], self.pq[3][0:1, 512:1024], axis=AX.X),
                    waits=((self.cpe, t_mm),))
                self.wt("vector", self.cdve, t_red)
                self.free_bank((3, 1), self.cdve, t_red)
                op("vector", self.cdve, lambda v: v.tensor_tensor(
                    self.kq[0:1, 1:2], self.kq[0:1, 1:2], self.kq[0:1, 2:3],
                    ALU.max))
            self.t_bsq_free = self.cpe.n
        t_prod = op("vector", self.cdve, lambda v: v.tensor_tensor(
            self.kq[0:1, 3:4], self.kq[0:1, 0:1], self.kq[0:1, 1:2],
            ALU.mult))
        op("scalar", self.cact, lambda s: s.activation(
            self.kq[0:1, 3:4], self.kq[0:1, 3:4], AF.Ln,
            bias=self.epsc[0:1, 1:2], scale=1.0),
            waits=((self.cdve, t_prod),))
        t_c = op("scalar", self.cact, lambda s: s.activation(
            self.kq[0:1, 3:4], self.kq[0:1, 3:4], AF.Exp,
            bias=self.zc[0:1, 0:1], scale=0.5))
        t_negc0 = op("vector", self.cdve, lambda v: v.tensor_scalar(
            self.kq[0:1, 4:5], self.kq[0:1, 3:4], -1.0, None, ALU.mult),
            waits=((self.cact, t_c),))
        self.pe_bank((3, 1))
        wt("tensor", self.cdve, t_negc0)
        op("tensor", self.cpe, lambda t: t.matmul(
            self.pq[3][:, 512:513], self.onesrow[:], self.kq[0:1, 4:5],
            start=True, stop=True), inc=False)
        t_bc = self.pe_fence(self.pq[3][0:1, 512:768])
        t_negc = op("vector", self.cdve, lambda v: v.tensor_copy(
            self.negc[:], self.pq[3][:, 512:513]), waits=((self.cpe, t_bc),))
        self.free_bank((3, 1), self.cdve, t_negc)
        if self.dump_scr:
            self.dma("scalar", self.qact,
                     self.dbg_negc[g:g + 1].rearrange("a b -> b a"),
                     self.negc[:], waits=((self.cdve, t_negc),))
            self.dma("scalar", self.qact, self.dbg_kq[g:g + 1], self.kq[:],
                     waits=((self.cdve, t_negc),))
        def osl(s):
            return (self.pq[2][:, 0:512], self.pq[2][:, 512:1024],
                    self.pq[3][:, 0:512])[s]
        OKEY = {0: (2, 0), 1: (2, 1), 2: (3, 0)}
        pending = []

        def emit_chain(job):
            c0, e0, slot, t_avh, hh, g0_, last = job
            if self.dump_scr:
                h_ = 2 * c0 + e0
                op("vector", self.cdve, lambda v, slot=slot, h_=h_:
                   v.tensor_copy(self.stats[0:1, 1280 + 2 * h_:1280 + 2 * h_ + 2],
                                 osl(slot)[64:65, 0:2]),
                   waits=((self.cpe, t_avh),))
            t_rec = op("vector", self.cdve, lambda v, slot=slot:
                       v.reciprocal(self.recrow[:], osl(slot)[64:65, :]),
                       waits=((self.cpe, t_avh),))
            self.pe_bank((3, 1))
            wt("tensor", self.cdve, t_rec)
            op("tensor", self.cpe, lambda t: t.matmul(
                self.pq[3][:, 512:1024], self.onesrow[:], self.recrow[:],
                start=True, stop=True), inc=False)
            t_bcr = self.pe_fence(self.pq[3][0:1, 512:768])
            t_recb = op("vector", self.cdve, lambda v: v.tensor_copy(
                self.recb[:], self.pq[3][:, 512:1024]),
                waits=((self.cpe, t_bcr),))
            self.free_bank((3, 1), self.cdve, t_recb)
            wt("vector", self.qact, self.t_aost_free)
            t_stage = op("vector", self.cdve, lambda v, slot=slot, hh=hh:
                         v.tensor_tensor(
                             self.aost[0:64, hh * 512:hh * 512 + 512],
                             osl(slot)[0:64, :], self.recb[0:64, 0:512],
                             ALU.mult))
            self.free_bank(OKEY[slot], self.cdve, t_stage)
            if hh == 15:
                h0 = 2 * c0 + e0 - 15
                dstap = self.ao_scr.rearrange("c (e p) t -> p c e t", e=2)[
                    :, h0 // 2:h0 // 2 + 8, :, g0_ * 512:g0_ * 512 + 512]
                srcap = self.aost[:].rearrange("p (c e t) -> p c e t",
                                               c=8, e=2)
                t_fl = dma("scalar", self.qact, dstap, srcap,
                           waits=((self.cdve, t_stage),))
                self.t_aost_free = t_fl
                self.t_ao_written = t_fl

        t_exp = {}
        for c in range(16):
            if c < 15:
                lo, hi = 12 + 8 * (g // 2), 20 + 4 * g
            else:
                lo, hi = 0, 20 + 4 * g
            par = c % 2
            t_k = dma("sync", self.qsp, self.kbuf[par][:, 0:(hi - lo) * 128],
                      self.k_scr[c][:, lo * 128:hi * 128],
                      waits=((self.qact, max(self.k_written[lo:hi])),
                             (self.cpe, self.t_kbuf_free[par])))
            t_v = dma("sync", self.qsp, self.vbuf[par][:, 0:(hi - lo) * 130],
                      self.v_scr.rearrange("t p c -> p t c")[
                          :, lo:hi, c * 130:c * 130 + 130],
                      waits=((self.qact, max(self.v_written[lo:hi])),
                             (self.cpe, self.t_vbuf_free[par])))
            tiles = list(range(lo, hi))
            nb = (len(tiles) + 1) // 2
            slot = {0: (2 * c) % 3, 1: (2 * c + 1) % 3}
            t_av = {0: 0, 1: 0}
            first_av = {0: True, 1: True}
            t_avdone = {0: 0, 1: 0}
            for bi in range(nb):
                bt = tiles[2 * bi: 2 * bi + 2]
                for e in range(2):
                    if bi >= 1:
                        self.wt("tensor", self.cact, t_exp.get((e, bi - 1), 0))
                    else:
                        self.pe_bank((e, 0))
                        self.pe_bank((e, 1))
                    wt("tensor", self.qsp, t_k)
                    for k, lt in enumerate(bt):
                        op("tensor", self.cpe, lambda t, k=k, lt=lt, par=par,
                           e=e, gi=gi, c=c, lo=lo: t.matmul(
                               self.pq[e][0:128, k * 512:k * 512 + 512],
                               self.kbuf[par][64 * e:64 * e + 64,
                                              (lt - lo) * 128:
                                              (lt - lo) * 128 + 128],
                               self.qg[gi][64 * e:64 * e + 64,
                                           c * 512:c * 512 + 512],
                               start=True, stop=True), inc=False)
                    t_s = self.pe_fence(self.pq[e][0:1, 0:256])
                    t_exp[(e, bi)] = op(
                        "scalar", self.cact, lambda s, e=e, n=len(bt), bi=bi:
                        s.activation(
                            self.abuf[2 * e + bi % 2][:, 0:n * 512],
                            self.pq[e][:, 0:n * 512], AF.Exp,
                            bias=self.negc[:, 0:1], scale=1.0),
                        waits=((self.cpe, t_s), (self.cdve, t_negc)))
                if c == 0 or (bi == 0 and pending):
                    pass
                if bi == 0 and pending:
                    for job in pending:
                        emit_chain(job)
                    pending = []
                for e in range(2):
                    ab = self.abuf[2 * e + bi % 2]
                    t_msk = 0
                    for k, lt in enumerate(bt):
                        mi = lt - (16 + 4 * g)
                        if mi >= 0:
                            t_msk = op("vector", self.cdve,
                                       lambda v, k=k, ab=ab, mi=mi:
                                       v.tensor_tensor(
                                           ab[:, k * 512:k * 512 + 512],
                                           ab[:, k * 512:k * 512 + 512],
                                           self.masks[:,
                                                      mi * 512:mi * 512 + 512],
                                           ALU.mult),
                                       waits=((self.cact, t_exp[(e, bi)]),
                                              (self.qpool, self.t_sdbb)))
                    wt("tensor", self.cact, t_exp[(e, bi)])
                    if t_msk:
                        wt("tensor", self.cdve, t_msk)
                    wt("tensor", self.qsp, t_v)
                    if first_av[e]:
                        self.pe_bank(OKEY[slot[e]])
                    for k, lt in enumerate(bt):
                        op("tensor", self.cpe, lambda t, k=k, lt=lt, par=par,
                           e=e, ab=ab, first=first_av[e],
                           last=(lt == tiles[-1]), lo=lo, sl=slot[e]:
                           t.matmul(
                               osl(sl)[0:65, :],
                               self.vbuf[par][:, (lt - lo) * 130 + 65 * e:
                                              (lt - lo) * 130 + 65 * e + 65],
                               ab[:, k * 512:k * 512 + 512],
                               start=first, stop=last), inc=False)
                        first_av[e] = False
                    if bt[-1] == tiles[-1]:
                        t_avdone[e] = self.pe_fence(osl(slot[e])[0:1, 0:256])
            for e in range(2):
                self.free_bank((e, 0), self.cact, t_exp[(e, nb - 1)])
                self.free_bank((e, 1), self.cact, t_exp[(e, nb - 1)])
            import os as _os
            for e in range(2):
                pending.append((c, e, slot[e], t_avdone[e], (2 * c + e) % 16,
                                g, c == 15))
            if _os.environ.get("MK_NODEFER"):
                for job in pending:
                    emit_chain(job)
                pending = []
            self.t_kbuf_free[par] = self.cpe.n
            self.t_vbuf_free[par] = self.cpe.n
            self.sprinkle(2)
        for job in pending:
            emit_chain(job)
        pending = []
        if self.dump_scr:
            dma("scalar", self.qact, self.dbg_sums[g:g + 1],
                self.stats[0:1, 1280:1344], waits=((self.cdve, self.cdve.n),))

    def ffn_super(self, sup):
        dma, op, wt = self.dma, self.op, self.wt
        cols = slice(1024 * sup, 1024 * sup + 1024)
        self.lockstep = (self.lockmode == "full")
        t_prev = self.t_super_done
        if sup == 0:
            for eng in ("sync", "gpsimd"):
                for sem in (self.cpe, self.cdve, self.cact, self.cpool):
                    self.wt(eng, sem, sem.n)
                self.wt(eng, self.qsp, self.qsp.n)
                self.wt(eng, self.qact, self.qact.n)
                self.wt(eng, self.qpool, self.qpool.n)
        xsr = self.xn_scr.rearrange("c p t -> p c t")
        t_xn = dma("sync", self.qsp, self.xnao[:].rearrange(
            "p (c t) -> p c t", c=16), xsr[:, :, cols],
            waits=((self.cpe, t_prev), (self.qact, self.t_xnscr_all)))
        t_mm = 0
        for h in range(44):
            par = h % 2
            rg, ru = (2 * h) % 3, (2 * h + 1) % 3
            t_wg = dma("scalar", self.qact, self.wgu[rg][:], self.wg_c[h],
                       waits=((self.cpe, self.t_wgu_free[rg]),
                              (self.qact, self.t_cache_done)))
            t_wu = dma("scalar", self.qact, self.wgu[ru][:], self.wu_c[h],
                       waits=((self.cpe, self.t_wgu_free[ru]),))
            t_gmm = t_umm = 0
            for which, wbuf, t_w, pi in (("g", self.wgu[rg], t_wg, par),
                                         ("u", self.wgu[ru], t_wu,
                                          2 + par)):
                self.pe_bank((pi, 0))
                self.pe_bank((pi, 1))
                wt("tensor", self.qact, t_w)
                wt("tensor", self.qsp, t_xn)
                for dc in range(16):
                    for s2 in range(2):
                        t_mm = op("tensor", self.cpe,
                                  lambda t, dc=dc, s2=s2, wbuf=wbuf, pi=pi:
                                  t.matmul(
                                      self.pq[pi][:, s2 * 512:s2 * 512 + 512],
                                      wbuf[:, dc * 128:dc * 128 + 128],
                                      self.xnao[:, dc * 1024 + s2 * 512:
                                                dc * 1024 + s2 * 512 + 512],
                                      start=(dc == 0), stop=(dc == 15)),
                                  inc=(dc == 15 and s2 == 1))
                if which == "g":
                    self.t_wgu_free[rg] = t_mm
                    t_gmm = t_mm
                else:
                    self.t_wgu_free[ru] = t_mm
                    t_umm = t_mm
            t_si = op("scalar", self.cact, lambda s, par=par, h=h:
                      s.activation(self.silu_sb[:], self.pq[par][:, 0:1024],
                                   AF.Silu, bias=self.zc[:, 0:1],
                                   scale=self.sgs[:, h:h + 1]),
                      waits=((self.cpe, t_gmm), (self.qsp, self.t_consts),
                             (self.cdve, getattr(self, "t_gu_prev", 0)),))
            t_up = op("scalar", self.cact, lambda s, par=par, h=h:
                      s.activation(self.up_sb[:], self.pq[2 + par][:, 0:1024],
                                   AF.Copy, bias=0.0,
                                   scale=self.sus[:, h:h + 1]),
                      waits=((self.cpe, t_umm),))
            t_up = self.act_fence(self.up_sb[0:1, 1022:1024])
            t_gu = op("vector", self.cdve, lambda v, h=h: v.tensor_tensor(
                self.gub[:, h * 1024:h * 1024 + 1024], self.silu_sb[:],
                self.up_sb[:], ALU.mult),
                waits=((self.cact, t_up), (self.cpe, t_prev)))
            self.t_gu_prev = t_gu
            self.free_bank((par, 0), self.cact, t_si)
            self.free_bank((par, 1), self.cact, t_si)
            self.free_bank((2 + par, 0), self.cact, t_up)
            self.free_bank((2 + par, 1), self.cact, t_up)
        t_gu_all = self.cdve.n
        if self.lockmode in ("tail", "full"):
            self.lockstep = True
        if self.dump_scr and sup == 0:
            self.dma("scalar", self.qact, self.dbg_gu, self.gub[:],
                     waits=((self.cdve, t_gu_all),))
        for iq in range(4):
            t_ao = dma("sync", self.qsp, self.xnao[:].rearrange(
                "p (c t) -> p c t", c=16)[:, 4 * iq:4 * iq + 4, :],
                self.ao_scr.rearrange("c p t -> p c t")[
                    :, 4 * iq:4 * iq + 4, cols],
                waits=((self.cpe, t_mm), (self.qact, self.t_ao_written)))
        for qtr in range(4):
            self._lock_force("tensor")
            self._lock_force("vector")
            for dcq in range(4):
                self.pe_bank((dcq, 0))
                self.pe_bank((dcq, 1))
            DP = 4
            if qtr == 0:
                self.t_wd_ready = [0] * 4
                self._wd_next = 0
                for _ in range(DP):
                    self.wd_load_flat(dma)
            for h in range(44):
                bpar = h % 4
                wt("tensor", self.qact, self.t_wd_ready[bpar])
                wt("tensor", self.cdve, t_gu_all)
                t_dmm = 0
                for dcq in range(4):
                    for s2 in range(2):
                        t_dmm = op("tensor", self.cpe, lambda t, dcq=dcq,
                                   s2=s2, bpar=bpar, h=h: t.matmul(
                                       self.pq[dcq][:, s2 * 512:s2 * 512 + 512],
                                       self.wdb[bpar][:,
                                                      dcq * 128:dcq * 128 + 128],
                                       self.gub[:, h * 1024 + s2 * 512:
                                                h * 1024 + s2 * 512 + 512],
                                       start=(h == 0), stop=False),
                                   inc=(dcq == 3 and s2 == 1))
                self.t_wdb_free[bpar] = t_dmm
                self.wd_load_flat(dma)
            t_wo_r = [0] * 4
            for dcq in range(4):
                t_wo_r[dcq] = dma(
                    "gpsimd", self.qpool,
                    self.wob[dcq][:].rearrange("p (c j) -> p c j", c=16),
                    self.wo_p[4 * qtr + dcq].rearrange("c p j -> p c j"),
                    waits=((self.cpe, self.t_wob_free[dcq]),))
            for dcq in range(4):
                dc = 4 * qtr + dcq
                par = dcq % 2
                t_wo = t_wo_r[dcq]
                wt("tensor", self.qpool, t_wo)
                wt("tensor", self.qsp, t_ao)
                t_mm2 = 0
                for pc in range(16):
                    for s2 in range(2):
                        t_mm2 = op("tensor", self.cpe,
                                   lambda t, pc=pc, s2=s2, par=par, dcq=dcq:
                                   t.matmul(
                                       self.pq[dcq][:,
                                                    s2 * 512:s2 * 512 + 512],
                                       self.wob[dcq][:,
                                                     pc * 128:pc * 128 + 128],
                                       self.xnao[:, pc * 1024 + s2 * 512:
                                                 pc * 1024 + s2 * 512 + 512],
                                       start=False, stop=(pc == 15)),
                                   inc=(pc == 15 and s2 == 1))
                self.t_wob_free[dcq] = t_mm2
                t_mm2 = self.pe_fence(self.pq[dcq][0:1, 0:256])
                t_xj = dma("sync", self.qsp, self.xjit[par][:],
                           self.xT_own.rearrange("(c p) t -> p c t", p=128)[
                               :, dc, cols],
                           waits=((self.cdve, self.t_xjit_free[par]),))
                if "evac" in getattr(self, "lock_sites", ()):
                    self._lock_force("vector")
                t_ev = op("vector", self.cdve, lambda v, dcq=dcq, dc=dc,
                          par=par: v.scalar_tensor_tensor(
                              self.outsb[par][:], self.pq[dcq][:, 0:1024],
                              self.bos[0:128, dc:dc + 1], self.xjit[par][:],
                              ALU.add, ALU.add),
                          waits=((self.cpe, t_mm2), (self.qsp, t_xj),
                                 (self.qsp, self.t_consts),
                                 (self.qact, self.t_outsb_free[par])))
                self.t_xjit_free[par] = t_ev
                t_evf = self.dve_fence(self.outsb[par][0:1, 0:1])
                self.free_bank((dcq, 0), self.cdve, t_evf)
                self.free_bank((dcq, 1), self.cdve, t_evf)
                t_st = dma("scalar", self.qact, self.out_fm[dc][:, cols],
                           self.outsb[par][:], waits=((self.cdve, t_ev),))
                self.t_outsb_free[par] = t_st
        self.t_super_done = self.cpe.n

    def finalize(self):
        self.wt("sync", self.qact, self.qact.n)
        self.wt("sync", self.qsp, self.qsp.n)
        self.wt("sync", self.qpool, self.qpool.n)



def _prep_shared(Wq, Wk, Wv, Wo, bo, wg, sg, wu, su, wd, sd):
    f32 = np.float32
    pc_layout = lambda w: np.ascontiguousarray(
        w.reshape(16, 128, 16, 128).transpose(0, 3, 2, 1).astype(f32))
    d = {}
    d["wq_p"] = pc_layout(Wq)
    d["wk_p"] = pc_layout(Wk)
    wvT = np.ascontiguousarray(Wv.T)
    d["wv_m"] = np.ascontiguousarray(
        wvT.reshape(16, 128, 2, 1024).transpose(2, 0, 1, 3).astype(f32))
    d["wvg_m"] = np.ascontiguousarray(
        wvT[:, 1920:2048].reshape(16, 128, 128).astype(f32))
    d["wo_p"] = np.ascontiguousarray(
        Wo.reshape(16, 128, 16, 128).transpose(0, 2, 3, 1).astype(f32))
    gl = lambda w: np.ascontiguousarray(
        w.reshape(44, 128, 16, 128).transpose(0, 3, 2, 1).astype(np.int32))
    d["wg_p"] = gl(wg)
    d["wu_p"] = gl(wu)
    wdT = np.ascontiguousarray(wd.T)
    d["wd_p"] = np.ascontiguousarray(
        wdT.reshape(44, 128, 4, 4, 128).transpose(2, 0, 1, 3, 4).astype(
            np.int32))
    d["sg_r"] = np.ascontiguousarray(sg.reshape(44, 128).T.astype(f32))
    d["su_r"] = np.ascontiguousarray(su.reshape(44, 128).T.astype(f32))
    d["bo_r"] = np.ascontiguousarray(bo.reshape(16, 128).T.astype(f32))
    d["sdb"] = np.ascontiguousarray(
        np.broadcast_to(sd[None, :].astype(f32), (128, 2048)))
    m = np.zeros((128, 4, 512), dtype=f32)
    ii = np.arange(128)[:, None]
    jj = np.arange(512)[None, :]
    for dd in range(4):
        m[:, dd, :] = (jj >= ii + dd * 128).astype(f32)
    d["masks_f"] = np.ascontiguousarray(m.reshape(128, 2048))
    r2 = np.zeros((128, 128), dtype=f32)
    for i2 in range(128):
        base = (i2 // 64) * 64
        k = i2 - base
        if k < 32:
            r2[base + k + 32, i2] = -1.0
        else:
            r2[base + k - 32, i2] = 1.0
    d["r2t_f"] = r2
    inv = 1.0 / (10000.0 ** (np.arange(0, 64, 2, dtype=np.float64) / 64.0))
    rowf = inv[np.arange(128) % 32]
    return d, rowf


def _rope_tables(rowf, half):
    f32 = np.float32
    pos_own = np.arange(2048, dtype=np.float64) + 2048.0 * half
    pos_pref = np.arange(2048, dtype=np.float64)
    ang_own = rowf[:, None] * pos_own[None, :]
    ang_pref = rowf[:, None] * pos_pref[None, :]
    cos_q = (np.cos(ang_own) * 0.125).astype(f32)
    sin_q = (np.sin(ang_own) * 0.125).astype(f32)
    cos_k = np.concatenate([np.cos(ang_pref), np.cos(ang_own)],
                           axis=1).astype(f32)
    sin_k = np.concatenate([np.sin(ang_pref), np.sin(ang_own)],
                           axis=1).astype(f32)
    return cos_q, sin_q, cos_k, sin_k


_PROG_CACHE = {}


def _get_prog():
    if "nc" not in _PROG_CACHE:
        _PROG_CACHE["nc"] = MK().build()
    return _PROG_CACHE["nc"]


def kernel(L0_attn_hidden_states, L0_attn_Wq, L0_attn_Wk, L0_attn_Wv,
           L0_attn_Wo, L0_attn_bo, L0_ffn_wq_gate, L0_ffn_s_gate,
           L0_ffn_wq_up, L0_ffn_s_up, L0_ffn_wq_down, L0_ffn_s_down):
    x = np.asarray(L0_attn_hidden_states, dtype=np.float32)
    shared, rowf = _prep_shared(
        np.asarray(L0_attn_Wq, np.float32), np.asarray(L0_attn_Wk, np.float32),
        np.asarray(L0_attn_Wv, np.float32), np.asarray(L0_attn_Wo, np.float32),
        np.asarray(L0_attn_bo, np.float32),
        np.asarray(L0_ffn_wq_gate, np.int32),
        np.asarray(L0_ffn_s_gate, np.float32),
        np.asarray(L0_ffn_wq_up, np.int32),
        np.asarray(L0_ffn_s_up, np.float32),
        np.asarray(L0_ffn_wq_down, np.int32),
        np.asarray(L0_ffn_s_down, np.float32))
    in_maps = []
    for core in range(8):
        b2, half = core // 2, core % 2
        xT = np.ascontiguousarray(x[b2].T)
        cos_q, sin_q, cos_k, sin_k = _rope_tables(rowf, half)
        m = dict(shared)
        m["xT_own"] = np.ascontiguousarray(
            xT[:, 2048 * half:2048 * half + 2048])
        m["xT_pref"] = np.ascontiguousarray(xT[:, 0:2048])
        m["cos_q"], m["sin_q"] = cos_q, sin_q
        m["cos_k"], m["sin_k"] = cos_k, sin_k
        m["pm"] = np.full((128, 1), 1.0 if half else 0.0, dtype=np.float32)
        in_maps.append(m)
    nc = _get_prog()
    res = run_bass_kernel_spmd(nc, in_maps, core_ids=list(range(8)))
    out = np.empty((4, 4096, 2048), dtype=np.float32)
    for core in range(8):
        b2, half = core // 2, core % 2
        fm = res.results[core]["out_fm"].reshape(2048, 2048)
        out[b2, 2048 * half:2048 * half + 2048, :] = fm.T
    return out


# version 18
# speedup vs baseline: 1.1566x; 1.0159x over previous
import contextlib

import numpy as np
import concourse.bass as bass
import concourse.mybir as mybir
from concourse.bass_utils import run_bass_kernel_spmd
from concourse.alu_op_type import AluOpType as ALU

F32 = mybir.dt.float32
BF16 = mybir.dt.bfloat16
I32 = mybir.dt.int32
AF = mybir.ActivationFunctionType
AX = mybir.AxisListType

D = 2048
TOWN = 2048
LSUB = 256
EPS = 1e-5

ENGINES = ("tensor", "vector", "scalar", "gpsimd", "sync")


class SemT:
    def __init__(self, handle):
        self.h = handle
        self.n = 0

    def bump(self, k=1):
        self.n += k
        return self.n


class MK:

    def __init__(self, dump_scr=False):
        self.dump_scr = dump_scr
        self.nc = bass.Bass("TRN2", target_bir_lowering=False)
        nc = self.nc
        ap = lambda n, s, d=F32: nc.dram_tensor(n, s, d,
                                                kind="ExternalInput").ap()
        self.xT_own = ap("xT_own", [D, TOWN])
        self.xT_pref = ap("xT_pref", [D, TOWN])
        self.wq_p = ap("wq_p", [16, 128, 16, 128])
        self.wk_p = ap("wk_p", [16, 128, 16, 128])
        self.wv_m = ap("wv_m", [2, 16, 128, 1024])
        self.wvg_m = ap("wvg_m", [16, 128, 128])
        self.wo_p = ap("wo_p", [16, 16, 128, 128])
        self.wg_p = ap("wg_p", [44, 128, 16, 128], I32)
        self.wu_p = ap("wu_p", [44, 128, 16, 128], I32)
        self.wd_p = ap("wd_p", [4, 44, 128, 4, 128], I32)
        self.sg_r = ap("sg_r", [128, 44])
        self.su_r = ap("su_r", [128, 44])
        self.bo_r = ap("bo_r", [128, 16])
        self.sdb = ap("sdb", [128, 2048])
        self.cos_q = ap("cos_q", [128, 2048])
        self.sin_q = ap("sin_q", [128, 2048])
        self.cos_k = ap("cos_k", [128, 4096])
        self.sin_k = ap("sin_k", [128, 4096])
        self.r2t_f = ap("r2t_f", [128, 128])
        self.masks_f = ap("masks_f", [128, 4 * 512])
        self.pm = ap("pm", [128, 1])
        scr_kind = "ExternalOutput" if dump_scr else "Internal"
        scr = lambda n, s, d: nc.dram_tensor(n, s, d, kind=scr_kind).ap()
        self.xn_scr = scr("xn_scr", [16, 128, 2048], BF16)
        self.ao_scr = scr("ao_scr", [16, 128, 2048], BF16)
        self.k_scr = scr("k_scr", [16, 128, 4096], BF16)
        self.v_scr = scr("v_scr", [32, 128, 2080], BF16)
        self.wg_c = scr("wg_c", [44, 128, 2048], BF16)
        self.wu_c = scr("wu_c", [44, 128, 2048], BF16)
        self.wd_c = scr("wd_c", [4, 44, 128, 512], BF16)
        self.out_fm = nc.dram_tensor("out_fm", [16, 128, 2048], F32,
                                     kind="ExternalOutput").ap()
        if dump_scr:
            self.dbg_negc = nc.dram_tensor("dbg_negc", [4, 128], F32,
                                           kind="ExternalOutput").ap()
            self.dbg_kq = nc.dram_tensor("dbg_kq", [4, 8], F32,
                                         kind="ExternalOutput").ap()
            self.dbg_rec = nc.dram_tensor("dbg_rec", [4, 512], F32,
                                          kind="ExternalOutput").ap()
            self.dbg_sums = nc.dram_tensor("dbg_sums", [4, 64], F32,
                                           kind="ExternalOutput").ap()
            self.dbg_qg = nc.dram_tensor("dbg_qg", [2, 128, 8192], BF16,
                                         kind="ExternalOutput").ap()
            self.dbg_gu = nc.dram_tensor("dbg_gu", [128, 45056], BF16,
                                         kind="ExternalOutput").ap()
            self.dbg_dn = nc.dram_tensor("dbg_dn", [4, 128, 1024], F32,
                                         kind="ExternalOutput").ap()
            self.dbg_wd = nc.dram_tensor("dbg_wd", [4, 44, 128, 512], BF16,
                                         kind="ExternalOutput").ap()

        self.q = {e: [] for e in ENGINES}
        self.last_wait = {}
        self.t_xbuf_free = 0
        self.t_bsq_free = 0
        self.t_xnp_free = 0
        self.t_qg_free = 0
        self.t_kst_free = 0
        self.t_k2_free = 0
        self.t_ropepr = 0
        self.t_cs_free = [0, 0]
        self.t_aost_free = 0
        self.t_ao_written = 0
        self.t_xnscr_all = 0
        self.t_super_done = 0
        self.k_written = [0] * 32
        self.v_written = [0] * 32

    def em(self, eng, fn):
        self.q[eng].append(fn)

    def wt(self, eng, sem, tgt):
        if tgt <= 0:
            return
        key = (eng, id(sem))
        if self.last_wait.get(key, -1) >= tgt:
            return
        self.last_wait[key] = tgt
        self.q[eng].append(lambda E, s=sem.h, t=tgt: E.wait_ge(s, t))

    def flush(self):
        for e in ENGINES:
            fns = self.q[e]
            if not fns:
                continue
            self.q[e] = []

            def section(E, fns=fns):
                for f in fns:
                    f(E)

            getattr(self.block, e)(section)

    def dma(self, qeng, sem, dst, src, waits=()):
        if getattr(self, "lockstep", False):
            self._lock(qeng)
        for (s, t) in waits:
            self.wt(qeng, s, t)
        tgt = sem.bump(16)
        self.em(qeng, lambda Q, d=dst, s=src, h=sem.h:
                Q.dma_start(d, s).then_inc(h, 16))
        return tgt

    def pe_fence(self, elem_ap):
        t = elem_ap.tensor if hasattr(elem_ap, "tensor") else None
        return self.op("tensor", self.cpe, lambda t2: t2.matmul(
            elem_ap, self.zb[:], self.zrow[:], start=False, stop=True))

    def sprinkle(self, k=1):
        if self.cache_jobs:
            self.emit_cache_jobs(k)

    def wd_load_flat(self, dma):
        j = self._wd_next
        if j >= 176:
            return
        self._wd_next += 1
        qtr, h = j // 44, j % 44
        bpar = j % 4
        t_wd = dma("scalar", self.qact, self.wdb[bpar][:],
                   self.wd_c[qtr, h],
                   waits=((self.cpe, self.t_wdb_free[bpar]),
                          (self.qact, self.t_cache_done)))
        if self.lockmode == "slimtail":
            t_tc = self.op("gpsimd", self.cpool,
                           lambda g, bpar=bpar: g.tensor_tensor(
                               self.fence_sb[0:1, 4:8],
                               self.wdb[bpar][0:1, 0:4],
                               self.wdb[bpar][0:1, 4:8], ALU.add),
                           waits=((self.qact, t_wd),))
            self.t_wd_ready[bpar] = (self.cpool, t_tc)
        else:
            self.t_wd_ready[bpar] = (self.qact, t_wd)

    def act_fence(self, src_ap):
        return self.op("scalar", self.cact, lambda sc: sc.activation(
            self.fence_sb[0:1, 2:4], src_ap, AF.Copy, bias=0.0, scale=1.0))

    def dve_fence(self, src_ap=None):
        if src_ap is None:
            src_ap = self.fence_sb[0:1, 1:2]
        return self.op("vector", self.cdve, lambda v: v.tensor_copy(
            self.fence_sb[0:1, 0:1], src_ap))

    def _lock_force(self, eng):
        for sem2 in (self.cpe, self.cdve, self.cact, self.qsp):
            self.wt(eng, sem2, sem2.n)

    def _lock(self, eng):
        if self.lock_engs and eng not in self.lock_engs:
            return
        sems = getattr(self, "lock_sems", None) or (
            self.cpe, self.cdve, self.cact, self.cpool,
            self.qsp, self.qact, self.qpool)
        for sem2 in sems:
            self.wt(eng, sem2, sem2.n)

    def op(self, eng, sem, fn, waits=(), inc=True):
        if getattr(self, "lockstep", False):
            self._lock(eng)
        for (s, t) in waits:
            self.wt(eng, s, t)
        if inc:
            tgt = sem.bump()
            self.em(eng, lambda E, f=fn, h=sem.h: f(E).then_inc(h, 1))
            return tgt
        self.em(eng, fn)
        return sem.n

    def pe_bank(self, key):
        s, t = self.bank_free[key]
        self.wt("tensor", s, t)

    def free_bank(self, key, sem, tgt):
        self.bank_free[key] = (sem, tgt)

    def psl(self, key):
        i, hlf = key
        return self.pq[i][:, hlf * 512:(hlf + 1) * 512]

    def build(self):
        nc = self.nc
        st = contextlib.ExitStack()
        E = st.enter_context
        with st:
            self.masks = E(nc.sbuf_tensor("masks", [128, 4 * 512], BF16))
            self.r2t = E(nc.sbuf_tensor("r2t", [128, 128], BF16))
            self.sgs = E(nc.sbuf_tensor("sgs", [128, 44], F32))
            self.sus = E(nc.sbuf_tensor("sus", [128, 44], F32))
            self.bos = E(nc.sbuf_tensor("bos", [128, 16], F32))
            self.sdbb = E(nc.sbuf_tensor("sdbb", [128, 2048], BF16))
            self.pmc = E(nc.sbuf_tensor("pmc", [128, 1], F32))
            self.ones_bf = E(nc.sbuf_tensor("ones_bf", [128, 1], BF16))
            self.onesrow = E(nc.sbuf_tensor("onesrow", [1, 128], F32))
            self.zc = E(nc.sbuf_tensor("zc", [128, 1], F32))
            self.fence_sb = E(nc.sbuf_tensor("fence_sb", [1, 8], F32))
            self.dqb = [E(nc.sbuf_tensor("dqb0", [128, 2048], BF16))]
            self.c128b = E(nc.sbuf_tensor("c128b", [128, 2048], BF16))
            self.zb = E(nc.sbuf_tensor("zb", [1, 1], BF16))
            self.zrow = E(nc.sbuf_tensor("zrow", [1, 256], BF16))
            self.epsc = E(nc.sbuf_tensor("epsc", [1, 2], F32))
            self.stats = E(nc.sbuf_tensor("stats", [1, 1536], F32))
            self.kq = E(nc.sbuf_tensor("kq", [1, 8], F32))
            self.negc = E(nc.sbuf_tensor("negc", [128, 1], F32))
            self.rstdb = E(nc.sbuf_tensor("rstdb", [128, 512], F32))
            self.pq = [E(nc.psum_tensor(f"pq{i}", [128, 1024], F32))
                       for i in range(4)]
            self.qsp = SemT(E(nc.semaphore("qsp")))
            self.qact = SemT(E(nc.semaphore("qact")))
            self.qpool = SemT(E(nc.semaphore("qpool")))
            self.cpe = SemT(E(nc.semaphore("cpe")))
            self.cdve = SemT(E(nc.semaphore("cdve")))
            self.cact = SemT(E(nc.semaphore("cact")))
            self.cpool = SemT(E(nc.semaphore("cpool")))
            self.block = E(nc.Block())
            self.bank_free = {(i, h): (self.cdve, 0)
                              for i in range(4) for h in range(2)}

            self.setup()
            self.build_ffn_cache()
            stA = contextlib.ExitStack()
            EA = stA.enter_context
            with stA:
                self.xbuf = EA(nc.sbuf_tensor("xbuf", [128, 16 * LSUB], F32))
                self.recrow = EA(nc.sbuf_tensor("recrow", [1, 512], F32))
                self.recb = EA(nc.sbuf_tensor("recb", [128, 512], F32))
                self.q0b = EA(nc.sbuf_tensor("q0b", [128, 512], BF16))
                self.t1b = EA(nc.sbuf_tensor("t1b", [128, 512], F32))
                self.t2b = EA(nc.sbuf_tensor("t2b", [128, 512], F32))
                self.kst = EA(nc.sbuf_tensor("kst", [128, 512], BF16))
                self.k2b = self.q0b
                self.abuf = [EA(nc.sbuf_tensor(f"abuf{i}", [128, 1024], BF16))
                             for i in range(2)]
                self.csb = [EA(nc.sbuf_tensor(f"csb{i}", [128, 1024], F32))
                            for i in range(2)]


                self.bsq = EA(nc.sbuf_tensor("bsq", [128, 16 * 256], BF16))
                self.xnp = EA(nc.sbuf_tensor("xnp", [128, 16 * 1024], BF16))
                self.qg = [EA(nc.sbuf_tensor(f"qg{i}", [128, 16 * 512], BF16))
                           for i in range(2)]
                self.kbuf = [EA(nc.sbuf_tensor(f"kbuf{i}", [128, 4096], BF16))
                             for i in range(2)]
                self.vbuf = [EA(nc.sbuf_tensor(f"vbuf{i}", [128, 32 * 130],
                                               BF16)) for i in range(2)]
                self.vstage = [EA(nc.sbuf_tensor("vstage0", [128, 1040],
                                                 BF16))] * 2
                self.t_vst_free = [0, 0]
                self.aost = EA(nc.sbuf_tensor("aost", [64, 16 * 512], BF16))
                self.wkq = [EA(nc.sbuf_tensor(f"wkq{i}", [128, 2048], BF16))
                            for i in range(2)]
                self.t_wkq_free = [0, 0]
                self.wvb = [EA(nc.sbuf_tensor(f"wvb{i}", [128, 1024], BF16))
                            for i in range(2)]
                self.t_wvb_free = [0, 0]
                self.t_kbuf_free = [0, 0]
                self.t_vbuf_free = [0, 0]
                stP = contextlib.ExitStack()
                EP = stP.enter_context
                with stP:
                    self.wvgb = EP(nc.sbuf_tensor("wvgb", [128, 2048], BF16))
                    self.wk15 = EP(nc.sbuf_tensor("wk15", [128, 2048], BF16))
                    self.t_wvg = self.dma(
                        "gpsimd", self.qpool,
                        self.wvgb[:].rearrange("p (c j) -> p c j", c=16),
                        self.wvg_m.rearrange("c p j -> p c j"))
                    self.t_wk15 = self.dma(
                        "gpsimd", self.qpool,
                        self.wk15[:].rearrange("p (c j) -> p c j", c=16),
                        self.wk_p[15])
                    for g in range(4):
                        with nc.named_scope(f"prefix{g}"):
                            self.prefix_group(g)
                            self.emit_cache_jobs(6)
                            self.flush()
                self.abuf += [EA(nc.sbuf_tensor(f"abuf{i}", [128, 1024],
                                                BF16)) for i in (2, 3)]
                self.dqb.append(EA(nc.sbuf_tensor("dqb1", [128, 2048],
                                                  BF16)))
                for pair in range(2):
                    with nc.named_scope(f"pair{pair}"):
                        self.own_pair(pair)
                        self.flush()
                with nc.named_scope("cachetail"):
                    self.emit_cache_jobs(10000)
                    self.flush()
            stB = contextlib.ExitStack()
            EB = stB.enter_context
            with stB:
                self.xnao = EB(nc.sbuf_tensor("xnao", [128, 16 * 1024], BF16))
                self.gub = EB(nc.sbuf_tensor("gub", [128, 44 * 1024], BF16))
                self.wgu = [EB(nc.sbuf_tensor(f"wgu{i}", [128, 2048], BF16))
                            for i in range(3)]
                self.t_wgu_free = [0, 0, 0]
                self.wdb = [EB(nc.sbuf_tensor(f"wdb{i}", [128, 512], BF16))
                            for i in range(4)]
                self.t_wdb_free = [0] * 4
                self.wob = [EB(nc.sbuf_tensor(f"wob{i}", [128, 2048], BF16))
                            for i in range(4)]
                self.t_wob_free = [0] * 4
                self.silu_sb = EB(nc.sbuf_tensor("silu_sb", [128, 1024], F32))
                if self.dump_scr:
                    self.dbg_dn_sb = EB(nc.sbuf_tensor("dbg_dn_sb",
                                                       [128, 1024], F32))
                self.up_sb = EB(nc.sbuf_tensor("up_sb", [128, 1024], F32))
                self.outsb = [EB(nc.sbuf_tensor(f"outsb{i}", [128, 1024], F32))
                              for i in range(2)]
                self.xjit = [EB(nc.sbuf_tensor(f"xjit{i}", [128, 1024], F32))
                             for i in range(2)]
                self.t_outsb_free = [0, 0]
                self.t_xjit_free = [0, 0]
                import os
                self.lockmode = os.environ.get("MK_LOCKMODE", "tail")
                self.lock_engs = tuple(
                    x for x in os.environ.get("MK_LOCK_ENGS", "").split(",")
                    if x)
                self._env_lock_engs = self.lock_engs
                self.lock_sems = None
                self.lock_sites = tuple(
                    x for x in os.environ.get("MK_LOCK_SITES", "").split(",")
                    if x)
                self.wuq = os.environ.get("MK_WUQ", "sp")
                for sup in range(2):
                    with nc.named_scope(f"ffn{sup}"):
                        self.ffn_super(sup)
                        self.flush()
                self.lockstep = False
            self.finalize()
            self.flush()
        return nc

    def setup(self):
        self.dma("gpsimd", self.qpool, self.masks[:], self.masks_f)
        self.dma("gpsimd", self.qpool, self.r2t[:], self.r2t_f)
        self.t_sdbb = self.dma("gpsimd", self.qpool, self.sdbb[:], self.sdb)
        self.dma("sync", self.qsp, self.sgs[:], self.sg_r)
        self.dma("sync", self.qsp, self.sus[:], self.su_r)
        self.dma("sync", self.qsp, self.bos[:], self.bo_r)
        self.t_consts = self.dma("sync", self.qsp, self.pmc[:], self.pm)
        self.op("vector", self.cdve, lambda v: v.memset(self.ones_bf[:], 1.0))
        self.op("vector", self.cdve, lambda v: v.memset(self.zc[:], 0.0))
        self.op("vector", self.cdve, lambda v: v.memset(self.epsc[0:1, 0:1], EPS))
        self.op("vector", self.cdve, lambda v: v.memset(self.epsc[0:1, 1:2], 1e-18))
        self.op("vector", self.cdve, lambda v: v.memset(self.onesrow[:], 1.0))
        self.op("vector", self.cdve, lambda v: v.memset(self.zb[:], 0.0))
        self.op("vector", self.cdve, lambda v: v.memset(self.zrow[:], 0.0))
        self.t_c128 = self.op("gpsimd", self.cpool,
                              lambda p: p.memset(self.c128b[:], 128.0))
        self.op("vector", self.cdve, lambda v: v.memset(self.fence_sb[:], 0.0))
        self.t_setup_dve = self.op("vector", self.cdve,
                                   lambda v: v.memset(self.kq[:], 0.0))
        self.flush()

    def build_ffn_cache(self):
        self.t_dq_free = [0, 0]
        self.dq_i = 0
        self.cache_jobs = []
        for h in range(44):
            self.cache_jobs.append(("g", h))
            self.cache_jobs.append(("u", h))
        for qtr in range(4):
            for h in range(44):
                self.cache_jobs.append(("d", (qtr, h)))
        self.t_cache_done = 0

    def emit_cache_jobs(self, k):
        dma, op = self.dma, self.op
        for _ in range(min(k, len(self.cache_jobs))):
            kind, arg = self.cache_jobs.pop(0)
            par = self.dq_i % len(self.dqb)
            self.dq_i += 1
            if kind in ("g", "u"):
                h = arg
                src = (self.wg_p if kind == "g" else self.wu_p)[h]
                dst = (self.wg_c if kind == "g" else self.wu_c)[h]
                cols = 2048
                sd = None
            else:
                qtr, h = arg
                src = self.wd_p[qtr, h]
                dst = self.wd_c[qtr, h]
                cols = 512
                sd = slice(qtr * 512, qtr * 512 + 512)
            t_in = dma("gpsimd", self.qpool, self.dqb[par][:, 0:cols], src,
                       waits=((self.qact, self.t_dq_free[par]),))
            t_d = op("gpsimd", self.cpool, lambda v, par=par, cols=cols:
                     v.tensor_tensor(self.dqb[par][:, 0:cols],
                                     self.dqb[par][:, 0:cols],
                                     self.c128b[:, 0:cols], ALU.subtract),
                     waits=((self.qpool, t_in), (self.cpool, self.t_c128)))
            if sd is not None:
                t_d = op("gpsimd", self.cpool, lambda v, par=par, ss=sd,
                         cols=cols: v.tensor_tensor(
                             self.dqb[par][:, 0:cols],
                             self.dqb[par][:, 0:cols],
                             self.sdbb[:, ss], ALU.mult),
                         waits=((self.qpool, self.t_sdbb),))
            t_o = dma("scalar", self.qact, dst, self.dqb[par][:, 0:cols],
                      waits=((self.cpool, t_d),))
            self.t_dq_free[par] = t_o
        self.t_cache_done = self.qact.n

    def ln_sub(self, xT, lt_base, sub, dst_off):
        op, wt, dma = self.op, self.wt, self.dma
        c0 = lt_base * 128 + sub * LSUB
        xr = xT.rearrange("(c p) t -> p c t", p=128)
        tx = dma("sync", self.qsp, self.xbuf[:].rearrange(
            "p (c t) -> p c t", c=16), xr[:, :, c0:c0 + LSUB],
            waits=((self.cdve, self.t_xbuf_free),))
        txb = dma("gpsimd", self.qpool, self.bsq[:, 0:16 * LSUB].rearrange(
            "p (c t) -> p c t", c=16), xr[:, :, c0:c0 + LSUB],
            waits=((self.cpe, self.t_bsq_free),))
        self.pe_bank((3, 0))
        wt("tensor", self.qpool, txb)
        wt("tensor", self.cdve, self.t_setup_dve)
        t_sum = 0
        for dc in range(16):
            t_sum = op("tensor", self.cpe, lambda t, dc=dc: t.matmul(
                self.pq[3][0:1, 0:LSUB], self.ones_bf[:],
                self.bsq[:, dc * LSUB:(dc + 1) * LSUB],
                start=(dc == 0), stop=(dc == 15)), inc=(dc == 15))
        t_sq = op("vector", self.cdve, lambda v: v.tensor_tensor(
            self.bsq[:, 0:16 * LSUB], self.bsq[:, 0:16 * LSUB],
            self.bsq[:, 0:16 * LSUB], ALU.mult), waits=((self.cpe, t_sum),))
        self.pe_bank((3, 1))
        wt("tensor", self.cdve, t_sq)
        t_ssq = 0
        for dc in range(16):
            t_ssq = op("tensor", self.cpe, lambda t, dc=dc: t.matmul(
                self.pq[3][0:1, 512:512 + LSUB], self.ones_bf[:],
                self.bsq[:, dc * LSUB:(dc + 1) * LSUB],
                start=(dc == 0), stop=(dc == 15)), inc=(dc == 15))
        self.t_bsq_free = t_ssq
        st = self.stats
        MEAN, VAR, M2, RSTD, MRS = (0, 256, 512, 768, 1024)
        op("vector", self.cdve, lambda v: v.tensor_scalar(
            st[0:1, MEAN:MEAN + LSUB], self.pq[3][0:1, 0:LSUB], 1.0 / D, None,
            ALU.mult), waits=((self.cpe, t_sum),))
        t_rd1 = op("vector", self.cdve, lambda v: v.tensor_scalar(
            st[0:1, VAR:VAR + LSUB], self.pq[3][0:1, 512:512 + LSUB],
            1.0 / D, None, ALU.mult), waits=((self.cpe, t_ssq),))
        self.free_bank((3, 0), self.cdve, t_rd1)
        self.free_bank((3, 1), self.cdve, t_rd1)
        op("vector", self.cdve, lambda v: v.tensor_tensor(
            st[0:1, M2:M2 + LSUB], st[0:1, MEAN:MEAN + LSUB],
            st[0:1, MEAN:MEAN + LSUB], ALU.mult))
        t_var = op("vector", self.cdve, lambda v: v.tensor_tensor(
            st[0:1, VAR:VAR + LSUB], st[0:1, VAR:VAR + LSUB],
            st[0:1, M2:M2 + LSUB], ALU.subtract))
        op("scalar", self.cact, lambda s: s.activation(
            st[0:1, RSTD:RSTD + LSUB], st[0:1, VAR:VAR + LSUB], AF.Ln,
            bias=self.epsc[0:1, 0:1], scale=1.0),
            waits=((self.cdve, t_var),))
        t_rstd = op("scalar", self.cact, lambda s: s.activation(
            st[0:1, RSTD:RSTD + LSUB], st[0:1, RSTD:RSTD + LSUB], AF.Exp,
            bias=self.zc[0:1, 0:1], scale=-0.5))
        t_mrs = op("vector", self.cdve, lambda v: v.tensor_tensor(
            st[0:1, MRS:MRS + LSUB], st[0:1, MEAN:MEAN + LSUB],
            st[0:1, RSTD:RSTD + LSUB], ALU.mult),
            waits=((self.cact, t_rstd),))
        self.pe_bank((2, 0))
        self.pe_bank((2, 1))
        wt("tensor", self.cdve, t_mrs)
        op("tensor", self.cpe, lambda t: t.matmul(
            self.pq[2][:, 0:LSUB], self.onesrow[:],
            st[0:1, RSTD:RSTD + LSUB], start=True, stop=True), inc=False)
        t_bc = op("tensor", self.cpe, lambda t: t.matmul(
            self.pq[2][:, 512:512 + LSUB], self.onesrow[:],
            st[0:1, MRS:MRS + LSUB], start=True, stop=True))
        op("vector", self.cdve, lambda v: v.tensor_copy(
            self.rstdb[:, 0:LSUB], self.pq[2][:, 0:LSUB]),
            waits=((self.cpe, t_bc),))
        t_ev = op("vector", self.cdve, lambda v: v.tensor_copy(
            self.rstdb[:, 256:256 + LSUB], self.pq[2][:, 512:512 + LSUB]))
        self.free_bank((2, 0), self.cdve, t_ev)
        self.free_bank((2, 1), self.cdve, t_ev)
        xap = self.xbuf[:].rearrange("p (c t) -> p c t", c=16)
        rrep = bass.AP(self.rstdb, 0, [[512, 128], [0, 16], [1, LSUB]])
        mrep = bass.AP(self.rstdb, 256, [[512, 128], [0, 16], [1, LSUB]])
        wt("vector", self.qsp, tx)
        op("vector", self.cdve, lambda v: v.tensor_tensor(
            xap, xap, rrep, ALU.mult))
        dap = self.xnp[:].rearrange("p (c t) -> p c t", c=16)[
            :, :, dst_off + sub * LSUB: dst_off + sub * LSUB + LSUB]
        wt("vector", self.cpe, self.t_xnp_free)
        t_xn = op("vector", self.cdve, lambda v: v.tensor_tensor(
            dap, xap, mrep, ALU.subtract))
        self.t_xbuf_free = t_xn
        return t_xn

    def rope(self, pkey, dst_ap, cs_i, cs_tgt, t_proj, dst_waits=()):
        op, wt = self.op, self.wt
        ps = self.psl(pkey)
        t_q0 = op("vector", self.cdve, lambda v: v.tensor_copy(
            self.q0b[:], ps),
            waits=((self.cpe, t_proj), (self.cpe, self.t_ropepr)))
        self.pe_bank((2, 0))
        wt("tensor", self.cdve, t_q0)
        t_pr = op("tensor", self.cpe, lambda t: t.matmul(
            self.pq[2][:, 0:512], self.r2t[:], self.q0b[:],
            start=True, stop=True))
        self.t_ropepr = t_pr
        cs = self.csb[cs_i]
        wt("vector", self.qsp, cs_tgt)
        op("vector", self.cdve, lambda v: v.tensor_tensor(
            self.t1b[:], self.q0b[:], cs[:, 0:512], ALU.mult))
        t_t2 = op("vector", self.cdve, lambda v: v.tensor_tensor(
            self.t2b[:], self.pq[2][:, 0:512], cs[:, 512:1024], ALU.mult),
            waits=((self.cpe, t_pr),))
        self.free_bank((2, 0), self.cdve, t_t2)
        for (s, t) in dst_waits:
            wt("vector", s, t)
        t_out = op("vector", self.cdve, lambda v: v.tensor_tensor(
            dst_ap, self.t1b[:], self.t2b[:], ALU.add))
        return t_out

    def knorm(self):
        op, wt = self.op, self.wt
        t_k2 = op("vector", self.cdve, lambda v: v.tensor_tensor(
            self.k2b[:], self.kst[:], self.kst[:], ALU.mult),
            waits=((self.cpe, self.t_k2_free),))
        self.pe_bank((3, 0))
        wt("tensor", self.cdve, t_k2)
        t_mm = op("tensor", self.cpe, lambda t: t.matmul(
            self.pq[3][0:1, 0:512], self.ones_bf[:], self.k2b[:],
            start=True, stop=True))
        self.t_k2_free = t_mm
        t_red = op("vector", self.cdve, lambda v: v.reduce_max(
            self.kq[0:1, 2:3], self.pq[3][0:1, 0:512], axis=AX.X),
            waits=((self.cpe, t_mm),))
        self.wt("vector", self.cdve, t_red)
        self.free_bank((3, 0), self.cdve, t_red)
        op("vector", self.cdve, lambda v: v.tensor_tensor(
            self.kq[0:1, 0:1], self.kq[0:1, 0:1], self.kq[0:1, 2:3], ALU.max))

    def prefix_group(self, g):
        dma, op, wt = self.dma, self.op, self.wt
        for sub in range(2):
            t_xn = self.ln_sub(self.xT_pref, 4 * g, sub, 0)
        dma("sync", self.qsp, self.csb[0][:, 0:512],
            self.cos_k[:, 512 * g:512 * g + 512],
            waits=((self.cdve, self.t_cs_free[0]),))
        t_cs = dma("sync", self.qsp, self.csb[0][:, 512:1024],
                   self.sin_k[:, 512 * g:512 * g + 512])
        pcs = [15] if g < 3 else list(range(16))
        for i, pc in enumerate(pcs):
            par = i % 2
            if pc == 15:
                wb, t_w = self.wk15, self.t_wk15
            else:
                wb = self.wkq[par]
                t_w = dma("gpsimd", self.qpool, wb[:].rearrange("p (c j) -> p c j", c=16),
                          self.wk_p[pc],
                          waits=((self.cpe, self.t_wkq_free[par]),))
            key = (par, 0)
            self.pe_bank(key)
            wt("tensor", self.qpool, t_w)
            wt("tensor", self.cdve, t_xn)
            t_mm = 0
            for dc in range(16):
                t_mm = op("tensor", self.cpe, lambda t, dc=dc, wb=wb, key=key:
                          t.matmul(self.psl(key),
                                   wb[:, dc * 128:dc * 128 + 128],
                                   self.xnp[:, dc * 1024:dc * 1024 + 512],
                                   start=(dc == 0), stop=(dc == 15)),
                          inc=(dc == 15))
            if pc != 15:
                self.t_wkq_free[par] = t_mm
            t_rope = self.rope(key, self.kst[:], 0, t_cs, t_mm,
                               dst_waits=((self.qact, self.t_kst_free),))
            self.free_bank(key, self.cdve, t_rope)
            t_st = dma("scalar", self.qact,
                       self.k_scr[pc][:, 512 * g:512 * g + 512], self.kst[:],
                       waits=((self.cdve, t_rope),))
            self.t_kst_free = t_st
            for lt in range(4 * g, 4 * g + 4):
                self.k_written[lt] = max(self.k_written[lt], t_st)
            self.knorm()
        self.t_cs_free[0] = self.cdve.n
        if g < 3:
            for tt in range(4):
                key = (tt % 2, 1)
                self.pe_bank(key)
                wt("tensor", self.qpool, self.t_wvg)
                t_mm = 0
                for dc in range(16):
                    t_mm = op("tensor", self.cpe,
                              lambda t, dc=dc, tt=tt, key=key: t.matmul(
                                  self.psl(key)[:, 0:128],
                                  self.xnp[:, dc * 1024 + tt * 128:
                                           dc * 1024 + tt * 128 + 128],
                                  self.wvgb[:, dc * 128:dc * 128 + 128],
                                  start=(dc == 0), stop=(dc == 15)),
                              inc=(dc == 15))
                self.v_evac_global(4 * g + tt, key, t_mm)
        else:
            self.v_own_quad(list(range(12, 16)), 0, prefix=True)
        self.t_xnp_free = self.cpe.n

    def v_evac_global(self, lt, key, t_mm):
        dma, op, wt = self.dma, self.op, self.wt
        par = lt % 2
        vs = self.vstage[par]
        wt("vector", self.qact, self.t_vst_free[par])
        src = self.psl(key)[:, 0:128].rearrange("p (h d) -> p h d", d=64)
        vr = vs[:, 0:1040].rearrange("p (h d) -> p h d", d=65)
        t_ev = op("vector", self.cdve, lambda v: v.tensor_scalar(
            vr[:, 14:16, 0:64], src, self.pmc[:, 0:1], None, ALU.mult),
            waits=((self.cpe, t_mm), (self.qsp, self.t_consts)))
        self.free_bank(key, self.cdve, t_ev)
        pmrep = bass.AP(self.pmc, 0, [[1, 128], [0, 2], [0, 1]])
        t_ev = op("vector", self.cdve, lambda v: v.tensor_scalar(
            vr[:, 14:16, 64:65], pmrep, 1.0, None, ALU.mult))
        t_st = dma("scalar", self.qact, self.v_scr[lt][:, 1950:2080],
                   vs[:, 910:1040], waits=((self.cdve, t_ev),))
        self.t_vst_free = [t_st, t_st]
        self.v_written[lt] = max(self.v_written[lt], t_st)

    def v_own_quad(self, lts, xoff, prefix=False):
        dma, op, wt = self.dma, self.op, self.wt
        for half in range(2):
            t_mm = 0
            for dc in range(16):
                par = dc % 2
                t_w = dma("gpsimd", self.qpool, self.wvb[par][:],
                          self.wv_m[half, dc],
                          waits=((self.cpe, self.t_wvb_free[par]),))
                wt("tensor", self.qpool, t_w)
                for i in range(4):
                    if dc == 0:
                        self.pe_bank((i, 0))
                        self.pe_bank((i, 1))
                    for s2 in range(2):
                        t_mm = op("tensor", self.cpe,
                                  lambda t, dc=dc, i=i, s2=s2, par=par:
                                  t.matmul(
                                      self.pq[i][:, s2 * 512:s2 * 512 + 512],
                                      self.xnp[:, dc * 1024 + xoff + i * 128:
                                               dc * 1024 + xoff + i * 128 + 128],
                                      self.wvb[par][:, s2 * 512:s2 * 512 + 512],
                                      start=(dc == 0), stop=(dc == 15)),
                                  inc=(i == 3 and s2 == 1))
                self.t_wvb_free[par] = t_mm
            for i, lt in enumerate(lts):
                par = i % 2
                vs = self.vstage[par]
                wt("vector", self.qact, self.t_vst_free[par])
                src = self.pq[i][:, 0:1024].rearrange("p (h d) -> p h d", d=64)
                vr = vs[:, 0:1040].rearrange("p (h d) -> p h d", d=65)
                if prefix:
                    t_ev = op("vector", self.cdve, lambda v, s=src, vr=vr:
                              v.tensor_scalar(vr[:, :, 0:64], s,
                                              self.pmc[:, 0:1], None, ALU.mult),
                              waits=((self.cpe, t_mm),
                                     (self.qsp, self.t_consts)))
                    pmrep = bass.AP(self.pmc, 0, [[1, 128], [0, 16], [0, 1]])
                    t_ev = op("vector", self.cdve, lambda v, vr=vr:
                              v.tensor_scalar(vr[:, :, 64:65], pmrep, 1.0,
                                              None, ALU.mult))
                else:
                    t_ev = op("vector", self.cdve, lambda v, s=src, vr=vr:
                              v.tensor_copy(vr[:, :, 0:64], s),
                              waits=((self.cpe, t_mm),))
                    t_ev = op("vector", self.cdve, lambda v, vr=vr:
                              v.memset(vr[:, :, 64:65], 1.0))
                t_st = dma("scalar", self.qact,
                           self.v_scr[lt][:, half * 1040:half * 1040 + 1040],
                           vs[:], waits=((self.cdve, t_ev),))
                self.t_vst_free = [t_st, t_st]
                self.v_written[lt] = max(self.v_written[lt], t_st)
                self.free_bank((i, 0), self.cdve, t_ev)
                self.free_bank((i, 1), self.cdve, t_ev)

    def own_pair(self, pair):
        dma, op, wt = self.dma, self.op, self.wt
        g0, g1 = 2 * pair, 2 * pair + 1
        xsr = self.xn_scr.rearrange("c p t -> p c t")

        def ln_group(gi, g):
            t_xn = 0
            for sub in range(2):
                t_xn = self.ln_sub(self.xT_own, 4 * g, sub, gi * 512)
            src = self.xnp[:].rearrange("p (c t) -> p c t", c=16)[
                :, :, gi * 512:gi * 512 + 512]
            self.t_xnscr_all = dma(
                "scalar", self.qact, xsr[:, :, g * 512:g * 512 + 512], src,
                waits=((self.cdve, t_xn),))
            return t_xn

        def proj(which, sp, g, t_xn):
            t_cs0 = dma("sync", self.qsp, self.csb[sp][:, 0:512],
                        (self.cos_q if which == "q" else self.cos_k)[
                            :, (2048 if which == "k" else 0) + g * 512:
                            (2048 if which == "k" else 0) + g * 512 + 512],
                        waits=((self.cdve, self.t_cs_free[sp]),))
            t_cs = dma("sync", self.qsp, self.csb[sp][:, 512:1024],
                       (self.sin_q if which == "q" else self.sin_k)[
                           :, (2048 if which == "k" else 0) + g * 512:
                           (2048 if which == "k" else 0) + g * 512 + 512])
            wsrc = self.wq_p if which == "q" else self.wk_p
            for pc in range(16):
                par = pc % 2
                key = (pc % 2, 0)
                t_w = dma("gpsimd", self.qpool,
                          self.wkq[par][:].rearrange("p (c j) -> p c j", c=16),
                          wsrc[pc], waits=((self.cpe, self.t_wkq_free[par]),))
                self.pe_bank(key)
                wt("tensor", self.qpool, t_w)
                wt("tensor", self.cdve, t_xn)
                t_mm = 0
                for dc in range(16):
                    t_mm = op("tensor", self.cpe,
                              lambda t, dc=dc, sp=sp, par=par, key=key:
                              t.matmul(
                                  self.psl(key),
                                  self.wkq[par][:, dc * 128:dc * 128 + 128],
                                  self.xnp[:, dc * 1024 + sp * 512:
                                           dc * 1024 + sp * 512 + 512],
                                  start=(dc == 0), stop=(dc == 15)),
                              inc=False)
                t_mm = self.pe_fence(self.psl(key)[0:1, 0:256])
                self.t_wkq_free[par] = t_mm
                if which == "q":
                    t_rope = self.rope(
                        key, self.qg[sp][:, pc * 512:pc * 512 + 512], sp,
                        t_cs, t_mm, dst_waits=((self.cpe, self.t_qg_free),))
                    self.free_bank(key, self.cdve, t_rope)
                else:
                    t_rope = self.rope(key, self.kst[:], sp, t_cs, t_mm,
                                       dst_waits=((self.qact,
                                                   self.t_kst_free),))
                    self.free_bank(key, self.cdve, t_rope)
                    col = 2048 + g * 512
                    t_st = dma("scalar", self.qact,
                               self.k_scr[pc][:, col:col + 512], self.kst[:],
                               waits=((self.cdve, t_rope),))
                    self.t_kst_free = t_st
                    for lt in range(16 + 4 * g, 16 + 4 * g + 4):
                        self.k_written[lt] = max(self.k_written[lt], t_st)
                    self.knorm()
                self.sprinkle()
            self.t_cs_free[sp] = self.cdve.n

        t_xn0 = ln_group(0, g0)
        proj("q", 0, g0, t_xn0)
        proj("k", 0, g0, t_xn0)
        t_xn1 = ln_group(1, g1)
        self.v_own_quad([16 + 8 * pair + i for i in range(4)], 0,
                        prefix=False)
        self.attention(g0, 0)
        proj("q", 1, g1, t_xn1)
        proj("k", 1, g1, t_xn1)
        self.v_own_quad([20 + 8 * pair + i for i in range(4)], 512,
                        prefix=False)
        self.t_xnp_free = self.cpe.n
        self.attention(g1, 1)
        self.t_qg_free = self.cpe.n

    def attention(self, g, gi):
        dma, op, wt = self.dma, self.op, self.wt
        op("vector", self.cdve, lambda v: v.memset(self.kq[0:1, 1:2], 0.0))
        for hf in range(2):
            t_q2 = op("vector", self.cdve, lambda v, hf=hf: v.tensor_tensor(
                self.bsq[:], self.qg[gi][:, hf * 4096:hf * 4096 + 4096],
                self.qg[gi][:, hf * 4096:hf * 4096 + 4096], ALU.mult),
                waits=((self.cpe, self.t_bsq_free),))
            for pc in range(8):
                self.pe_bank((3, 1))
                wt("tensor", self.cdve, t_q2)
                t_mm = op("tensor", self.cpe, lambda t, pc=pc: t.matmul(
                    self.pq[3][0:1, 512:1024], self.ones_bf[:],
                    self.bsq[:, pc * 512:pc * 512 + 512],
                    start=True, stop=True))
                t_red = op("vector", self.cdve, lambda v: v.reduce_max(
                    self.kq[0:1, 2:3], self.pq[3][0:1, 512:1024], axis=AX.X),
                    waits=((self.cpe, t_mm),))
                self.wt("vector", self.cdve, t_red)
                self.free_bank((3, 1), self.cdve, t_red)
                op("vector", self.cdve, lambda v: v.tensor_tensor(
                    self.kq[0:1, 1:2], self.kq[0:1, 1:2], self.kq[0:1, 2:3],
                    ALU.max))
            self.t_bsq_free = self.cpe.n
        t_prod = op("vector", self.cdve, lambda v: v.tensor_tensor(
            self.kq[0:1, 3:4], self.kq[0:1, 0:1], self.kq[0:1, 1:2],
            ALU.mult))
        op("scalar", self.cact, lambda s: s.activation(
            self.kq[0:1, 3:4], self.kq[0:1, 3:4], AF.Ln,
            bias=self.epsc[0:1, 1:2], scale=1.0),
            waits=((self.cdve, t_prod),))
        t_c = op("scalar", self.cact, lambda s: s.activation(
            self.kq[0:1, 3:4], self.kq[0:1, 3:4], AF.Exp,
            bias=self.zc[0:1, 0:1], scale=0.5))
        t_negc0 = op("vector", self.cdve, lambda v: v.tensor_scalar(
            self.kq[0:1, 4:5], self.kq[0:1, 3:4], -1.0, None, ALU.mult),
            waits=((self.cact, t_c),))
        self.pe_bank((3, 1))
        wt("tensor", self.cdve, t_negc0)
        op("tensor", self.cpe, lambda t: t.matmul(
            self.pq[3][:, 512:513], self.onesrow[:], self.kq[0:1, 4:5],
            start=True, stop=True), inc=False)
        t_bc = self.pe_fence(self.pq[3][0:1, 512:768])
        t_negc = op("vector", self.cdve, lambda v: v.tensor_copy(
            self.negc[:], self.pq[3][:, 512:513]), waits=((self.cpe, t_bc),))
        self.free_bank((3, 1), self.cdve, t_negc)
        if self.dump_scr:
            self.dma("scalar", self.qact,
                     self.dbg_negc[g:g + 1].rearrange("a b -> b a"),
                     self.negc[:], waits=((self.cdve, t_negc),))
            self.dma("scalar", self.qact, self.dbg_kq[g:g + 1], self.kq[:],
                     waits=((self.cdve, t_negc),))
        def osl(s):
            return (self.pq[2][:, 0:512], self.pq[2][:, 512:1024],
                    self.pq[3][:, 0:512])[s]
        OKEY = {0: (2, 0), 1: (2, 1), 2: (3, 0)}
        pending = []

        def emit_chain(job):
            c0, e0, slot, t_avh, hh, g0_, last = job
            if self.dump_scr:
                h_ = 2 * c0 + e0
                op("vector", self.cdve, lambda v, slot=slot, h_=h_:
                   v.tensor_copy(self.stats[0:1, 1280 + 2 * h_:1280 + 2 * h_ + 2],
                                 osl(slot)[64:65, 0:2]),
                   waits=((self.cpe, t_avh),))
            t_rec = op("vector", self.cdve, lambda v, slot=slot:
                       v.reciprocal(self.recrow[:], osl(slot)[64:65, :]),
                       waits=((self.cpe, t_avh),))
            self.pe_bank((3, 1))
            wt("tensor", self.cdve, t_rec)
            op("tensor", self.cpe, lambda t: t.matmul(
                self.pq[3][:, 512:1024], self.onesrow[:], self.recrow[:],
                start=True, stop=True), inc=False)
            t_bcr = self.pe_fence(self.pq[3][0:1, 512:768])
            t_recb = op("vector", self.cdve, lambda v: v.tensor_copy(
                self.recb[:], self.pq[3][:, 512:1024]),
                waits=((self.cpe, t_bcr),))
            self.free_bank((3, 1), self.cdve, t_recb)
            wt("vector", self.qact, self.t_aost_free)
            t_stage = op("vector", self.cdve, lambda v, slot=slot, hh=hh:
                         v.tensor_tensor(
                             self.aost[0:64, hh * 512:hh * 512 + 512],
                             osl(slot)[0:64, :], self.recb[0:64, 0:512],
                             ALU.mult))
            self.free_bank(OKEY[slot], self.cdve, t_stage)
            if hh == 15:
                h0 = 2 * c0 + e0 - 15
                dstap = self.ao_scr.rearrange("c (e p) t -> p c e t", e=2)[
                    :, h0 // 2:h0 // 2 + 8, :, g0_ * 512:g0_ * 512 + 512]
                srcap = self.aost[:].rearrange("p (c e t) -> p c e t",
                                               c=8, e=2)
                t_fl = dma("scalar", self.qact, dstap, srcap,
                           waits=((self.cdve, t_stage),))
                self.t_aost_free = t_fl
                self.t_ao_written = t_fl

        t_exp = {}
        for c in range(16):
            if c < 15:
                lo, hi = 12 + 8 * (g // 2), 20 + 4 * g
            else:
                lo, hi = 0, 20 + 4 * g
            par = c % 2
            t_k = dma("sync", self.qsp, self.kbuf[par][:, 0:(hi - lo) * 128],
                      self.k_scr[c][:, lo * 128:hi * 128],
                      waits=((self.qact, max(self.k_written[lo:hi])),
                             (self.cpe, self.t_kbuf_free[par])))
            t_v = dma("sync", self.qsp, self.vbuf[par][:, 0:(hi - lo) * 130],
                      self.v_scr.rearrange("t p c -> p t c")[
                          :, lo:hi, c * 130:c * 130 + 130],
                      waits=((self.qact, max(self.v_written[lo:hi])),
                             (self.cpe, self.t_vbuf_free[par])))
            tiles = list(range(lo, hi))
            nb = (len(tiles) + 1) // 2
            slot = {0: (2 * c) % 3, 1: (2 * c + 1) % 3}
            t_av = {0: 0, 1: 0}
            first_av = {0: True, 1: True}
            t_avdone = {0: 0, 1: 0}
            for bi in range(nb):
                bt = tiles[2 * bi: 2 * bi + 2]
                for e in range(2):
                    if bi >= 1:
                        self.wt("tensor", self.cact, t_exp.get((e, bi - 1), 0))
                    else:
                        self.pe_bank((e, 0))
                        self.pe_bank((e, 1))
                    wt("tensor", self.qsp, t_k)
                    for k, lt in enumerate(bt):
                        op("tensor", self.cpe, lambda t, k=k, lt=lt, par=par,
                           e=e, gi=gi, c=c, lo=lo: t.matmul(
                               self.pq[e][0:128, k * 512:k * 512 + 512],
                               self.kbuf[par][64 * e:64 * e + 64,
                                              (lt - lo) * 128:
                                              (lt - lo) * 128 + 128],
                               self.qg[gi][64 * e:64 * e + 64,
                                           c * 512:c * 512 + 512],
                               start=True, stop=True), inc=False)
                    t_s = self.pe_fence(self.pq[e][0:1, 0:256])
                    t_exp[(e, bi)] = op(
                        "scalar", self.cact, lambda s, e=e, n=len(bt), bi=bi:
                        s.activation(
                            self.abuf[2 * e + bi % 2][:, 0:n * 512],
                            self.pq[e][:, 0:n * 512], AF.Exp,
                            bias=self.negc[:, 0:1], scale=1.0),
                        waits=((self.cpe, t_s), (self.cdve, t_negc)))
                if c == 0 or (bi == 0 and pending):
                    pass
                if bi == 0 and pending:
                    for job in pending:
                        emit_chain(job)
                    pending = []
                for e in range(2):
                    ab = self.abuf[2 * e + bi % 2]
                    t_msk = 0
                    for k, lt in enumerate(bt):
                        mi = lt - (16 + 4 * g)
                        if mi >= 0:
                            t_msk = op("vector", self.cdve,
                                       lambda v, k=k, ab=ab, mi=mi:
                                       v.tensor_tensor(
                                           ab[:, k * 512:k * 512 + 512],
                                           ab[:, k * 512:k * 512 + 512],
                                           self.masks[:,
                                                      mi * 512:mi * 512 + 512],
                                           ALU.mult),
                                       waits=((self.cact, t_exp[(e, bi)]),
                                              (self.qpool, self.t_sdbb)))
                    wt("tensor", self.cact, t_exp[(e, bi)])
                    if t_msk:
                        wt("tensor", self.cdve, t_msk)
                    wt("tensor", self.qsp, t_v)
                    if first_av[e]:
                        self.pe_bank(OKEY[slot[e]])
                    for k, lt in enumerate(bt):
                        op("tensor", self.cpe, lambda t, k=k, lt=lt, par=par,
                           e=e, ab=ab, first=first_av[e],
                           last=(lt == tiles[-1]), lo=lo, sl=slot[e]:
                           t.matmul(
                               osl(sl)[0:65, :],
                               self.vbuf[par][:, (lt - lo) * 130 + 65 * e:
                                              (lt - lo) * 130 + 65 * e + 65],
                               ab[:, k * 512:k * 512 + 512],
                               start=first, stop=last), inc=False)
                        first_av[e] = False
                    if bt[-1] == tiles[-1]:
                        t_avdone[e] = self.pe_fence(osl(slot[e])[0:1, 0:256])
            for e in range(2):
                self.free_bank((e, 0), self.cact, t_exp[(e, nb - 1)])
                self.free_bank((e, 1), self.cact, t_exp[(e, nb - 1)])
            import os as _os
            for e in range(2):
                pending.append((c, e, slot[e], t_avdone[e], (2 * c + e) % 16,
                                g, c == 15))
            if _os.environ.get("MK_NODEFER"):
                for job in pending:
                    emit_chain(job)
                pending = []
            self.t_kbuf_free[par] = self.cpe.n
            self.t_vbuf_free[par] = self.cpe.n
            self.sprinkle(2)
        for job in pending:
            emit_chain(job)
        pending = []
        if self.dump_scr:
            dma("scalar", self.qact, self.dbg_sums[g:g + 1],
                self.stats[0:1, 1280:1344], waits=((self.cdve, self.cdve.n),))

    def ffn_super(self, sup):
        dma, op, wt = self.dma, self.op, self.wt
        cols = slice(1024 * sup, 1024 * sup + 1024)
        self.lockstep = (self.lockmode == "full")
        self.lock_engs = self._env_lock_engs
        self.lock_sems = None
        t_prev = self.t_super_done
        if sup == 0:
            for eng in ("sync", "gpsimd"):
                for sem in (self.cpe, self.cdve, self.cact, self.cpool):
                    self.wt(eng, sem, sem.n)
                self.wt(eng, self.qsp, self.qsp.n)
                self.wt(eng, self.qact, self.qact.n)
                self.wt(eng, self.qpool, self.qpool.n)
        xsr = self.xn_scr.rearrange("c p t -> p c t")
        t_xn = dma("sync", self.qsp, self.xnao[:].rearrange(
            "p (c t) -> p c t", c=16), xsr[:, :, cols],
            waits=((self.cpe, t_prev), (self.qact, self.t_xnscr_all)))
        t_mm = 0
        for h in range(44):
            par = h % 2
            rg, ru = (2 * h) % 3, (2 * h + 1) % 3
            t_wg = dma("scalar", self.qact, self.wgu[rg][:], self.wg_c[h],
                       waits=((self.cpe, self.t_wgu_free[rg]),
                              (self.qact, self.t_cache_done)))
            if self.wuq == "sp":
                wu_q, wu_sem = "sync", self.qsp
            else:
                wu_q, wu_sem = "scalar", self.qact
            t_wu = dma(wu_q, wu_sem, self.wgu[ru][:], self.wu_c[h],
                       waits=((self.cpe, self.t_wgu_free[ru]),
                              (self.qact, self.t_cache_done)))
            t_gmm = t_umm = 0
            for which, wbuf, t_w, w_sem, pi in (
                    ("g", self.wgu[rg], t_wg, self.qact, par),
                    ("u", self.wgu[ru], t_wu, wu_sem, 2 + par)):
                self.pe_bank((pi, 0))
                self.pe_bank((pi, 1))
                wt("tensor", w_sem, t_w)
                wt("tensor", self.qsp, t_xn)
                for dc in range(16):
                    for s2 in range(2):
                        t_mm = op("tensor", self.cpe,
                                  lambda t, dc=dc, s2=s2, wbuf=wbuf, pi=pi:
                                  t.matmul(
                                      self.pq[pi][:, s2 * 512:s2 * 512 + 512],
                                      wbuf[:, dc * 128:dc * 128 + 128],
                                      self.xnao[:, dc * 1024 + s2 * 512:
                                                dc * 1024 + s2 * 512 + 512],
                                      start=(dc == 0), stop=(dc == 15)),
                                  inc=(dc == 15 and s2 == 1))
                if which == "g":
                    self.t_wgu_free[rg] = t_mm
                    t_gmm = t_mm
                else:
                    self.t_wgu_free[ru] = t_mm
                    t_umm = t_mm
            t_si = op("scalar", self.cact, lambda s, par=par, h=h:
                      s.activation(self.silu_sb[:], self.pq[par][:, 0:1024],
                                   AF.Silu, bias=self.zc[:, 0:1],
                                   scale=self.sgs[:, h:h + 1]),
                      waits=((self.cpe, t_gmm), (self.qsp, self.t_consts),
                             (self.cdve, getattr(self, "t_gu_prev", 0)),))
            t_up = op("scalar", self.cact, lambda s, par=par, h=h:
                      s.activation(self.up_sb[:], self.pq[2 + par][:, 0:1024],
                                   AF.Copy, bias=0.0,
                                   scale=self.sus[:, h:h + 1]),
                      waits=((self.cpe, t_umm),))
            t_up = self.act_fence(self.up_sb[0:1, 1022:1024])
            t_gu = op("vector", self.cdve, lambda v, h=h: v.tensor_tensor(
                self.gub[:, h * 1024:h * 1024 + 1024], self.silu_sb[:],
                self.up_sb[:], ALU.mult),
                waits=((self.cact, t_up), (self.cpe, t_prev)))
            self.t_gu_prev = t_gu
            self.free_bank((par, 0), self.cact, t_si)
            self.free_bank((par, 1), self.cact, t_si)
            self.free_bank((2 + par, 0), self.cact, t_up)
            self.free_bank((2 + par, 1), self.cact, t_up)
        t_gu_all = self.cdve.n
        if self.lockmode in ("tail", "full", "slimtail"):
            self.lockstep = True
            if self.lockmode == "slimtail":
                if not self._env_lock_engs:
                    self.lock_engs = ("tensor", "vector")
                import os as _os2
                qmap = {"sp": self.qsp, "act": self.qact, "pool": self.qpool}
                extra = tuple(qmap[x] for x in _os2.environ.get(
                    "MK_SLIM_Q", "sp,pool").split(",") if x)
                self.lock_sems = (self.cpe, self.cdve, self.cact) + extra
        if self.dump_scr and sup == 0:
            self.dma("scalar", self.qact, self.dbg_gu, self.gub[:],
                     waits=((self.cdve, t_gu_all),))
        for iq in range(4):
            t_ao = dma("sync", self.qsp, self.xnao[:].rearrange(
                "p (c t) -> p c t", c=16)[:, 4 * iq:4 * iq + 4, :],
                self.ao_scr.rearrange("c p t -> p c t")[
                    :, 4 * iq:4 * iq + 4, cols],
                waits=((self.cpe, t_mm), (self.qact, self.t_ao_written)))
        for qtr in range(4):
            self._lock_force("tensor")
            self._lock_force("vector")
            for dcq in range(4):
                self.pe_bank((dcq, 0))
                self.pe_bank((dcq, 1))
            DP = 4
            if qtr == 0:
                self.t_wd_ready = [(self.qact, 0)] * 4
                self._wd_next = 0
                for _ in range(DP):
                    self.wd_load_flat(dma)
            for h in range(44):
                bpar = h % 4
                wd_sem, wd_tok = self.t_wd_ready[bpar]
                wt("tensor", wd_sem, wd_tok)
                wt("tensor", self.cdve, t_gu_all)
                t_dmm = 0
                for dcq in range(4):
                    for s2 in range(2):
                        t_dmm = op("tensor", self.cpe, lambda t, dcq=dcq,
                                   s2=s2, bpar=bpar, h=h: t.matmul(
                                       self.pq[dcq][:, s2 * 512:s2 * 512 + 512],
                                       self.wdb[bpar][:,
                                                      dcq * 128:dcq * 128 + 128],
                                       self.gub[:, h * 1024 + s2 * 512:
                                                h * 1024 + s2 * 512 + 512],
                                       start=(h == 0), stop=False),
                                   inc=(dcq == 3 and s2 == 1))
                self.t_wdb_free[bpar] = t_dmm
                self.wd_load_flat(dma)
            t_wo_r = [0] * 4
            for dcq in range(4):
                t_wo_r[dcq] = dma(
                    "gpsimd", self.qpool,
                    self.wob[dcq][:].rearrange("p (c j) -> p c j", c=16),
                    self.wo_p[4 * qtr + dcq].rearrange("c p j -> p c j"),
                    waits=((self.cpe, self.t_wob_free[dcq]),))
            for dcq in range(4):
                dc = 4 * qtr + dcq
                par = dcq % 2
                t_wo = t_wo_r[dcq]
                wt("tensor", self.qpool, t_wo)
                wt("tensor", self.qsp, t_ao)
                t_xj = dma("sync", self.qsp, self.xjit[par][:],
                           self.xT_own.rearrange("(c p) t -> p c t", p=128)[
                               :, dc, cols],
                           waits=((self.cdve, self.t_xjit_free[par]),))
                t_mm2 = 0
                for pc in range(16):
                    for s2 in range(2):
                        t_mm2 = op("tensor", self.cpe,
                                   lambda t, pc=pc, s2=s2, par=par, dcq=dcq:
                                   t.matmul(
                                       self.pq[dcq][:,
                                                    s2 * 512:s2 * 512 + 512],
                                       self.wob[dcq][:,
                                                     pc * 128:pc * 128 + 128],
                                       self.xnao[:, pc * 1024 + s2 * 512:
                                                 pc * 1024 + s2 * 512 + 512],
                                       start=False, stop=(pc == 15)),
                                   inc=(pc == 15 and s2 == 1))
                self.t_wob_free[dcq] = t_mm2
                t_mm2 = self.pe_fence(self.pq[dcq][0:1, 0:256])
                if "evac" in getattr(self, "lock_sites", ()):
                    self._lock_force("vector")
                t_ev = op("vector", self.cdve, lambda v, dcq=dcq, dc=dc,
                          par=par: v.scalar_tensor_tensor(
                              self.outsb[par][:], self.pq[dcq][:, 0:1024],
                              self.bos[0:128, dc:dc + 1], self.xjit[par][:],
                              ALU.add, ALU.add),
                          waits=((self.cpe, t_mm2), (self.qsp, t_xj),
                                 (self.qsp, self.t_consts),
                                 (self.qact, self.t_outsb_free[par])))
                self.t_xjit_free[par] = t_ev
                t_evf = self.dve_fence(self.outsb[par][0:1, 0:1])
                self.free_bank((dcq, 0), self.cdve, t_evf)
                self.free_bank((dcq, 1), self.cdve, t_evf)
                t_st = dma("scalar", self.qact, self.out_fm[dc][:, cols],
                           self.outsb[par][:], waits=((self.cdve, t_ev),))
                self.t_outsb_free[par] = t_st
        self.t_super_done = self.cpe.n

    def finalize(self):
        self.wt("sync", self.qact, self.qact.n)
        self.wt("sync", self.qsp, self.qsp.n)
        self.wt("sync", self.qpool, self.qpool.n)



def _prep_shared(Wq, Wk, Wv, Wo, bo, wg, sg, wu, su, wd, sd):
    f32 = np.float32
    pc_layout = lambda w: np.ascontiguousarray(
        w.reshape(16, 128, 16, 128).transpose(0, 3, 2, 1).astype(f32))
    d = {}
    d["wq_p"] = pc_layout(Wq)
    d["wk_p"] = pc_layout(Wk)
    wvT = np.ascontiguousarray(Wv.T)
    d["wv_m"] = np.ascontiguousarray(
        wvT.reshape(16, 128, 2, 1024).transpose(2, 0, 1, 3).astype(f32))
    d["wvg_m"] = np.ascontiguousarray(
        wvT[:, 1920:2048].reshape(16, 128, 128).astype(f32))
    d["wo_p"] = np.ascontiguousarray(
        Wo.reshape(16, 128, 16, 128).transpose(0, 2, 3, 1).astype(f32))
    gl = lambda w: np.ascontiguousarray(
        w.reshape(44, 128, 16, 128).transpose(0, 3, 2, 1).astype(np.int32))
    d["wg_p"] = gl(wg)
    d["wu_p"] = gl(wu)
    wdT = np.ascontiguousarray(wd.T)
    d["wd_p"] = np.ascontiguousarray(
        wdT.reshape(44, 128, 4, 4, 128).transpose(2, 0, 1, 3, 4).astype(
            np.int32))
    d["sg_r"] = np.ascontiguousarray(sg.reshape(44, 128).T.astype(f32))
    d["su_r"] = np.ascontiguousarray(su.reshape(44, 128).T.astype(f32))
    d["bo_r"] = np.ascontiguousarray(bo.reshape(16, 128).T.astype(f32))
    d["sdb"] = np.ascontiguousarray(
        np.broadcast_to(sd[None, :].astype(f32), (128, 2048)))
    m = np.zeros((128, 4, 512), dtype=f32)
    ii = np.arange(128)[:, None]
    jj = np.arange(512)[None, :]
    for dd in range(4):
        m[:, dd, :] = (jj >= ii + dd * 128).astype(f32)
    d["masks_f"] = np.ascontiguousarray(m.reshape(128, 2048))
    r2 = np.zeros((128, 128), dtype=f32)
    for i2 in range(128):
        base = (i2 // 64) * 64
        k = i2 - base
        if k < 32:
            r2[base + k + 32, i2] = -1.0
        else:
            r2[base + k - 32, i2] = 1.0
    d["r2t_f"] = r2
    inv = 1.0 / (10000.0 ** (np.arange(0, 64, 2, dtype=np.float64) / 64.0))
    rowf = inv[np.arange(128) % 32]
    return d, rowf


def _rope_tables(rowf, half):
    f32 = np.float32
    pos_own = np.arange(2048, dtype=np.float64) + 2048.0 * half
    pos_pref = np.arange(2048, dtype=np.float64)
    ang_own = rowf[:, None] * pos_own[None, :]
    ang_pref = rowf[:, None] * pos_pref[None, :]
    cos_q = (np.cos(ang_own) * 0.125).astype(f32)
    sin_q = (np.sin(ang_own) * 0.125).astype(f32)
    cos_k = np.concatenate([np.cos(ang_pref), np.cos(ang_own)],
                           axis=1).astype(f32)
    sin_k = np.concatenate([np.sin(ang_pref), np.sin(ang_own)],
                           axis=1).astype(f32)
    return cos_q, sin_q, cos_k, sin_k


_PROG_CACHE = {}


def _get_prog():
    if "nc" not in _PROG_CACHE:
        _PROG_CACHE["nc"] = MK().build()
    return _PROG_CACHE["nc"]


def kernel(L0_attn_hidden_states, L0_attn_Wq, L0_attn_Wk, L0_attn_Wv,
           L0_attn_Wo, L0_attn_bo, L0_ffn_wq_gate, L0_ffn_s_gate,
           L0_ffn_wq_up, L0_ffn_s_up, L0_ffn_wq_down, L0_ffn_s_down):
    x = np.asarray(L0_attn_hidden_states, dtype=np.float32)
    shared, rowf = _prep_shared(
        np.asarray(L0_attn_Wq, np.float32), np.asarray(L0_attn_Wk, np.float32),
        np.asarray(L0_attn_Wv, np.float32), np.asarray(L0_attn_Wo, np.float32),
        np.asarray(L0_attn_bo, np.float32),
        np.asarray(L0_ffn_wq_gate, np.int32),
        np.asarray(L0_ffn_s_gate, np.float32),
        np.asarray(L0_ffn_wq_up, np.int32),
        np.asarray(L0_ffn_s_up, np.float32),
        np.asarray(L0_ffn_wq_down, np.int32),
        np.asarray(L0_ffn_s_down, np.float32))
    in_maps = []
    for core in range(8):
        b2, half = core // 2, core % 2
        xT = np.ascontiguousarray(x[b2].T)
        cos_q, sin_q, cos_k, sin_k = _rope_tables(rowf, half)
        m = dict(shared)
        m["xT_own"] = np.ascontiguousarray(
            xT[:, 2048 * half:2048 * half + 2048])
        m["xT_pref"] = np.ascontiguousarray(xT[:, 0:2048])
        m["cos_q"], m["sin_q"] = cos_q, sin_q
        m["cos_k"], m["sin_k"] = cos_k, sin_k
        m["pm"] = np.full((128, 1), 1.0 if half else 0.0, dtype=np.float32)
        in_maps.append(m)
    nc = _get_prog()
    res = run_bass_kernel_spmd(nc, in_maps, core_ids=list(range(8)))
    out = np.empty((4, 4096, 2048), dtype=np.float32)
    for core in range(8):
        b2, half = core // 2, core % 2
        fm = res.results[core]["out_fm"].reshape(2048, 2048)
        out[b2, 2048 * half:2048 * half + 2048, :] = fm.T
    return out


# version 19
# speedup vs baseline: 1.3899x; 1.2017x over previous
import contextlib

import numpy as np
import concourse.bass as bass
import concourse.mybir as mybir
from concourse.bass_utils import run_bass_kernel_spmd
from concourse.alu_op_type import AluOpType as ALU

F32 = mybir.dt.float32
BF16 = mybir.dt.bfloat16
I32 = mybir.dt.int32
AF = mybir.ActivationFunctionType
AX = mybir.AxisListType

D = 2048
TOWN = 2048
LSUB = 256
EPS = 1e-5

ENGINES = ("tensor", "vector", "scalar", "gpsimd", "sync")


class SemT:
    def __init__(self, handle):
        self.h = handle
        self.n = 0

    def bump(self, k=1):
        self.n += k
        return self.n


class MK:

    def __init__(self, dump_scr=False):
        self.dump_scr = dump_scr
        self.nc = bass.Bass("TRN2", target_bir_lowering=False)
        nc = self.nc
        ap = lambda n, s, d=F32: nc.dram_tensor(n, s, d,
                                                kind="ExternalInput").ap()
        self.xT_own = ap("xT_own", [D, TOWN])
        self.xT_pref = ap("xT_pref", [D, TOWN])
        self.wq_p = ap("wq_p", [16, 128, 16, 128])
        self.wk_p = ap("wk_p", [16, 128, 16, 128])
        self.wv_m = ap("wv_m", [2, 16, 128, 1024])
        self.wvg_m = ap("wvg_m", [16, 128, 128])
        self.wo_p = ap("wo_p", [16, 16, 128, 128])
        self.wg_p = ap("wg_p", [44, 128, 16, 128], I32)
        self.wu_p = ap("wu_p", [44, 128, 16, 128], I32)
        self.wd_p = ap("wd_p", [4, 44, 128, 4, 128], I32)
        self.sg_r = ap("sg_r", [128, 44])
        self.su_r = ap("su_r", [128, 44])
        self.bo_r = ap("bo_r", [128, 16])
        self.sdb = ap("sdb", [128, 2048])
        self.cos_q = ap("cos_q", [128, 2048])
        self.sin_q = ap("sin_q", [128, 2048])
        self.cos_k = ap("cos_k", [128, 4096])
        self.sin_k = ap("sin_k", [128, 4096])
        self.r2t_f = ap("r2t_f", [128, 128])
        self.masks_f = ap("masks_f", [128, 4 * 512])
        self.pm = ap("pm", [128, 1])
        scr_kind = "ExternalOutput" if dump_scr else "Internal"
        scr = lambda n, s, d: nc.dram_tensor(n, s, d, kind=scr_kind).ap()
        self.xn_scr = scr("xn_scr", [16, 128, 2048], BF16)
        self.ao_scr = scr("ao_scr", [16, 128, 2048], BF16)
        self.k_scr = scr("k_scr", [16, 128, 4096], BF16)
        self.v_scr = scr("v_scr", [32, 128, 2080], BF16)
        self.wg_c = scr("wg_c", [44, 128, 2048], BF16)
        self.wu_c = scr("wu_c", [44, 128, 2048], BF16)
        self.wd_c = scr("wd_c", [4, 44, 128, 512], BF16)
        self.out_fm = nc.dram_tensor("out_fm", [16, 128, 2048], F32,
                                     kind="ExternalOutput").ap()
        if dump_scr:
            self.dbg_negc = nc.dram_tensor("dbg_negc", [4, 128], F32,
                                           kind="ExternalOutput").ap()
            self.dbg_kq = nc.dram_tensor("dbg_kq", [4, 8], F32,
                                         kind="ExternalOutput").ap()
            self.dbg_rec = nc.dram_tensor("dbg_rec", [4, 512], F32,
                                          kind="ExternalOutput").ap()
            self.dbg_sums = nc.dram_tensor("dbg_sums", [4, 64], F32,
                                           kind="ExternalOutput").ap()
            self.dbg_qg = nc.dram_tensor("dbg_qg", [2, 128, 8192], BF16,
                                         kind="ExternalOutput").ap()
            self.dbg_gu = nc.dram_tensor("dbg_gu", [128, 45056], BF16,
                                         kind="ExternalOutput").ap()
            self.dbg_dn = nc.dram_tensor("dbg_dn", [4, 128, 1024], F32,
                                         kind="ExternalOutput").ap()
            self.dbg_wd = nc.dram_tensor("dbg_wd", [4, 44, 128, 512], BF16,
                                         kind="ExternalOutput").ap()

        self.q = {e: [] for e in ENGINES}
        self.last_wait = {}
        self.t_xbuf_free = 0
        self.t_bsq_free = 0
        self.t_xnp_free = 0
        self.t_qg_free = 0
        self.t_kst_free = 0
        self.t_k2_free = 0
        self.t_ropepr = 0
        self.t_cs_free = [0, 0]
        self.t_aost_free = 0
        self.t_ao_written = 0
        self.t_xnscr_all = 0
        self.t_super_done = 0
        self.k_written = [0] * 32
        self.v_written = [0] * 32

    def em(self, eng, fn):
        self.q[eng].append(fn)

    def wt(self, eng, sem, tgt):
        if tgt <= 0:
            return
        key = (eng, id(sem))
        if self.last_wait.get(key, -1) >= tgt:
            return
        self.last_wait[key] = tgt
        self.q[eng].append(lambda E, s=sem.h, t=tgt: E.wait_ge(s, t))

    def flush(self):
        for e in ENGINES:
            fns = self.q[e]
            if not fns:
                continue
            self.q[e] = []

            def section(E, fns=fns):
                for f in fns:
                    f(E)

            getattr(self.block, e)(section)

    def dma(self, qeng, sem, dst, src, waits=()):
        if getattr(self, "lockstep", False):
            self._lock(qeng)
        for (s, t) in waits:
            self.wt(qeng, s, t)
        tgt = sem.bump(16)
        self.em(qeng, lambda Q, d=dst, s=src, h=sem.h:
                Q.dma_start(d, s).then_inc(h, 16))
        return tgt

    def pe_fence(self, elem_ap):
        t = elem_ap.tensor if hasattr(elem_ap, "tensor") else None
        return self.op("tensor", self.cpe, lambda t2: t2.matmul(
            elem_ap, self.zb[:], self.zrow[:], start=False, stop=True))

    def sprinkle(self, k=1):
        if self.cache_jobs:
            self.emit_cache_jobs(k)

    def wd_load_flat(self, dma):
        j = self._wd_next
        if j >= 176:
            return
        self._wd_next += 1
        qtr, h = j // 44, j % 44
        bpar = j % 4
        t_wd = dma("scalar", self.qact, self.wdb[bpar][:],
                   self.wd_c[qtr, h],
                   waits=((self.cpe, self.t_wdb_free[bpar]),
                          (self.qact, self.t_cache_done)))
        if self.lockmode == "slimtail":
            t_tc = self.op("gpsimd", self.cpool,
                           lambda g, bpar=bpar: g.tensor_tensor(
                               self.fence_sb[0:1, 4:8],
                               self.wdb[bpar][0:1, 0:4],
                               self.wdb[bpar][0:1, 4:8], ALU.add),
                           waits=((self.qact, t_wd),))
            self.t_wd_ready[bpar] = (self.cpool, t_tc)
        else:
            self.t_wd_ready[bpar] = (self.qact, t_wd)

    def act_fence(self, src_ap):
        return self.op("scalar", self.cact, lambda sc: sc.activation(
            self.fence_sb[0:1, 2:4], src_ap, AF.Copy, bias=0.0, scale=1.0))

    def dve_fence(self, src_ap=None):
        if src_ap is None:
            src_ap = self.fence_sb[0:1, 1:2]
        return self.op("vector", self.cdve, lambda v: v.tensor_copy(
            self.fence_sb[0:1, 0:1], src_ap))

    def _lock_force(self, eng):
        for sem2 in (self.cpe, self.cdve, self.cact, self.qsp):
            self.wt(eng, sem2, sem2.n)

    def _lock(self, eng):
        if self.lock_engs and eng not in self.lock_engs:
            return
        sems = getattr(self, "lock_sems", None) or (
            self.cpe, self.cdve, self.cact, self.cpool,
            self.qsp, self.qact, self.qpool)
        for sem2 in sems:
            self.wt(eng, sem2, sem2.n)

    def op(self, eng, sem, fn, waits=(), inc=True):
        if getattr(self, "lockstep", False):
            self._lock(eng)
        for (s, t) in waits:
            self.wt(eng, s, t)
        if inc:
            tgt = sem.bump()
            self.em(eng, lambda E, f=fn, h=sem.h: f(E).then_inc(h, 1))
            return tgt
        self.em(eng, fn)
        return sem.n

    def pe_bank(self, key):
        s, t = self.bank_free[key]
        self.wt("tensor", s, t)

    def free_bank(self, key, sem, tgt):
        self.bank_free[key] = (sem, tgt)

    def psl(self, key):
        i, hlf = key
        return self.pq[i][:, hlf * 512:(hlf + 1) * 512]

    def build(self):
        nc = self.nc
        st = contextlib.ExitStack()
        E = st.enter_context
        with st:
            self.masks = E(nc.sbuf_tensor("masks", [128, 4 * 512], BF16))
            self.r2t = E(nc.sbuf_tensor("r2t", [128, 128], BF16))
            self.sgs = E(nc.sbuf_tensor("sgs", [128, 44], F32))
            self.sus = E(nc.sbuf_tensor("sus", [128, 44], F32))
            self.bos = E(nc.sbuf_tensor("bos", [128, 16], F32))
            self.sdbb = E(nc.sbuf_tensor("sdbb", [128, 2048], BF16))
            self.pmc = E(nc.sbuf_tensor("pmc", [128, 1], F32))
            self.ones_bf = E(nc.sbuf_tensor("ones_bf", [128, 1], BF16))
            self.onesrow = E(nc.sbuf_tensor("onesrow", [1, 128], F32))
            self.zc = E(nc.sbuf_tensor("zc", [128, 1], F32))
            self.fence_sb = E(nc.sbuf_tensor("fence_sb", [1, 8], F32))
            self.dqb = [E(nc.sbuf_tensor("dqb0", [128, 2048], BF16))]
            self.c128b = E(nc.sbuf_tensor("c128b", [128, 2048], BF16))
            self.zb = E(nc.sbuf_tensor("zb", [1, 1], BF16))
            self.zrow = E(nc.sbuf_tensor("zrow", [1, 256], BF16))
            self.epsc = E(nc.sbuf_tensor("epsc", [1, 2], F32))
            self.stats = E(nc.sbuf_tensor("stats", [1, 1536], F32))
            self.kq = E(nc.sbuf_tensor("kq", [1, 8], F32))
            self.negc = E(nc.sbuf_tensor("negc", [128, 1], F32))
            self.rstdb = E(nc.sbuf_tensor("rstdb", [128, 512], F32))
            self.pq = [E(nc.psum_tensor(f"pq{i}", [128, 1024], F32))
                       for i in range(4)]
            self.qsp = SemT(E(nc.semaphore("qsp")))
            self.qact = SemT(E(nc.semaphore("qact")))
            self.qpool = SemT(E(nc.semaphore("qpool")))
            self.cpe = SemT(E(nc.semaphore("cpe")))
            self.cdve = SemT(E(nc.semaphore("cdve")))
            self.cact = SemT(E(nc.semaphore("cact")))
            self.cpool = SemT(E(nc.semaphore("cpool")))
            self.block = E(nc.Block())
            self.bank_free = {(i, h): (self.cdve, 0)
                              for i in range(4) for h in range(2)}

            self.setup()
            self.build_ffn_cache()
            stA = contextlib.ExitStack()
            EA = stA.enter_context
            with stA:
                self.xbuf = EA(nc.sbuf_tensor("xbuf", [128, 16 * LSUB], F32))
                self.recrow = EA(nc.sbuf_tensor("recrow", [1, 512], F32))
                self.recb = EA(nc.sbuf_tensor("recb", [128, 512], F32))
                self.q0b = EA(nc.sbuf_tensor("q0b", [128, 512], BF16))
                self.t1b = EA(nc.sbuf_tensor("t1b", [128, 512], F32))
                self.t2b = EA(nc.sbuf_tensor("t2b", [128, 512], F32))
                self.kst = EA(nc.sbuf_tensor("kst", [128, 512], BF16))
                self.k2b = self.q0b
                self.abuf = [EA(nc.sbuf_tensor(f"abuf{i}", [128, 1024], BF16))
                             for i in range(2)]
                self.csb = [EA(nc.sbuf_tensor(f"csb{i}", [128, 1024], F32))
                            for i in range(2)]


                self.bsq = EA(nc.sbuf_tensor("bsq", [128, 16 * 256], BF16))
                self.xnp = EA(nc.sbuf_tensor("xnp", [128, 16 * 1024], BF16))
                self.qg = [EA(nc.sbuf_tensor(f"qg{i}", [128, 16 * 512], BF16))
                           for i in range(2)]
                self.kbuf = [EA(nc.sbuf_tensor(f"kbuf{i}", [128, 4096], BF16))
                             for i in range(2)]
                self.vbuf = [EA(nc.sbuf_tensor(f"vbuf{i}", [128, 32 * 130],
                                               BF16)) for i in range(2)]
                self.vstage = [EA(nc.sbuf_tensor("vstage0", [128, 1040],
                                                 BF16))] * 2
                self.t_vst_free = [0, 0]
                self.aost = EA(nc.sbuf_tensor("aost", [64, 16 * 512], BF16))
                self.wkq = [EA(nc.sbuf_tensor(f"wkq{i}", [128, 2048], BF16))
                            for i in range(2)]
                self.t_wkq_free = [0, 0]
                self.wvb = [EA(nc.sbuf_tensor(f"wvb{i}", [128, 1024], BF16))
                            for i in range(2)]
                self.t_wvb_free = [0, 0]
                self.t_kbuf_free = [0, 0]
                self.t_vbuf_free = [0, 0]
                stP = contextlib.ExitStack()
                EP = stP.enter_context
                with stP:
                    self.wvgb = EP(nc.sbuf_tensor("wvgb", [128, 2048], BF16))
                    self.wk15 = EP(nc.sbuf_tensor("wk15", [128, 2048], BF16))
                    self.t_wvg = self.dma(
                        "gpsimd", self.qpool,
                        self.wvgb[:].rearrange("p (c j) -> p c j", c=16),
                        self.wvg_m.rearrange("c p j -> p c j"))
                    self.t_wk15 = self.dma(
                        "gpsimd", self.qpool,
                        self.wk15[:].rearrange("p (c j) -> p c j", c=16),
                        self.wk_p[15])
                    for g in range(4):
                        with nc.named_scope(f"prefix{g}"):
                            self.prefix_group(g)
                            self.emit_cache_jobs(6)
                            self.flush()
                self.abuf += [EA(nc.sbuf_tensor(f"abuf{i}", [128, 1024],
                                                BF16)) for i in (2, 3)]
                self.dqb.append(EA(nc.sbuf_tensor("dqb1", [128, 2048],
                                                  BF16)))
                for pair in range(2):
                    with nc.named_scope(f"pair{pair}"):
                        self.own_pair(pair)
                        self.flush()
                with nc.named_scope("cachetail"):
                    self.emit_cache_jobs(10000)
                    self.flush()
            stB = contextlib.ExitStack()
            EB = stB.enter_context
            with stB:
                self.xnao = EB(nc.sbuf_tensor("xnao", [128, 16 * 1024], BF16))
                self.gub = EB(nc.sbuf_tensor("gub", [128, 44 * 1024], BF16))
                self.wgu = [EB(nc.sbuf_tensor(f"wgu{i}", [128, 2048], BF16))
                            for i in range(3)]
                self.t_wgu_free = [0, 0, 0]
                self.wdb = [EB(nc.sbuf_tensor(f"wdb{i}", [128, 512], BF16))
                            for i in range(4)]
                self.t_wdb_free = [0] * 4
                self.wob = [EB(nc.sbuf_tensor(f"wob{i}", [128, 2048], BF16))
                            for i in range(4)]
                self.t_wob_free = [0] * 4
                self.silu_sb = EB(nc.sbuf_tensor("silu_sb", [128, 1024], F32))
                if self.dump_scr:
                    self.dbg_dn_sb = EB(nc.sbuf_tensor("dbg_dn_sb",
                                                       [128, 1024], F32))
                self.up_sb = EB(nc.sbuf_tensor("up_sb", [128, 1024], F32))
                self.outsb = [EB(nc.sbuf_tensor(f"outsb{i}", [128, 1024], F32))
                              for i in range(2)]
                self.xjit = [EB(nc.sbuf_tensor(f"xjit{i}", [128, 1024], F32))
                             for i in range(2)]
                self.t_outsb_free = [0, 0]
                self.t_xjit_free = [0, 0]
                import os
                self.lockmode = os.environ.get("MK_LOCKMODE", "slimtail")
                self.lock_engs = tuple(
                    x for x in os.environ.get("MK_LOCK_ENGS", "").split(",")
                    if x)
                self._env_lock_engs = self.lock_engs
                self.lock_sems = None
                self.lock_sites = tuple(
                    x for x in os.environ.get("MK_LOCK_SITES", "").split(",")
                    if x)
                self.wuq = os.environ.get("MK_WUQ", "sp")
                for sup in range(2):
                    with nc.named_scope(f"ffn{sup}"):
                        self.ffn_super(sup)
                        self.flush()
                self.lockstep = False
            self.finalize()
            self.flush()
        return nc

    def setup(self):
        self.dma("gpsimd", self.qpool, self.masks[:], self.masks_f)
        self.dma("gpsimd", self.qpool, self.r2t[:], self.r2t_f)
        self.t_sdbb = self.dma("gpsimd", self.qpool, self.sdbb[:], self.sdb)
        self.dma("sync", self.qsp, self.sgs[:], self.sg_r)
        self.dma("sync", self.qsp, self.sus[:], self.su_r)
        self.dma("sync", self.qsp, self.bos[:], self.bo_r)
        self.t_consts = self.dma("sync", self.qsp, self.pmc[:], self.pm)
        self.op("vector", self.cdve, lambda v: v.memset(self.ones_bf[:], 1.0))
        self.op("vector", self.cdve, lambda v: v.memset(self.zc[:], 0.0))
        self.op("vector", self.cdve, lambda v: v.memset(self.epsc[0:1, 0:1], EPS))
        self.op("vector", self.cdve, lambda v: v.memset(self.epsc[0:1, 1:2], 1e-18))
        self.op("vector", self.cdve, lambda v: v.memset(self.onesrow[:], 1.0))
        self.op("vector", self.cdve, lambda v: v.memset(self.zb[:], 0.0))
        self.op("vector", self.cdve, lambda v: v.memset(self.zrow[:], 0.0))
        self.t_c128 = self.op("gpsimd", self.cpool,
                              lambda p: p.memset(self.c128b[:], 128.0))
        self.op("vector", self.cdve, lambda v: v.memset(self.fence_sb[:], 0.0))
        self.t_setup_dve = self.op("vector", self.cdve,
                                   lambda v: v.memset(self.kq[:], 0.0))
        self.flush()

    def build_ffn_cache(self):
        self.t_dq_free = [0, 0]
        self.dq_i = 0
        self.cache_jobs = []
        for h in range(44):
            self.cache_jobs.append(("g", h))
            self.cache_jobs.append(("u", h))
        for qtr in range(4):
            for h in range(44):
                self.cache_jobs.append(("d", (qtr, h)))
        self.t_cache_done = 0

    def emit_cache_jobs(self, k):
        dma, op = self.dma, self.op
        for _ in range(min(k, len(self.cache_jobs))):
            kind, arg = self.cache_jobs.pop(0)
            par = self.dq_i % len(self.dqb)
            self.dq_i += 1
            if kind in ("g", "u"):
                h = arg
                src = (self.wg_p if kind == "g" else self.wu_p)[h]
                dst = (self.wg_c if kind == "g" else self.wu_c)[h]
                cols = 2048
                sd = None
            else:
                qtr, h = arg
                src = self.wd_p[qtr, h]
                dst = self.wd_c[qtr, h]
                cols = 512
                sd = slice(qtr * 512, qtr * 512 + 512)
            t_in = dma("gpsimd", self.qpool, self.dqb[par][:, 0:cols], src,
                       waits=((self.qact, self.t_dq_free[par]),))
            t_d = op("gpsimd", self.cpool, lambda v, par=par, cols=cols:
                     v.tensor_tensor(self.dqb[par][:, 0:cols],
                                     self.dqb[par][:, 0:cols],
                                     self.c128b[:, 0:cols], ALU.subtract),
                     waits=((self.qpool, t_in), (self.cpool, self.t_c128)))
            if sd is not None:
                t_d = op("gpsimd", self.cpool, lambda v, par=par, ss=sd,
                         cols=cols: v.tensor_tensor(
                             self.dqb[par][:, 0:cols],
                             self.dqb[par][:, 0:cols],
                             self.sdbb[:, ss], ALU.mult),
                         waits=((self.qpool, self.t_sdbb),))
            t_o = dma("scalar", self.qact, dst, self.dqb[par][:, 0:cols],
                      waits=((self.cpool, t_d),))
            self.t_dq_free[par] = t_o
        self.t_cache_done = self.qact.n

    def ln_sub(self, xT, lt_base, sub, dst_off):
        op, wt, dma = self.op, self.wt, self.dma
        c0 = lt_base * 128 + sub * LSUB
        xr = xT.rearrange("(c p) t -> p c t", p=128)
        tx = dma("sync", self.qsp, self.xbuf[:].rearrange(
            "p (c t) -> p c t", c=16), xr[:, :, c0:c0 + LSUB],
            waits=((self.cdve, self.t_xbuf_free),))
        bq3 = self.bsq[:, 0:16 * LSUB].rearrange("p (c t) -> p c t", c=16)
        for hb in range(2):
            txb = dma("gpsimd", self.qpool, bq3[:, hb * 8:hb * 8 + 8, :],
                      xr[:, hb * 8:hb * 8 + 8, c0:c0 + LSUB],
                      waits=(((self.cpe, self.t_bsq_free),)
                             if hb == 0 else ()))
        self.pe_bank((3, 0))
        wt("tensor", self.qpool, txb)
        wt("tensor", self.cdve, self.t_setup_dve)
        t_sum = 0
        for dc in range(16):
            t_sum = op("tensor", self.cpe, lambda t, dc=dc: t.matmul(
                self.pq[3][0:1, 0:LSUB], self.ones_bf[:],
                self.bsq[:, dc * LSUB:(dc + 1) * LSUB],
                start=(dc == 0), stop=(dc == 15)), inc=(dc == 15))
        t_sq = op("vector", self.cdve, lambda v: v.tensor_tensor(
            self.bsq[:, 0:16 * LSUB], self.bsq[:, 0:16 * LSUB],
            self.bsq[:, 0:16 * LSUB], ALU.mult), waits=((self.cpe, t_sum),))
        self.pe_bank((3, 1))
        wt("tensor", self.cdve, t_sq)
        t_ssq = 0
        for dc in range(16):
            t_ssq = op("tensor", self.cpe, lambda t, dc=dc: t.matmul(
                self.pq[3][0:1, 512:512 + LSUB], self.ones_bf[:],
                self.bsq[:, dc * LSUB:(dc + 1) * LSUB],
                start=(dc == 0), stop=(dc == 15)), inc=(dc == 15))
        self.t_bsq_free = t_ssq
        st = self.stats
        MEAN, VAR, M2, RSTD, MRS = (0, 256, 512, 768, 1024)
        t_mean = op("vector", self.cdve, lambda v: v.tensor_scalar(
            st[0:1, MEAN:MEAN + LSUB], self.pq[3][0:1, 0:LSUB], 1.0 / D, None,
            ALU.mult), waits=((self.cpe, t_sum),))
        t_rd1 = op("vector", self.cdve, lambda v: v.tensor_scalar(
            st[0:1, VAR:VAR + LSUB], self.pq[3][0:1, 512:512 + LSUB],
            1.0 / D, None, ALU.mult), waits=((self.cpe, t_ssq),))
        self.free_bank((3, 0), self.cdve, t_rd1)
        self.free_bank((3, 1), self.cdve, t_rd1)
        t_m2 = op("vector", self.cdve, lambda v: v.tensor_tensor(
            st[0:1, M2:M2 + LSUB], st[0:1, MEAN:MEAN + LSUB],
            st[0:1, MEAN:MEAN + LSUB], ALU.mult),
            waits=((self.cdve, t_mean),))
        t_var = op("vector", self.cdve, lambda v: v.tensor_tensor(
            st[0:1, VAR:VAR + LSUB], st[0:1, VAR:VAR + LSUB],
            st[0:1, M2:M2 + LSUB], ALU.subtract),
            waits=((self.cdve, t_m2),))
        t_ln = op("scalar", self.cact, lambda s: s.activation(
            st[0:1, RSTD:RSTD + LSUB], st[0:1, VAR:VAR + LSUB], AF.Ln,
            bias=self.epsc[0:1, 0:1], scale=1.0),
            waits=((self.cdve, t_var),))
        t_rstd = op("scalar", self.cact, lambda s: s.activation(
            st[0:1, RSTD:RSTD + LSUB], st[0:1, RSTD:RSTD + LSUB], AF.Exp,
            bias=self.zc[0:1, 0:1], scale=-0.5),
            waits=((self.cact, t_ln),))
        t_mrs = op("vector", self.cdve, lambda v: v.tensor_tensor(
            st[0:1, MRS:MRS + LSUB], st[0:1, MEAN:MEAN + LSUB],
            st[0:1, RSTD:RSTD + LSUB], ALU.mult),
            waits=((self.cact, t_rstd),))
        self.pe_bank((2, 0))
        self.pe_bank((2, 1))
        wt("tensor", self.cdve, t_mrs)
        op("tensor", self.cpe, lambda t: t.matmul(
            self.pq[2][:, 0:LSUB], self.onesrow[:],
            st[0:1, RSTD:RSTD + LSUB], start=True, stop=True), inc=False)
        t_bc = op("tensor", self.cpe, lambda t: t.matmul(
            self.pq[2][:, 512:512 + LSUB], self.onesrow[:],
            st[0:1, MRS:MRS + LSUB], start=True, stop=True))
        op("vector", self.cdve, lambda v: v.tensor_copy(
            self.rstdb[:, 0:LSUB], self.pq[2][:, 0:LSUB]),
            waits=((self.cpe, t_bc),))
        t_ev = op("vector", self.cdve, lambda v: v.tensor_copy(
            self.rstdb[:, 256:256 + LSUB], self.pq[2][:, 512:512 + LSUB]))
        self.free_bank((2, 0), self.cdve, t_ev)
        self.free_bank((2, 1), self.cdve, t_ev)
        xap = self.xbuf[:].rearrange("p (c t) -> p c t", c=16)
        rrep = bass.AP(self.rstdb, 0, [[512, 128], [0, 16], [1, LSUB]])
        mrep = bass.AP(self.rstdb, 256, [[512, 128], [0, 16], [1, LSUB]])
        wt("vector", self.qsp, tx)
        op("vector", self.cdve, lambda v: v.tensor_tensor(
            xap, xap, rrep, ALU.mult))
        dap = self.xnp[:].rearrange("p (c t) -> p c t", c=16)[
            :, :, dst_off + sub * LSUB: dst_off + sub * LSUB + LSUB]
        wt("vector", self.cpe, self.t_xnp_free)
        t_xn = op("vector", self.cdve, lambda v: v.tensor_tensor(
            dap, xap, mrep, ALU.subtract))
        self.t_xbuf_free = t_xn
        return t_xn

    def rope(self, pkey, dst_ap, cs_i, cs_tgt, t_proj, dst_waits=()):
        op, wt = self.op, self.wt
        ps = self.psl(pkey)
        t_q0 = op("vector", self.cdve, lambda v: v.tensor_copy(
            self.q0b[:], ps),
            waits=((self.cpe, t_proj), (self.cpe, self.t_ropepr)))
        self.pe_bank((2, 0))
        wt("tensor", self.cdve, t_q0)
        t_pr = op("tensor", self.cpe, lambda t: t.matmul(
            self.pq[2][:, 0:512], self.r2t[:], self.q0b[:],
            start=True, stop=True))
        self.t_ropepr = t_pr
        cs = self.csb[cs_i]
        wt("vector", self.qsp, cs_tgt)
        op("vector", self.cdve, lambda v: v.tensor_tensor(
            self.t1b[:], self.q0b[:], cs[:, 0:512], ALU.mult))
        t_t2 = op("vector", self.cdve, lambda v: v.tensor_tensor(
            self.t2b[:], self.pq[2][:, 0:512], cs[:, 512:1024], ALU.mult),
            waits=((self.cpe, t_pr),))
        self.free_bank((2, 0), self.cdve, t_t2)
        for (s, t) in dst_waits:
            wt("vector", s, t)
        t_out = op("vector", self.cdve, lambda v: v.tensor_tensor(
            dst_ap, self.t1b[:], self.t2b[:], ALU.add))
        return t_out

    def knorm(self):
        op, wt = self.op, self.wt
        t_k2 = op("vector", self.cdve, lambda v: v.tensor_tensor(
            self.k2b[:], self.kst[:], self.kst[:], ALU.mult),
            waits=((self.cpe, self.t_k2_free),))
        self.pe_bank((3, 0))
        wt("tensor", self.cdve, t_k2)
        t_mm = op("tensor", self.cpe, lambda t: t.matmul(
            self.pq[3][0:1, 0:512], self.ones_bf[:], self.k2b[:],
            start=True, stop=True))
        self.t_k2_free = t_mm
        t_red = op("vector", self.cdve, lambda v: v.reduce_max(
            self.kq[0:1, 2:3], self.pq[3][0:1, 0:512], axis=AX.X),
            waits=((self.cpe, t_mm),))
        self.wt("vector", self.cdve, t_red)
        self.free_bank((3, 0), self.cdve, t_red)
        op("vector", self.cdve, lambda v: v.tensor_tensor(
            self.kq[0:1, 0:1], self.kq[0:1, 0:1], self.kq[0:1, 2:3], ALU.max))

    def prefix_group(self, g):
        dma, op, wt = self.dma, self.op, self.wt
        for sub in range(2):
            t_xn = self.ln_sub(self.xT_pref, 4 * g, sub, 0)
        dma("sync", self.qsp, self.csb[0][:, 0:512],
            self.cos_k[:, 512 * g:512 * g + 512],
            waits=((self.cdve, self.t_cs_free[0]),))
        t_cs = dma("sync", self.qsp, self.csb[0][:, 512:1024],
                   self.sin_k[:, 512 * g:512 * g + 512])
        pcs = [15] if g < 3 else list(range(16))
        for i, pc in enumerate(pcs):
            par = i % 2
            if pc == 15:
                wb, t_w = self.wk15, self.t_wk15
            else:
                wb = self.wkq[par]
                t_w = dma("gpsimd", self.qpool, wb[:].rearrange("p (c j) -> p c j", c=16),
                          self.wk_p[pc],
                          waits=((self.cpe, self.t_wkq_free[par]),))
            key = (par, 0)
            self.pe_bank(key)
            wt("tensor", self.qpool, t_w)
            wt("tensor", self.cdve, t_xn)
            t_mm = 0
            for dc in range(16):
                t_mm = op("tensor", self.cpe, lambda t, dc=dc, wb=wb, key=key:
                          t.matmul(self.psl(key),
                                   wb[:, dc * 128:dc * 128 + 128],
                                   self.xnp[:, dc * 1024:dc * 1024 + 512],
                                   start=(dc == 0), stop=(dc == 15)),
                          inc=(dc == 15))
            if pc != 15:
                self.t_wkq_free[par] = t_mm
            t_rope = self.rope(key, self.kst[:], 0, t_cs, t_mm,
                               dst_waits=((self.qact, self.t_kst_free),))
            self.free_bank(key, self.cdve, t_rope)
            t_st = dma("scalar", self.qact,
                       self.k_scr[pc][:, 512 * g:512 * g + 512], self.kst[:],
                       waits=((self.cdve, t_rope),))
            self.t_kst_free = t_st
            for lt in range(4 * g, 4 * g + 4):
                self.k_written[lt] = max(self.k_written[lt], t_st)
            self.knorm()
        self.t_cs_free[0] = self.cdve.n
        if g < 3:
            for tt in range(4):
                key = (tt % 2, 1)
                self.pe_bank(key)
                wt("tensor", self.qpool, self.t_wvg)
                t_mm = 0
                for dc in range(16):
                    t_mm = op("tensor", self.cpe,
                              lambda t, dc=dc, tt=tt, key=key: t.matmul(
                                  self.psl(key)[:, 0:128],
                                  self.xnp[:, dc * 1024 + tt * 128:
                                           dc * 1024 + tt * 128 + 128],
                                  self.wvgb[:, dc * 128:dc * 128 + 128],
                                  start=(dc == 0), stop=(dc == 15)),
                              inc=(dc == 15))
                self.v_evac_global(4 * g + tt, key, t_mm)
        else:
            self.v_own_quad(list(range(12, 16)), 0, prefix=True)
        self.t_xnp_free = self.cpe.n

    def v_evac_global(self, lt, key, t_mm):
        dma, op, wt = self.dma, self.op, self.wt
        par = lt % 2
        vs = self.vstage[par]
        wt("vector", self.qact, self.t_vst_free[par])
        src = self.psl(key)[:, 0:128].rearrange("p (h d) -> p h d", d=64)
        vr = vs[:, 0:1040].rearrange("p (h d) -> p h d", d=65)
        t_ev = op("vector", self.cdve, lambda v: v.tensor_scalar(
            vr[:, 14:16, 0:64], src, self.pmc[:, 0:1], None, ALU.mult),
            waits=((self.cpe, t_mm), (self.qsp, self.t_consts)))
        self.free_bank(key, self.cdve, t_ev)
        pmrep = bass.AP(self.pmc, 0, [[1, 128], [0, 2], [0, 1]])
        t_ev = op("vector", self.cdve, lambda v: v.tensor_scalar(
            vr[:, 14:16, 64:65], pmrep, 1.0, None, ALU.mult))
        t_st = dma("scalar", self.qact, self.v_scr[lt][:, 1950:2080],
                   vs[:, 910:1040], waits=((self.cdve, t_ev),))
        self.t_vst_free = [t_st, t_st]
        self.v_written[lt] = max(self.v_written[lt], t_st)

    def v_own_quad(self, lts, xoff, prefix=False):
        dma, op, wt = self.dma, self.op, self.wt
        for half in range(2):
            t_mm = 0
            for dc in range(16):
                par = dc % 2
                t_w = dma("gpsimd", self.qpool, self.wvb[par][:],
                          self.wv_m[half, dc],
                          waits=((self.cpe, self.t_wvb_free[par]),))
                wt("tensor", self.qpool, t_w)
                for i in range(4):
                    if dc == 0:
                        self.pe_bank((i, 0))
                        self.pe_bank((i, 1))
                    for s2 in range(2):
                        t_mm = op("tensor", self.cpe,
                                  lambda t, dc=dc, i=i, s2=s2, par=par:
                                  t.matmul(
                                      self.pq[i][:, s2 * 512:s2 * 512 + 512],
                                      self.xnp[:, dc * 1024 + xoff + i * 128:
                                               dc * 1024 + xoff + i * 128 + 128],
                                      self.wvb[par][:, s2 * 512:s2 * 512 + 512],
                                      start=(dc == 0), stop=(dc == 15)),
                                  inc=(i == 3 and s2 == 1))
                self.t_wvb_free[par] = t_mm
            for i, lt in enumerate(lts):
                par = i % 2
                vs = self.vstage[par]
                wt("vector", self.qact, self.t_vst_free[par])
                src = self.pq[i][:, 0:1024].rearrange("p (h d) -> p h d", d=64)
                vr = vs[:, 0:1040].rearrange("p (h d) -> p h d", d=65)
                if prefix:
                    t_ev = op("vector", self.cdve, lambda v, s=src, vr=vr:
                              v.tensor_scalar(vr[:, :, 0:64], s,
                                              self.pmc[:, 0:1], None, ALU.mult),
                              waits=((self.cpe, t_mm),
                                     (self.qsp, self.t_consts)))
                    pmrep = bass.AP(self.pmc, 0, [[1, 128], [0, 16], [0, 1]])
                    t_ev = op("vector", self.cdve, lambda v, vr=vr:
                              v.tensor_scalar(vr[:, :, 64:65], pmrep, 1.0,
                                              None, ALU.mult))
                else:
                    t_ev = op("vector", self.cdve, lambda v, s=src, vr=vr:
                              v.tensor_copy(vr[:, :, 0:64], s),
                              waits=((self.cpe, t_mm),))
                    t_ev = op("vector", self.cdve, lambda v, vr=vr:
                              v.memset(vr[:, :, 64:65], 1.0))
                t_st = dma("scalar", self.qact,
                           self.v_scr[lt][:, half * 1040:half * 1040 + 1040],
                           vs[:], waits=((self.cdve, t_ev),))
                self.t_vst_free = [t_st, t_st]
                self.v_written[lt] = max(self.v_written[lt], t_st)
                self.free_bank((i, 0), self.cdve, t_ev)
                self.free_bank((i, 1), self.cdve, t_ev)

    def own_pair(self, pair):
        dma, op, wt = self.dma, self.op, self.wt
        g0, g1 = 2 * pair, 2 * pair + 1
        xsr = self.xn_scr.rearrange("c p t -> p c t")

        def ln_group(gi, g):
            t_xn = 0
            for sub in range(2):
                t_xn = self.ln_sub(self.xT_own, 4 * g, sub, gi * 512)
            src = self.xnp[:].rearrange("p (c t) -> p c t", c=16)[
                :, :, gi * 512:gi * 512 + 512]
            self.t_xnscr_all = dma(
                "scalar", self.qact, xsr[:, :, g * 512:g * 512 + 512], src,
                waits=((self.cdve, t_xn),))
            return t_xn

        def proj(which, sp, g, t_xn):
            t_cs0 = dma("sync", self.qsp, self.csb[sp][:, 0:512],
                        (self.cos_q if which == "q" else self.cos_k)[
                            :, (2048 if which == "k" else 0) + g * 512:
                            (2048 if which == "k" else 0) + g * 512 + 512],
                        waits=((self.cdve, self.t_cs_free[sp]),))
            t_cs = dma("sync", self.qsp, self.csb[sp][:, 512:1024],
                       (self.sin_q if which == "q" else self.sin_k)[
                           :, (2048 if which == "k" else 0) + g * 512:
                           (2048 if which == "k" else 0) + g * 512 + 512])
            wsrc = self.wq_p if which == "q" else self.wk_p
            for pc in range(16):
                par = pc % 2
                key = (pc % 2, 0)
                t_w = dma("gpsimd", self.qpool,
                          self.wkq[par][:].rearrange("p (c j) -> p c j", c=16),
                          wsrc[pc], waits=((self.cpe, self.t_wkq_free[par]),))
                self.pe_bank(key)
                wt("tensor", self.qpool, t_w)
                wt("tensor", self.cdve, t_xn)
                t_mm = 0
                for dc in range(16):
                    t_mm = op("tensor", self.cpe,
                              lambda t, dc=dc, sp=sp, par=par, key=key:
                              t.matmul(
                                  self.psl(key),
                                  self.wkq[par][:, dc * 128:dc * 128 + 128],
                                  self.xnp[:, dc * 1024 + sp * 512:
                                           dc * 1024 + sp * 512 + 512],
                                  start=(dc == 0), stop=(dc == 15)),
                              inc=False)
                t_mm = self.pe_fence(self.psl(key)[0:1, 0:256])
                self.t_wkq_free[par] = t_mm
                if which == "q":
                    t_rope = self.rope(
                        key, self.qg[sp][:, pc * 512:pc * 512 + 512], sp,
                        t_cs, t_mm, dst_waits=((self.cpe, self.t_qg_free),))
                    self.free_bank(key, self.cdve, t_rope)
                else:
                    t_rope = self.rope(key, self.kst[:], sp, t_cs, t_mm,
                                       dst_waits=((self.qact,
                                                   self.t_kst_free),))
                    self.free_bank(key, self.cdve, t_rope)
                    col = 2048 + g * 512
                    t_st = dma("scalar", self.qact,
                               self.k_scr[pc][:, col:col + 512], self.kst[:],
                               waits=((self.cdve, t_rope),))
                    self.t_kst_free = t_st
                    for lt in range(16 + 4 * g, 16 + 4 * g + 4):
                        self.k_written[lt] = max(self.k_written[lt], t_st)
                    self.knorm()
                self.sprinkle()
            self.t_cs_free[sp] = self.cdve.n

        t_xn0 = ln_group(0, g0)
        proj("q", 0, g0, t_xn0)
        proj("k", 0, g0, t_xn0)
        t_xn1 = ln_group(1, g1)
        self.v_own_quad([16 + 8 * pair + i for i in range(4)], 0,
                        prefix=False)
        self.attention(g0, 0)
        proj("q", 1, g1, t_xn1)
        proj("k", 1, g1, t_xn1)
        self.v_own_quad([20 + 8 * pair + i for i in range(4)], 512,
                        prefix=False)
        self.t_xnp_free = self.cpe.n
        self.attention(g1, 1)
        self.t_qg_free = self.cpe.n

    def attention(self, g, gi):
        dma, op, wt = self.dma, self.op, self.wt
        op("vector", self.cdve, lambda v: v.memset(self.kq[0:1, 1:2], 0.0))
        for hf in range(2):
            t_q2 = op("vector", self.cdve, lambda v, hf=hf: v.tensor_tensor(
                self.bsq[:], self.qg[gi][:, hf * 4096:hf * 4096 + 4096],
                self.qg[gi][:, hf * 4096:hf * 4096 + 4096], ALU.mult),
                waits=((self.cpe, self.t_bsq_free),))
            for pc in range(8):
                self.pe_bank((3, 1))
                wt("tensor", self.cdve, t_q2)
                t_mm = op("tensor", self.cpe, lambda t, pc=pc: t.matmul(
                    self.pq[3][0:1, 512:1024], self.ones_bf[:],
                    self.bsq[:, pc * 512:pc * 512 + 512],
                    start=True, stop=True))
                t_red = op("vector", self.cdve, lambda v: v.reduce_max(
                    self.kq[0:1, 2:3], self.pq[3][0:1, 512:1024], axis=AX.X),
                    waits=((self.cpe, t_mm),))
                self.wt("vector", self.cdve, t_red)
                self.free_bank((3, 1), self.cdve, t_red)
                op("vector", self.cdve, lambda v: v.tensor_tensor(
                    self.kq[0:1, 1:2], self.kq[0:1, 1:2], self.kq[0:1, 2:3],
                    ALU.max))
            self.t_bsq_free = self.cpe.n
        t_prod = op("vector", self.cdve, lambda v: v.tensor_tensor(
            self.kq[0:1, 3:4], self.kq[0:1, 0:1], self.kq[0:1, 1:2],
            ALU.mult))
        op("scalar", self.cact, lambda s: s.activation(
            self.kq[0:1, 3:4], self.kq[0:1, 3:4], AF.Ln,
            bias=self.epsc[0:1, 1:2], scale=1.0),
            waits=((self.cdve, t_prod),))
        t_c = op("scalar", self.cact, lambda s: s.activation(
            self.kq[0:1, 3:4], self.kq[0:1, 3:4], AF.Exp,
            bias=self.zc[0:1, 0:1], scale=0.5))
        t_negc0 = op("vector", self.cdve, lambda v: v.tensor_scalar(
            self.kq[0:1, 4:5], self.kq[0:1, 3:4], -1.0, None, ALU.mult),
            waits=((self.cact, t_c),))
        self.pe_bank((3, 1))
        wt("tensor", self.cdve, t_negc0)
        op("tensor", self.cpe, lambda t: t.matmul(
            self.pq[3][:, 512:513], self.onesrow[:], self.kq[0:1, 4:5],
            start=True, stop=True), inc=False)
        t_bc = self.pe_fence(self.pq[3][0:1, 512:768])
        t_negc = op("vector", self.cdve, lambda v: v.tensor_copy(
            self.negc[:], self.pq[3][:, 512:513]), waits=((self.cpe, t_bc),))
        self.free_bank((3, 1), self.cdve, t_negc)
        if self.dump_scr:
            self.dma("scalar", self.qact,
                     self.dbg_negc[g:g + 1].rearrange("a b -> b a"),
                     self.negc[:], waits=((self.cdve, t_negc),))
            self.dma("scalar", self.qact, self.dbg_kq[g:g + 1], self.kq[:],
                     waits=((self.cdve, t_negc),))
        def osl(s):
            return (self.pq[2][:, 0:512], self.pq[2][:, 512:1024],
                    self.pq[3][:, 0:512])[s]
        OKEY = {0: (2, 0), 1: (2, 1), 2: (3, 0)}
        pending = []

        def emit_chain(job):
            c0, e0, slot, t_avh, hh, g0_, last = job
            if self.dump_scr:
                h_ = 2 * c0 + e0
                op("vector", self.cdve, lambda v, slot=slot, h_=h_:
                   v.tensor_copy(self.stats[0:1, 1280 + 2 * h_:1280 + 2 * h_ + 2],
                                 osl(slot)[64:65, 0:2]),
                   waits=((self.cpe, t_avh),))
            t_rec = op("vector", self.cdve, lambda v, slot=slot:
                       v.reciprocal(self.recrow[:], osl(slot)[64:65, :]),
                       waits=((self.cpe, t_avh),))
            self.pe_bank((3, 1))
            wt("tensor", self.cdve, t_rec)
            op("tensor", self.cpe, lambda t: t.matmul(
                self.pq[3][:, 512:1024], self.onesrow[:], self.recrow[:],
                start=True, stop=True), inc=False)
            t_bcr = self.pe_fence(self.pq[3][0:1, 512:768])
            t_recb = op("vector", self.cdve, lambda v: v.tensor_copy(
                self.recb[:], self.pq[3][:, 512:1024]),
                waits=((self.cpe, t_bcr),))
            self.free_bank((3, 1), self.cdve, t_recb)
            wt("vector", self.qact, self.t_aost_free)
            t_stage = op("vector", self.cdve, lambda v, slot=slot, hh=hh:
                         v.tensor_tensor(
                             self.aost[0:64, hh * 512:hh * 512 + 512],
                             osl(slot)[0:64, :], self.recb[0:64, 0:512],
                             ALU.mult))
            self.free_bank(OKEY[slot], self.cdve, t_stage)
            if hh == 15:
                h0 = 2 * c0 + e0 - 15
                dstap = self.ao_scr.rearrange("c (e p) t -> p c e t", e=2)[
                    :, h0 // 2:h0 // 2 + 8, :, g0_ * 512:g0_ * 512 + 512]
                srcap = self.aost[:].rearrange("p (c e t) -> p c e t",
                                               c=8, e=2)
                t_fl = dma("scalar", self.qact, dstap, srcap,
                           waits=((self.cdve, t_stage),))
                self.t_aost_free = t_fl
                self.t_ao_written = t_fl

        t_exp = {}
        for c in range(16):
            if c < 15:
                lo, hi = 12 + 8 * (g // 2), 20 + 4 * g
            else:
                lo, hi = 0, 20 + 4 * g
            par = c % 2
            t_k = dma("sync", self.qsp, self.kbuf[par][:, 0:(hi - lo) * 128],
                      self.k_scr[c][:, lo * 128:hi * 128],
                      waits=((self.qact, max(self.k_written[lo:hi])),
                             (self.cpe, self.t_kbuf_free[par])))
            t_v = dma("sync", self.qsp, self.vbuf[par][:, 0:(hi - lo) * 130],
                      self.v_scr.rearrange("t p c -> p t c")[
                          :, lo:hi, c * 130:c * 130 + 130],
                      waits=((self.qact, max(self.v_written[lo:hi])),
                             (self.cpe, self.t_vbuf_free[par])))
            tiles = list(range(lo, hi))
            nb = (len(tiles) + 1) // 2
            slot = {0: (2 * c) % 3, 1: (2 * c + 1) % 3}
            t_av = {0: 0, 1: 0}
            first_av = {0: True, 1: True}
            t_avdone = {0: 0, 1: 0}
            for bi in range(nb):
                bt = tiles[2 * bi: 2 * bi + 2]
                for e in range(2):
                    if bi >= 1:
                        self.wt("tensor", self.cact, t_exp.get((e, bi - 1), 0))
                    else:
                        self.pe_bank((e, 0))
                        self.pe_bank((e, 1))
                    wt("tensor", self.qsp, t_k)
                    for k, lt in enumerate(bt):
                        op("tensor", self.cpe, lambda t, k=k, lt=lt, par=par,
                           e=e, gi=gi, c=c, lo=lo: t.matmul(
                               self.pq[e][0:128, k * 512:k * 512 + 512],
                               self.kbuf[par][64 * e:64 * e + 64,
                                              (lt - lo) * 128:
                                              (lt - lo) * 128 + 128],
                               self.qg[gi][64 * e:64 * e + 64,
                                           c * 512:c * 512 + 512],
                               start=True, stop=True), inc=False)
                    t_s = self.pe_fence(self.pq[e][0:1, 0:256])
                    t_exp[(e, bi)] = op(
                        "scalar", self.cact, lambda s, e=e, n=len(bt), bi=bi:
                        s.activation(
                            self.abuf[2 * e + bi % 2][:, 0:n * 512],
                            self.pq[e][:, 0:n * 512], AF.Exp,
                            bias=self.negc[:, 0:1], scale=1.0),
                        waits=((self.cpe, t_s), (self.cdve, t_negc)))
                if c == 0 or (bi == 0 and pending):
                    pass
                if bi == 0 and pending:
                    for job in pending:
                        emit_chain(job)
                    pending = []
                for e in range(2):
                    ab = self.abuf[2 * e + bi % 2]
                    t_msk = 0
                    for k, lt in enumerate(bt):
                        mi = lt - (16 + 4 * g)
                        if mi >= 0:
                            t_msk = op("vector", self.cdve,
                                       lambda v, k=k, ab=ab, mi=mi:
                                       v.tensor_tensor(
                                           ab[:, k * 512:k * 512 + 512],
                                           ab[:, k * 512:k * 512 + 512],
                                           self.masks[:,
                                                      mi * 512:mi * 512 + 512],
                                           ALU.mult),
                                       waits=((self.cact, t_exp[(e, bi)]),
                                              (self.qpool, self.t_sdbb)))
                    wt("tensor", self.cact, t_exp[(e, bi)])
                    if t_msk:
                        wt("tensor", self.cdve, t_msk)
                    wt("tensor", self.qsp, t_v)
                    if first_av[e]:
                        self.pe_bank(OKEY[slot[e]])
                    for k, lt in enumerate(bt):
                        op("tensor", self.cpe, lambda t, k=k, lt=lt, par=par,
                           e=e, ab=ab, first=first_av[e],
                           last=(lt == tiles[-1]), lo=lo, sl=slot[e]:
                           t.matmul(
                               osl(sl)[0:65, :],
                               self.vbuf[par][:, (lt - lo) * 130 + 65 * e:
                                              (lt - lo) * 130 + 65 * e + 65],
                               ab[:, k * 512:k * 512 + 512],
                               start=first, stop=last), inc=False)
                        first_av[e] = False
                    if bt[-1] == tiles[-1]:
                        t_avdone[e] = self.pe_fence(osl(slot[e])[0:1, 0:256])
            for e in range(2):
                self.free_bank((e, 0), self.cact, t_exp[(e, nb - 1)])
                self.free_bank((e, 1), self.cact, t_exp[(e, nb - 1)])
            import os as _os
            for e in range(2):
                pending.append((c, e, slot[e], t_avdone[e], (2 * c + e) % 16,
                                g, c == 15))
            if _os.environ.get("MK_NODEFER"):
                for job in pending:
                    emit_chain(job)
                pending = []
            self.t_kbuf_free[par] = self.cpe.n
            self.t_vbuf_free[par] = self.cpe.n
            self.sprinkle(2)
        for job in pending:
            emit_chain(job)
        pending = []
        if self.dump_scr:
            dma("scalar", self.qact, self.dbg_sums[g:g + 1],
                self.stats[0:1, 1280:1344], waits=((self.cdve, self.cdve.n),))

    def ffn_super(self, sup):
        dma, op, wt = self.dma, self.op, self.wt
        cols = slice(1024 * sup, 1024 * sup + 1024)
        self.lockstep = (self.lockmode == "full")
        self.lock_engs = self._env_lock_engs
        self.lock_sems = None
        t_prev = self.t_super_done
        if sup == 0:
            for eng in ("sync", "gpsimd"):
                for sem in (self.cpe, self.cdve, self.cact, self.cpool):
                    self.wt(eng, sem, sem.n)
                self.wt(eng, self.qsp, self.qsp.n)
                self.wt(eng, self.qact, self.qact.n)
                self.wt(eng, self.qpool, self.qpool.n)
        xsr = self.xn_scr.rearrange("c p t -> p c t")
        t_xn = dma("sync", self.qsp, self.xnao[:].rearrange(
            "p (c t) -> p c t", c=16), xsr[:, :, cols],
            waits=((self.cpe, t_prev), (self.qact, self.t_xnscr_all)))
        t_mm = 0
        for h in range(44):
            par = h % 2
            rg, ru = (2 * h) % 3, (2 * h + 1) % 3
            t_wg = dma("scalar", self.qact, self.wgu[rg][:], self.wg_c[h],
                       waits=((self.cpe, self.t_wgu_free[rg]),
                              (self.qact, self.t_cache_done)))
            if self.wuq == "sp":
                wu_q, wu_sem = "sync", self.qsp
            else:
                wu_q, wu_sem = "scalar", self.qact
            t_wu = dma(wu_q, wu_sem, self.wgu[ru][:], self.wu_c[h],
                       waits=((self.cpe, self.t_wgu_free[ru]),
                              (self.qact, self.t_cache_done)))
            t_gmm = t_umm = 0
            for which, wbuf, t_w, w_sem, pi in (
                    ("g", self.wgu[rg], t_wg, self.qact, par),
                    ("u", self.wgu[ru], t_wu, wu_sem, 2 + par)):
                self.pe_bank((pi, 0))
                self.pe_bank((pi, 1))
                wt("tensor", w_sem, t_w)
                wt("tensor", self.qsp, t_xn)
                for dc in range(16):
                    for s2 in range(2):
                        t_mm = op("tensor", self.cpe,
                                  lambda t, dc=dc, s2=s2, wbuf=wbuf, pi=pi:
                                  t.matmul(
                                      self.pq[pi][:, s2 * 512:s2 * 512 + 512],
                                      wbuf[:, dc * 128:dc * 128 + 128],
                                      self.xnao[:, dc * 1024 + s2 * 512:
                                                dc * 1024 + s2 * 512 + 512],
                                      start=(dc == 0), stop=(dc == 15)),
                                  inc=(dc == 15 and s2 == 1))
                if which == "g":
                    self.t_wgu_free[rg] = t_mm
                    t_gmm = t_mm
                else:
                    self.t_wgu_free[ru] = t_mm
                    t_umm = t_mm
            t_si = op("scalar", self.cact, lambda s, par=par, h=h:
                      s.activation(self.silu_sb[:], self.pq[par][:, 0:1024],
                                   AF.Silu, bias=self.zc[:, 0:1],
                                   scale=self.sgs[:, h:h + 1]),
                      waits=((self.cpe, t_gmm), (self.qsp, self.t_consts),
                             (self.cdve, getattr(self, "t_gu_prev", 0)),))
            t_up = op("scalar", self.cact, lambda s, par=par, h=h:
                      s.activation(self.up_sb[:], self.pq[2 + par][:, 0:1024],
                                   AF.Copy, bias=0.0,
                                   scale=self.sus[:, h:h + 1]),
                      waits=((self.cpe, t_umm),))
            t_up = self.act_fence(self.up_sb[0:1, 1022:1024])
            t_gu = op("vector", self.cdve, lambda v, h=h: v.tensor_tensor(
                self.gub[:, h * 1024:h * 1024 + 1024], self.silu_sb[:],
                self.up_sb[:], ALU.mult),
                waits=((self.cact, t_up), (self.cpe, t_prev)))
            self.t_gu_prev = t_gu
            self.free_bank((par, 0), self.cact, t_si)
            self.free_bank((par, 1), self.cact, t_si)
            self.free_bank((2 + par, 0), self.cact, t_up)
            self.free_bank((2 + par, 1), self.cact, t_up)
        t_gu_all = self.cdve.n
        if self.lockmode in ("tail", "full", "slimtail"):
            self.lockstep = True
            if self.lockmode == "slimtail":
                if not self._env_lock_engs:
                    self.lock_engs = ("tensor", "vector")
                import os as _os2
                qmap = {"sp": self.qsp, "act": self.qact, "pool": self.qpool}
                extra = tuple(qmap[x] for x in _os2.environ.get(
                    "MK_SLIM_Q", "sp,pool").split(",") if x)
                self.lock_sems = (self.cpe, self.cdve, self.cact) + extra
        if self.dump_scr and sup == 0:
            self.dma("scalar", self.qact, self.dbg_gu, self.gub[:],
                     waits=((self.cdve, t_gu_all),))
        for iq in range(4):
            t_ao = dma("sync", self.qsp, self.xnao[:].rearrange(
                "p (c t) -> p c t", c=16)[:, 4 * iq:4 * iq + 4, :],
                self.ao_scr.rearrange("c p t -> p c t")[
                    :, 4 * iq:4 * iq + 4, cols],
                waits=((self.cpe, t_mm), (self.qact, self.t_ao_written)))
        for qtr in range(4):
            self._lock_force("tensor")
            self._lock_force("vector")
            for dcq in range(4):
                self.pe_bank((dcq, 0))
                self.pe_bank((dcq, 1))
            DP = 4
            if qtr == 0:
                self.t_wd_ready = [(self.qact, 0)] * 4
                self._wd_next = 0
                for _ in range(DP):
                    self.wd_load_flat(dma)
            for h in range(44):
                bpar = h % 4
                wd_sem, wd_tok = self.t_wd_ready[bpar]
                wt("tensor", wd_sem, wd_tok)
                wt("tensor", self.cdve, t_gu_all)
                t_dmm = 0
                for dcq in range(4):
                    for s2 in range(2):
                        t_dmm = op("tensor", self.cpe, lambda t, dcq=dcq,
                                   s2=s2, bpar=bpar, h=h: t.matmul(
                                       self.pq[dcq][:, s2 * 512:s2 * 512 + 512],
                                       self.wdb[bpar][:,
                                                      dcq * 128:dcq * 128 + 128],
                                       self.gub[:, h * 1024 + s2 * 512:
                                                h * 1024 + s2 * 512 + 512],
                                       start=(h == 0), stop=False),
                                   inc=(dcq == 3 and s2 == 1))
                self.t_wdb_free[bpar] = t_dmm
                self.wd_load_flat(dma)
            t_wo_r = [0] * 4
            for dcq in range(4):
                t_wo_r[dcq] = dma(
                    "gpsimd", self.qpool,
                    self.wob[dcq][:].rearrange("p (c j) -> p c j", c=16),
                    self.wo_p[4 * qtr + dcq].rearrange("c p j -> p c j"),
                    waits=((self.cpe, self.t_wob_free[dcq]),))
            for dcq in range(4):
                dc = 4 * qtr + dcq
                par = dcq % 2
                t_wo = t_wo_r[dcq]
                wt("tensor", self.qpool, t_wo)
                wt("tensor", self.qsp, t_ao)
                t_xj = dma("sync", self.qsp, self.xjit[par][:],
                           self.xT_own.rearrange("(c p) t -> p c t", p=128)[
                               :, dc, cols],
                           waits=((self.cdve, self.t_xjit_free[par]),))
                t_mm2 = 0
                for pc in range(16):
                    for s2 in range(2):
                        t_mm2 = op("tensor", self.cpe,
                                   lambda t, pc=pc, s2=s2, par=par, dcq=dcq:
                                   t.matmul(
                                       self.pq[dcq][:,
                                                    s2 * 512:s2 * 512 + 512],
                                       self.wob[dcq][:,
                                                     pc * 128:pc * 128 + 128],
                                       self.xnao[:, pc * 1024 + s2 * 512:
                                                 pc * 1024 + s2 * 512 + 512],
                                       start=False, stop=(pc == 15)),
                                   inc=(pc == 15 and s2 == 1))
                self.t_wob_free[dcq] = t_mm2
                t_mm2 = self.pe_fence(self.pq[dcq][0:1, 0:256])
                if "evac" in getattr(self, "lock_sites", ()):
                    self._lock_force("vector")
                t_ev = op("vector", self.cdve, lambda v, dcq=dcq, dc=dc,
                          par=par: v.scalar_tensor_tensor(
                              self.outsb[par][:], self.pq[dcq][:, 0:1024],
                              self.bos[0:128, dc:dc + 1], self.xjit[par][:],
                              ALU.add, ALU.add),
                          waits=((self.cpe, t_mm2), (self.qsp, t_xj),
                                 (self.qsp, self.t_consts),
                                 (self.qact, self.t_outsb_free[par])))
                self.t_xjit_free[par] = t_ev
                t_evf = self.dve_fence(self.outsb[par][0:1, 0:1])
                self.free_bank((dcq, 0), self.cdve, t_evf)
                self.free_bank((dcq, 1), self.cdve, t_evf)
                t_st = dma("scalar", self.qact, self.out_fm[dc][:, cols],
                           self.outsb[par][:], waits=((self.cdve, t_ev),))
                self.t_outsb_free[par] = t_st
        self.t_super_done = self.cpe.n

    def finalize(self):
        self.wt("sync", self.qact, self.qact.n)
        self.wt("sync", self.qsp, self.qsp.n)
        self.wt("sync", self.qpool, self.qpool.n)



def _prep_shared(Wq, Wk, Wv, Wo, bo, wg, sg, wu, su, wd, sd):
    f32 = np.float32
    pc_layout = lambda w: np.ascontiguousarray(
        w.reshape(16, 128, 16, 128).transpose(0, 3, 2, 1).astype(f32))
    d = {}
    d["wq_p"] = pc_layout(Wq)
    d["wk_p"] = pc_layout(Wk)
    wvT = np.ascontiguousarray(Wv.T)
    d["wv_m"] = np.ascontiguousarray(
        wvT.reshape(16, 128, 2, 1024).transpose(2, 0, 1, 3).astype(f32))
    d["wvg_m"] = np.ascontiguousarray(
        wvT[:, 1920:2048].reshape(16, 128, 128).astype(f32))
    d["wo_p"] = np.ascontiguousarray(
        Wo.reshape(16, 128, 16, 128).transpose(0, 2, 3, 1).astype(f32))
    gl = lambda w: np.ascontiguousarray(
        w.reshape(44, 128, 16, 128).transpose(0, 3, 2, 1).astype(np.int32))
    d["wg_p"] = gl(wg)
    d["wu_p"] = gl(wu)
    wdT = np.ascontiguousarray(wd.T)
    d["wd_p"] = np.ascontiguousarray(
        wdT.reshape(44, 128, 4, 4, 128).transpose(2, 0, 1, 3, 4).astype(
            np.int32))
    d["sg_r"] = np.ascontiguousarray(sg.reshape(44, 128).T.astype(f32))
    d["su_r"] = np.ascontiguousarray(su.reshape(44, 128).T.astype(f32))
    d["bo_r"] = np.ascontiguousarray(bo.reshape(16, 128).T.astype(f32))
    d["sdb"] = np.ascontiguousarray(
        np.broadcast_to(sd[None, :].astype(f32), (128, 2048)))
    m = np.zeros((128, 4, 512), dtype=f32)
    ii = np.arange(128)[:, None]
    jj = np.arange(512)[None, :]
    for dd in range(4):
        m[:, dd, :] = (jj >= ii + dd * 128).astype(f32)
    d["masks_f"] = np.ascontiguousarray(m.reshape(128, 2048))
    r2 = np.zeros((128, 128), dtype=f32)
    for i2 in range(128):
        base = (i2 // 64) * 64
        k = i2 - base
        if k < 32:
            r2[base + k + 32, i2] = -1.0
        else:
            r2[base + k - 32, i2] = 1.0
    d["r2t_f"] = r2
    inv = 1.0 / (10000.0 ** (np.arange(0, 64, 2, dtype=np.float64) / 64.0))
    rowf = inv[np.arange(128) % 32]
    return d, rowf


def _rope_tables(rowf, half):
    f32 = np.float32
    pos_own = np.arange(2048, dtype=np.float64) + 2048.0 * half
    pos_pref = np.arange(2048, dtype=np.float64)
    ang_own = rowf[:, None] * pos_own[None, :]
    ang_pref = rowf[:, None] * pos_pref[None, :]
    cos_q = (np.cos(ang_own) * 0.125).astype(f32)
    sin_q = (np.sin(ang_own) * 0.125).astype(f32)
    cos_k = np.concatenate([np.cos(ang_pref), np.cos(ang_own)],
                           axis=1).astype(f32)
    sin_k = np.concatenate([np.sin(ang_pref), np.sin(ang_own)],
                           axis=1).astype(f32)
    return cos_q, sin_q, cos_k, sin_k


_PROG_CACHE = {}


def _get_prog():
    if "nc" not in _PROG_CACHE:
        _PROG_CACHE["nc"] = MK().build()
    return _PROG_CACHE["nc"]


def kernel(L0_attn_hidden_states, L0_attn_Wq, L0_attn_Wk, L0_attn_Wv,
           L0_attn_Wo, L0_attn_bo, L0_ffn_wq_gate, L0_ffn_s_gate,
           L0_ffn_wq_up, L0_ffn_s_up, L0_ffn_wq_down, L0_ffn_s_down):
    x = np.asarray(L0_attn_hidden_states, dtype=np.float32)
    shared, rowf = _prep_shared(
        np.asarray(L0_attn_Wq, np.float32), np.asarray(L0_attn_Wk, np.float32),
        np.asarray(L0_attn_Wv, np.float32), np.asarray(L0_attn_Wo, np.float32),
        np.asarray(L0_attn_bo, np.float32),
        np.asarray(L0_ffn_wq_gate, np.int32),
        np.asarray(L0_ffn_s_gate, np.float32),
        np.asarray(L0_ffn_wq_up, np.int32),
        np.asarray(L0_ffn_s_up, np.float32),
        np.asarray(L0_ffn_wq_down, np.int32),
        np.asarray(L0_ffn_s_down, np.float32))
    in_maps = []
    for core in range(8):
        b2, half = core // 2, core % 2
        xT = np.ascontiguousarray(x[b2].T)
        cos_q, sin_q, cos_k, sin_k = _rope_tables(rowf, half)
        m = dict(shared)
        m["xT_own"] = np.ascontiguousarray(
            xT[:, 2048 * half:2048 * half + 2048])
        m["xT_pref"] = np.ascontiguousarray(xT[:, 0:2048])
        m["cos_q"], m["sin_q"] = cos_q, sin_q
        m["cos_k"], m["sin_k"] = cos_k, sin_k
        m["pm"] = np.full((128, 1), 1.0 if half else 0.0, dtype=np.float32)
        in_maps.append(m)
    nc = _get_prog()
    res = run_bass_kernel_spmd(nc, in_maps, core_ids=list(range(8)))
    out = np.empty((4, 4096, 2048), dtype=np.float32)
    for core in range(8):
        b2, half = core // 2, core % 2
        fm = res.results[core]["out_fm"].reshape(2048, 2048)
        out[b2, 2048 * half:2048 * half + 2048, :] = fm.T
    return out
